# version 07
# baseline (speedup 1.0000x reference)
import numpy as np
import ml_dtypes
import concourse.bass as bass
import concourse.mybir as mybir
from concourse.bass_utils import run_bass_kernel_spmd
from contextlib import ExitStack

F32 = mybir.dt.float32
BF16 = mybir.dt.bfloat16
AF = mybir.ActivationFunctionType
ALU = mybir.AluOpType
AX = mybir.AxisListType
BFNP = ml_dtypes.bfloat16

D = 4096
NT = 2048
NH, KVH, HD = 32, 8, 128
CH = 128
NCH = NT // CH
NTT = NT // 128
SSCALE = float(1.0 / np.sqrt(np.float32(HD)))
INV16 = 1.0 / 16.0

STREAMS = ("T", "V", "S", "SY", "G")
STREAM_SEM = {"T": "c_t", "V": "c_v", "S": "c_s", "SY": "c_sy", "G": "c_g"}
DMA_POOL = {"S": [f"d_s{i}" for i in range(2)],
            "SY": [f"d_y{i}" for i in range(6)],
            "G": [f"d_g{i}" for i in range(4)]}
ALL_SEMS = (list(STREAM_SEM.values())
            + [x for p in DMA_POOL.values() for x in p])


class Walk:
    def __init__(self, mode, stream=None, eng=None, sems=None):
        self.mode = mode
        self.stream = stream
        self.eng = eng
        self.sems = sems
        self.waited = set()
        self.evt = {}
        self.C = {s: 0 for s in ALL_SEMS}
        self.dn = {s: 0 for s in DMA_POOL}
        self.hwm = {}

    def _wait(self, names, is_dma=False):
        for w in names:
            if self.mode == "collect":
                self.waited.add(w)
            elif self.mode == "emit":
                if w not in self.evt:
                    raise KeyError(f"wait on unknown event {w}")
                sem, val = self.evt[w]
                if not is_dma and sem == STREAM_SEM.get(self.stream):
                    continue
                if self.hwm.get(sem, -1) >= val:
                    continue
                self.eng.wait_ge(self.sems[sem], val)
                self.hwm[sem] = val

    def op(self, stream, name, emit_fn, waits=()):
        serial = stream in ("V", "S", "G")
        sem = STREAM_SEM[stream]
        if self.mode == "collect":
            self._wait(waits)
            return
        if self.mode == "count":
            if serial:
                self.C[sem] += 1
                if name is not None:
                    self.evt[name] = (sem, self.C[sem])
            elif name is not None and name in self.waited:
                self.C[sem] += 1
                self.evt[name] = (sem, self.C[sem])
            return
        if self.stream != stream:
            return
        self._wait(waits)
        if serial and self.C[sem] > 0:
            self.eng.wait_ge(self.sems[sem], self.C[sem])
        instr = emit_fn(self.eng)
        if serial:
            self.C[sem] += 1
            instr.then_inc(self.sems[sem], 1)
        elif name is not None and name in self.waited:
            self.C[sem] += 1
            instr.then_inc(self.sems[sem], 1)

    def seq(self, stream, name, fns, waits=()):
        for i, fn in enumerate(fns):
            self.op(stream, name if i == len(fns) - 1 else None, fn,
                    waits=waits if i == 0 else ())

    def dma(self, stream, name, emit_fn, waits=()):
        if self.mode == "collect":
            self._wait(waits)
            return
        pool = DMA_POOL[stream]
        sem = pool[self.dn[stream] % len(pool)]
        self.dn[stream] += 1
        if self.mode == "count":
            self.C[sem] += 16
            self.evt[name] = (sem, self.C[sem])
            return
        if self.stream != stream:
            return
        self._wait(waits, is_dma=True)
        if self.C[sem] > 0 and self.hwm.get(sem, -1) < self.C[sem]:
            self.eng.wait_ge(self.sems[sem], self.C[sem])
            self.hwm[sem] = self.C[sem]
        instr = emit_fn(self.eng)
        self.C[sem] += 16
        instr.then_inc(self.sems[sem], 16)


def _ins(ap, idx, step, count):
    dims = [list(d) for d in ap.ap]
    dims.insert(1 + idx, [step, count])
    return bass.AP(ap.tensor, ap.offset, dims)


def build_walk(w, t):
    op, dma = w.op, w.dma
    pb = t["pb"]
    pS = t["pS"]

    CONSTS = ["cos", "sin", "rmat", "tril", "mcur", "mprev", "ones", "twos",
              "onesf", "id128", "b0", "i4rep"]
    for nm in CONSTS:
        dma("SY", f"c_{nm}", lambda e, nm=nm: e.dma_start(out=t[nm + "_sb"][:], in_=t[nm + "_h"][:]))

    st1 = t["st1"]
    for i in range(NTT):
        ib = i % 2
        xb = t["xt"][ib]
        so = (i % 2) * 8
        dma("SY", f"p1x.{i}",
            lambda e, i=i, xb=xb: e.dma_start(out=xb, in_=t["x_h"][i * 128:(i + 1) * 128, :]),
            waits=[f"p1xn.{i-2}"] if i >= 2 else [])
        op("S", f"p1sq.{i}",
           lambda e, xb=xb, so=so: e.activation(t["xc"], xb, AF.Square,
                                                accum_out=st1[:, so + 1:so + 2]),
           waits=[f"p1x.{i}"] + ([f"p1r.{i-2}"] if i >= 2 else []))
        w.seq("V", f"p1nm.{i}", [
               lambda e, xb=xb, so=so: e.tensor_reduce(st1[:, so:so + 1], xb, AX.X, ALU.add),
               lambda e, so=so: e.tensor_scalar(st1[:, so:so + 1], st1[:, so:so + 1], -1.0 / D, None, ALU.mult)],
           waits=[f"p1x.{i}"] + ([f"p1xn.{i-2}"] if i >= 2 else []))
        w.seq("V", f"p1r.{i}", [
               lambda e, so=so: e.tensor_scalar(st1[:, so + 2:so + 3], st1[:, so + 1:so + 2], 1.0 / D, None, ALU.mult),
               lambda e, so=so: e.tensor_tensor(st1[:, so + 5:so + 6], st1[:, so:so + 1], st1[:, so:so + 1], ALU.mult),
               lambda e, so=so: e.tensor_tensor(st1[:, so + 2:so + 3], st1[:, so + 2:so + 3], st1[:, so + 5:so + 6], ALU.subtract),
               lambda e, so=so: e.tensor_scalar(st1[:, so + 2:so + 3], st1[:, so + 2:so + 3], 1e-5, None, ALU.add),
               lambda e, so=so: e.reciprocal(st1[:, so + 3:so + 4], st1[:, so + 2:so + 3])],
           waits=[f"p1sq.{i}"])
        w.seq("S", f"p1xn.{i}", [
               lambda e, so=so: e.activation(st1[:, so + 4:so + 5], st1[:, so + 3:so + 4], AF.Sqrt),
               lambda e, xb=xb, so=so: e.activation(t["xc"], xb, AF.Identity, bias=st1[:, so:so + 1]),
               lambda e, so=so, ib=ib: e.activation(t["xn_tm"][ib], t["xc"], AF.Identity, scale=st1[:, so + 4:so + 5])],
           waits=[f"p1r.{i}", f"p1nm.{i}"] + ([f"p1tr.{i-2}"] if i >= 2 else []))
        dma("S", f"p1tr.{i}",
            lambda e, i=i, ib=ib: e.dma_start_transpose(
                t["bigA"][:].rearrange("p (c tt) -> p c tt", c=32)[:, :, i * 128:(i + 1) * 128],
                t["xn_tm"][ib]),
            waits=[f"p1xn.{i}"])
    P1_END = [f"p1tr.{NTT-2}", f"p1tr.{NTT-1}", f"p1xn.{NTT-1}"]

    dma("G", "w_acat",
        lambda e: e.dma_start(out=t["acat"],
                              in_=t["acat_h"][:].rearrange("(k p) m -> p k m", p=128)),
        waits=P1_END)
    dma("G", "w_bt", lambda e: e.dma_start(out=t["bt"], in_=t["bt_h"][:]), waits=P1_END)
    xn = t["bigA"][:].rearrange("p (k tt) -> p k tt", k=32)
    for ts in range(4):
        def mm_h(e, ts=ts):
            ac = t["acat"].rearrange("p (k m) -> p k m", k=32)
            for k in range(32):
                ii = e.matmul(pb[0][:], ac[:, k, :], xn[:, k, ts * 512:(ts + 1) * 512],
                              start=(k == 0), stop=(k == 31))
            return ii
        op("T", f"p2h.{ts}", mm_h,
           waits=["w_acat", f"p1tr.{NTT-1}"] + ([f"p2hev.{ts-1}"] if ts else []))
        op("S", f"p2hev.{ts}",
           lambda e, ts=ts: e.copy(t["h"][:, ts * 512:(ts + 1) * 512], pb[0][:]),
           waits=[f"p2h.{ts}"])
    H_READY = [f"p2hev.{3}"]

    def _w0_load(b):
        src = bass.AP(t["w0t_h"].tensor, b * 256, [[D, 128], [128 * D, 32], [1, 256]])
        dma("G", f"w0.{b}",
            lambda e, src=src, b=b: e.dma_start(out=t["wbuf"][b % 2], in_=src),
            waits=P1_END if b < 2 else [f"p2mm.{(b-2)*8 + 7}"])
    _w0_load(0)
    _w0_load(1)
    for lt in range(128):
        j, ts = lt // 4, lt % 4
        src = bass.AP(t["maskt_h"].tensor, j * 128 * NT + ts * 512,
                      [[NT, 128], [D * NT, 4], [1, 512]])
        dma("G", f"msk.{lt}",
            lambda e, src=src, lt=lt: e.dma_start(out=t["mbig"][lt % 2], in_=src),
            waits=P1_END if lt < 2 else [f"p2tl3.{lt - 2}"])
        if lt % 8 == 7 and lt // 8 + 2 < 16:
            _w0_load(lt // 8 + 2)
    for it in range(128):
        j, ts = it // 4, it % 4
        b, j2 = j // 2, j % 2
        lt = it

        bk = 4 + it % 2
        def mm_base(e, b=b, j2=j2, ts=ts, bk=bk):
            wb = t["wbuf"][b % 2].rearrange("p (k m) -> p k m", k=32)
            for k in range(32):
                ii = e.matmul(pb[bk][:], wb[:, k, j2 * 128:j2 * 128 + 128],
                              xn[:, k, ts * 512:(ts + 1) * 512],
                              start=(k == 0), stop=False)
            return ii
        op("T", f"p2b.{it}", mm_base,
           waits=[f"w0.{b}"] + ([f"p2sil.{it-2}"] if it >= 2 else H_READY))

        def mm_y(e, j=j, ts=ts):
            for l in range(4):
                ii = e.matmul(pb[l][:], t["bt"][32 * l:32 * l + 16, j * 128:(j + 1) * 128],
                              t["h"][32 * l:32 * l + 16, ts * 512:(ts + 1) * 512],
                              start=True, stop=True, tile_position=(32 * l, 0))
            return ii
        op("T", f"p2y.{it}", mm_y,
           waits=["w_bt"] + ([f"p2tl3.{it-1}"] if it else []))
        for l in range(4):
            op("V", f"p2tl{l}.{it}",
               lambda e, l=l, ts=ts, lt=lt: e.tensor_tensor(
                   t["tl"][l], pb[l][:],
                   t["mbig"][lt % 2].rearrange("p (l4 tt) -> p l4 tt", l4=4)[:, l, :],
                   ALU.mult),
               waits=[f"p2y.{it}", f"msk.{lt}"])

        def mm_id(e, bk=bk):
            for l in range(4):
                ii = e.matmul(pb[bk][:], t["id128_sb"][:], t["tl"][l],
                              start=False, stop=(l == 3))
            return ii
        op("T", f"p2mm.{it}", mm_id,
           waits=["c_id128", f"p2tl3.{it}"])
        op("S", f"p2sil.{it}",
           lambda e, it=it, j=j, bk=bk: e.activation(t["gv"][it % 2], pb[bk][:], AF.Silu,
                                              bias=t["b0_sb"][:, j:j + 1]),
           waits=[f"p2mm.{it}", "c_b0"] + ([f"p2gs.{it-2}"] if it >= 2 else []))
        dma("SY", f"p2gs.{it}",
            lambda e, it=it, j=j, ts=ts: e.dma_start(
                out=t["g_sp"][j, :, ts * 512:(ts + 1) * 512], in_=t["gv"][it % 2]),
            waits=[f"p2sil.{it}"])
    P2_POOL_FREE = [f"p2tl3.{127}", f"p2sil.{127}"]

    for vq in range(4):
        src = bass.AP(t["wv_h"].tensor, vq * 256,
                      [[KVH * HD, 128], [128 * KVH * HD, 32], [1, 256]])
        dma("G", f"wv.{vq}",
            lambda e, src=src, vq=vq: e.dma_start(out=t["wbuf"][vq % 2], in_=src),
            waits=[f"p2mm.{119 + 8 * vq}"] if vq < 2 else [f"p3vmm.{(vq-1)*16 - 1}"])
        for tt in range(16):
            it = vq * 16 + tt

            def mm_v(e, vq=vq, tt=tt, it=it):
                wb = t["wbuf"][vq % 2].rearrange("p (k m) -> p k m", k=32)
                for k in range(32):
                    ii = e.matmul(pb[4 + it % 2][:, 0:256], xn[:, k, tt * 128:(tt + 1) * 128],
                                  wb[:, k, :], start=(k == 0), stop=(k == 31))
                return ii
            op("T", f"p3vmm.{it}", mm_v,
               waits=[f"wv.{vq}"] + ([f"p3vev.{it-2}"] if it >= 2 else [f"p2sil.{126}", f"p2sil.{127}"]))
            op("S", f"p3vev.{it}",
               lambda e, it=it: e.copy(t["ev"][it % 4][:, 0:256], pb[4 + it % 2][:, 0:256]),
               waits=[f"p3vmm.{it}"] + ([f"p3vsp.{it-4}"] if it >= 4 else P2_POOL_FREE),)
            dst = bass.AP(t["v_sp"].tensor, tt * 128 * KVH * 128 + vq * 256,
                          [[KVH * 128, 128], [1, 256]])
            dma("SY", f"p3vsp.{it}",
                lambda e, dst=dst, it=it: e.dma_start(out=dst, in_=t["ev"][it % 4][:, 0:256]),
                waits=[f"p3vev.{it}"])
    for b in range(4):
        src = bass.AP(t["wk_h"].tensor, b * 256,
                      [[KVH * HD, 128], [128 * KVH * HD, 32], [1, 256]])
        dma("G", f"wk.{b}",
            lambda e, src=src, b=b: e.dma_start(out=t["wbuf"][b % 2], in_=src),
            waits=[f"p3vmm.{47 + 16 * b}"] if b < 2 else [f"p3kmm.{(b-1)*8 - 1}"])
        for sub in range(8):
            j2, ts = sub // 4, sub % 4
            j = b * 2 + j2
            it = b * 8 + sub

            def mm_k(e, b=b, j2=j2, ts=ts, it=it):
                wb = t["wbuf"][b % 2].rearrange("p (k m) -> p k m", k=32)
                for k in range(32):
                    ii = e.matmul(pb[4 + it % 2][:], wb[:, k, j2 * 128:j2 * 128 + 128],
                                  xn[:, k, ts * 512:(ts + 1) * 512],
                                  start=(k == 0), stop=(k == 31))
                return ii
            op("T", f"p3kmm.{it}", mm_k,
               waits=[f"wk.{b}"] + ([f"p3kev.{it-2}"] if it >= 2 else [f"p3vev.{62 + it}"]))
            op("S", f"p3kev.{it}",
               lambda e, it=it: e.copy(t["ev"][it % 4], pb[4 + it % 2][:]),
               waits=[f"p3kmm.{it}"] + ([f"p3ksp.{it-4}"] if it >= 4 else [f"p3vsp.{60 + it}"]))
            dst = bass.AP(t["k_sp"].tensor, ts * 4 * 128 * KVH * 128 + j * 128,
                          [[KVH * 128, 128], [128 * KVH * 128, 4], [1, 128]])
            dma("SY", f"p3ksp.{it}",
                lambda e, dst=dst, it=it: e.dma_start(out=dst, in_=t["ev"][it % 4]),
                waits=[f"p3kev.{it}"])
    for b in range(16):
        src = bass.AP(t["wq_h"].tensor, b * 256, [[D, 128], [128 * D, 32], [1, 256]])
        dma("G", f"wqb.{b}",
            lambda e, src=src, b=b: e.dma_start(out=t["wbuf"][b % 2], in_=src),
            waits=[f"p3kmm.{23 + 8 * b}"] if b < 2 else [f"p3qmm.{(b-1)*8 - 1}"])
        for sub in range(8):
            j2, ts = sub // 4, sub % 4
            j = b * 2 + j2
            it = b * 8 + sub

            def mm_q(e, b=b, j2=j2, ts=ts, it=it):
                wb = t["wbuf"][b % 2].rearrange("p (k m) -> p k m", k=32)
                for k in range(32):
                    ii = e.matmul(pb[4 + it % 2][:], wb[:, k, j2 * 128:j2 * 128 + 128],
                                  xn[:, k, ts * 512:(ts + 1) * 512],
                                  start=(k == 0), stop=(k == 31))
                return ii
            op("T", f"p3qmm.{it}", mm_q,
               waits=[f"wqb.{b}"] + ([f"p3qev.{it-2}"] if it >= 2 else [f"p3kev.{30 + it}"]))
            op("S", f"p3qev.{it}",
               lambda e, it=it: e.copy(t["ev"][it % 4], pb[4 + it % 2][:]),
               waits=[f"p3qmm.{it}"] + ([f"p3qsp.{it-4}"] if it >= 4 else [f"p3ksp.{28 + it}"]))
            dst = bass.AP(t["q_sp"].tensor, ts * 4 * 128 * NH * 128 + j * 128,
                          [[NH * 128, 128], [128 * NH * 128, 4], [1, 128]])
            dma("SY", f"p3qsp.{it}",
                lambda e, dst=dst, it=it: e.dma_start(out=dst, in_=t["ev"][it % 4]),
                waits=[f"p3qev.{it}"])
    P3_SPILLED = [f"p3qsp.{i}" for i in range(121, 128)]

    w.seq("G", "s_zero", [
           lambda e: e.memset(t["S"], 0.0),
           lambda e: e.memset(t["Sbf"], 0.0)],
       waits=P3_SPILLED + [f"p3qev.{127}"])
    KW = KVH * 128

    def emit_p5_block(ts, b):
        src = bass.AP(t["wo_h"].tensor, b * 256, [[D, 128], [128 * D, 32], [1, 256]])
        wdep = list(P3_SPILLED) if (ts == 0 and b < 2) else [f"p5mm.{(2*b-3)*4+ts}"] if b >= 2 else [f"p5mm.{(2*(b+14)+1)*4+ts-1}"]
        dma("G", f"wo.{ts}.{b}",
            lambda e, src=src, b=b: e.dma_start(out=t["wobuf"][b % 2], in_=src),
            waits=wdep)

    p5_seq = []

    def emit_p5_jtile(ts, j):
        b, j2 = j // 2, j % 2
        it = j * 4 + ts
        if j2 == 0:
            emit_p5_block(ts, b)
        ei = len(p5_seq)
        prev = p5_seq[-1] if p5_seq else None
        pprev = p5_seq[-2] if len(p5_seq) >= 2 else None
        p5_seq.append(it)
        dma("SY", f"gld.{it}",
            lambda e, j=j, ts=ts, ei=ei: e.dma_start(
                out=t["gld5"][ei % 2], in_=t["g_sp"][j, :, ts * 512:(ts + 1) * 512]),
            waits=[f"p2gs.{it}"] + ([f"sg.{pprev}"] if pprev is not None else []))

        def mm_s(e, b=b, j2=j2, ts=ts):
            wb = t["wobuf"][b % 2].rearrange("p (k m) -> p k m", k=32)
            off = (4 * ts % 8) * 128
            for k in range(32):
                ii = e.matmul(t["pWo"][:], wb[:, k, j2 * 128:j2 * 128 + 128],
                              bass.AP(t["bigA"][:].tensor, k * 1024 + off,
                                      [[32 * NT, 128], [1, 512]]),
                              start=(k == 0), stop=(k == 31))
            return ii
        op("T", f"p5mm.{it}", mm_s,
           waits=[f"wo.{ts}.{b}", f"oev.{4*ts+3}.{KVH-1}"] +
                 ([f"sg.{prev}"] if prev is not None else []))
        op("V", f"sg.{it}",
           lambda e, it=it, ei=ei: e.tensor_tensor(t["sgv5"][ei % 2], t["pWo"][:],
                                                   t["gld5"][ei % 2], ALU.mult),
           waits=[f"p5mm.{it}", f"gld.{it}"] + ([f"sgt.{pprev}"] if pprev is not None else []))
        dma("S", f"sgt.{it}",
            lambda e, it=it, ei=ei: e.dma_start_transpose(
                t["sgt5"][ei % 2].rearrange("p (r q) -> p r q", r=4),
                t["sgv5"][ei % 2]),
            waits=[f"sg.{it}"] + ([f"sgsp.{pprev}"] if pprev is not None else []))
        dst = bass.AP(t["sg_sp"].tensor, ts * 512 * D + j * 128,
                      [[D, 128], [128 * D, 4], [1, 128]])
        dma("SY", f"sgsp.{it}",
            lambda e, dst=dst, it=it, ei=ei: e.dma_start(
                out=dst, in_=t["sgt5"][ei % 2].rearrange("p (r q) -> p r q", r=4)),
            waits=[f"sgt.{it}"])

    def emit_pass2(c, gg, cr):
        hs = slice(gg * 128, (gg + 1) * 128)
        gr = gg % 2

        def mm_o(e, c=c, g=gg, gr=gr, hs=hs):
            ii = e.matmul(pb[2 + gr][:], t["Sbf"][:, hs], t["qtil3"][g % 3], start=True, stop=False)
            ii = e.matmul(pb[2 + gr][:], t["vring"][c % 2][:, hs], t["am3"][g % 3], start=False, stop=False)
            ii = e.matmul(pb[2 + gr][:], t["vring"][c % 2][:, hs], t["pcm3"][g % 3], start=False,
                          stop=(c == 0))
            if c > 0:
                ii = e.matmul(pb[2 + gr][:], t["vring"][(c - 1) % 2][64:128, hs],
                              t["ppm3"][g % 3][64:128, :], start=False, stop=True)
            return ii
        ow = [f"qt.{c}.{gg}", f"am.{c}.{gg}", f"pcn.{c}.{gg}", f"vin.{c}"]
        if c > 0:
            ow += [f"ppn.{c}.{gg}"]
        else:
            ow += ["s_zero"]
        ow += [f"oev.{c}.{gg-2}"] if gg >= 2 else ([f"oev.{c-1}.{KVH-2+gg}"] if c else [])
        if gg == 0 and c >= 8:
            ow += [f"p5mm.{124 + (c - 8) // 4}"]
        op("T", f"o.{c}.{gg}", mm_o, waits=ow)
        op("S", f"oev.{c}.{gg}",
           lambda e, c=c, g=gg, gr=gr: e.copy(
               bass.AP(t["bigA"][:].tensor, 4 * g * 1024 + (c % 8) * 128,
                       [list(t["bigA"][:].ap[0]), [1024, 4], [1, 128]]),
               pb[2 + gr][:].rearrange("p (r tt) -> p r tt", r=4)),
           waits=[f"o.{c}.{gg}"])

        def mm_su(e, c=c, g=gg, hs=hs):
            return e.matmul(pS[:, (g % 4) * 128:(g % 4) * 128 + 128],
                            t["khT"][:, hs], t["vring"][c % 2][:, hs],
                            start=(g % 4 == 0), stop=(g % 4 == 3))
        op("T", f"su.{c}.{gg}", mm_su,
           waits=[f"khTev.{c}", f"vin.{c}", f"o.{c}.{gg}"] +
                 ([f"sadd.{c}.{gg-1}"] if (gg % 4 == 0 and gg >= 4) else []) +
                 ([f"sadd.{c-1}.{KVH-1}"] if (gg == 0 and c > 0) else []) +
                 (["s_zero"] if (gg == 0 and c == 0) else []))
        if gg % 4 == 3:
            w.seq("S", f"sdec.{c}.{gg}", [
                   (lambda e, H=H: e.activation(t["S"][:, H * 128:(H + 1) * 128],
                                                t["S"][:, H * 128:(H + 1) * 128],
                                                AF.Identity, scale=t["explend"][:, H:H + 1]))
                   for H in range(gg - 3, gg + 1)],
               waits=[f"explend.{c}", f"su.{c}.{gg}"] +
                     ([f"sbf.{c-1}"] if c else ["s_zero"]))
            op("V", f"sadd.{c}.{gg}",
               lambda e, c=c, g=gg: e.tensor_tensor(
                   t["S"][:, (g - 3) * 128:(g + 1) * 128],
                   t["S"][:, (g - 3) * 128:(g + 1) * 128], pS[:], ALU.add),
               waits=[f"sdec.{c}.{gg}", f"su.{c}.{gg}"])

    for c in range(NCH):
        cr = c % 2
        dma("SY", f"kin.{c}",
            lambda e, c=c: e.dma_start(out=t["kin"], in_=t["k_sp"][c, :, :]),
            waits=P3_SPILLED + ([f"p4kdone.{c-1}"] if c else []))
        dma("SY", f"vin.{c}",
            lambda e, c=c, cr=cr: e.dma_start(out=t["vring"][cr], in_=t["v_sp"][c, :, :]),
            waits=(P3_SPILLED + ["s_zero"] if c < 2 else []) +
                  ([f"p4odone.{c-1}"] if c >= 1 else []) +
                  ([f"sadd.{c-2}.{KVH-1}"] if c >= 2 else []))
        op("S", f"sig.{c}", lambda e: e.activation(t["ksig"], t["kin"], AF.Sigmoid),
           waits=[f"kin.{c}"])
        op("S", f"lsig.{c}", lambda e: e.activation(t["ksig"], t["ksig"], AF.Ln), waits=[])
        w.seq("V", f"lam.{c}", [
               (lambda e, H=H: e.tensor_tensor_scan(t["lam"][:, H * 128:(H + 1) * 128],
                                                    t["onesf_sb"][:], t["ksig"][:, H * 128:(H + 1) * 128],
                                                    0.0, ALU.mult, ALU.add)) for H in range(KVH)],
           waits=[f"lsig.{c}", "c_onesf"])
        op("S", f"expl.{c}", lambda e: e.activation(t["expl"], t["lam"], AF.Exp, scale=INV16),
           waits=[f"lam.{c}"])
        op("S", f"explend.{c}",
           lambda e: e.activation(t["explend"],
                                  bass.AP(t["lam"].tensor, t["lam"].offset + 127,
                                          [list(t["lam"].ap[0]), [128, 8]]),
                                  AF.Exp, scale=INV16),
           waits=[f"lam.{c}"] + ([f"sdec.{c-1}.{KVH-1}"] if c else []))
        op("S", f"expli.{c}", lambda e: e.activation(t["expli"], t["lam"], AF.Exp, scale=-INV16),
           waits=[f"lam.{c}"])
        op("S", f"ek.{c}", lambda e: e.activation(t["ekt"], t["kin"], AF.Exp),
           waits=[f"kin.{c}"])
        op("T", f"denk.{c}",
           lambda e: (e.matmul(pb[4][:], t["ones_sb"][:], t["ekt"][:, 0:512], start=True, stop=True),
                      e.matmul(pb[5][:], t["ones_sb"][:], t["ekt"][:, 512:1024], start=True, stop=True))[-1],
           waits=[f"ek.{c}", "c_ones"] + ([f"rs.{c-1}.{KVH-2}", f"rs.{c-1}.{KVH-1}"] if c else []))
        w.seq("S", f"rk.{c}", [
               lambda e: e.activation(t["lnscr"], pb[4][:], AF.Ln),
               lambda e: e.activation(t["rkt2"][:, 0:512], t["lnscr"], AF.Exp, scale=-1.0),
               lambda e: e.activation(t["lnscr"], pb[5][:], AF.Ln),
               lambda e: e.activation(t["rkt2"][:, 512:1024], t["lnscr"], AF.Exp, scale=-1.0)],
           waits=[f"denk.{c}"])
        w.seq("V", f"kt.{c}", [
               lambda e: e.tensor_tensor(t["ktmp"].bitcast(BF16)[:, 0:1024], t["ekt"], t["expli"], ALU.mult),
               lambda e: e.tensor_tensor(t["ktil"], t["ktmp"].bitcast(BF16)[:, 0:1024], t["rkt2"], ALU.mult)],
           waits=[f"rk.{c}", f"expli.{c}"])
        op("V", f"kh.{c}",
           lambda e: e.tensor_tensor(
               t["khat"].rearrange("p (H tt) -> p H tt", H=KVH),
               t["ktil"].rearrange("p (H tt) -> p H tt", H=KVH),
               _ins(t["explend"], 1, 0, 128), ALU.mult),
           waits=[f"kt.{c}", f"explend.{c}"])
        op("T", f"rotk.{c}",
           lambda e: (e.matmul(pb[4][:], t["rmat_sb"][:], t["kin"][:, 0:512], start=True, stop=True),
                      e.matmul(pb[5][:], t["rmat_sb"][:], t["kin"][:, 512:1024], start=True, stop=True))[-1],
           waits=[f"kin.{c}", "c_rmat", f"rk.{c}"])
        cosc = _ins(t["cos_sb"][:, c * 128:(c + 1) * 128], 0, 0, KVH)
        sinc = _ins(t["sin_sb"][:, c * 128:(c + 1) * 128], 0, 0, KVH)
        ktc = t["ktmp"].bitcast(BF16)[:, 0:1024]
        w.seq("V", f"sk.{c}", [
               lambda e, cosc=cosc, ktc=ktc: e.tensor_tensor(
                   ktc.rearrange("p (H tt) -> p H tt", H=KVH),
                   t["kin"].rearrange("p (H tt) -> p H tt", H=KVH), cosc, ALU.mult),
               lambda e, sinc=sinc: e.tensor_tensor(
                   t["krot"][:, 0:512].rearrange("p (r tt) -> p r tt", r=4),
                   pb[4][:].rearrange("p (r tt) -> p r tt", r=4), sinc[:, 0:4, :], ALU.mult),
               lambda e, sinc=sinc: e.tensor_tensor(
                   t["krot"][:, 512:1024].rearrange("p (r tt) -> p r tt", r=4),
                   pb[5][:].rearrange("p (r tt) -> p r tt", r=4), sinc[:, 4:8, :], ALU.mult),
               lambda e, cr=cr, ktc=ktc: e.tensor_tensor(t["skring"][cr], ktc, t["krot"], ALU.add)],
           waits=[f"rotk.{c}", "c_cos", "c_sin", f"kh.{c}"] +
                 ([f"scp.{c-1}.{KVH-1}"] if c >= 2 else []))
        op("T", f"khT.{c}",
           lambda e: [e.transpose(pb[0][:].bitcast(BF16)[:, H * 128:(H + 1) * 128],
                                  t["khat"][:, H * 128:(H + 1) * 128], t["id128_sb"][:])
                      for H in range(KVH)][-1],
           waits=[f"kh.{c}", "c_id128", f"sk.{c}"] + ([f"khTev.{c-1}"] if c else []) +
                 ([f"am.{c-1}.{KVH-2}"] if c else []))
        op("S", f"khTev.{c}", lambda e: e.copy(t["khT"], pb[0][:].bitcast(BF16)),
           waits=[f"khT.{c}"] + ([f"sadd.{c-1}.{KVH-1}"] if c else []))
        op("V", f"p4kdone.{c}", lambda e: e.tensor_copy(t["ktmp"].bitcast(BF16)[:, 0:1], t["ktil"][:, 0:1]),
           waits=[f"sk.{c}", f"kh.{c}", f"ek.{c}"])

        for g in range(KVH):
            gr = g % 2
            hs = slice(g * 128, (g + 1) * 128)
            dma("SY", f"qin.{c}.{g}",
                lambda e, c=c, g=g, gr=gr: e.dma_start(
                    out=t["qg"][gr],
                    in_=bass.AP(t["q_sp"].tensor, c * 128 * NH * 128 + g * 512,
                                [[NH * 128, 128], [1, 512]])),
                waits=P3_SPILLED + ([f"p4qfree.{c}.{g-2}"] if g >= 2 else
                                    ([f"p4qfree.{c-1}.{KVH-2+g}"] if c else [])))
            op("S", f"eq.{c}.{g}", lambda e, gr=gr, g=g: e.activation(t["eq2"][g % 2], t["qg"][gr], AF.Exp),
               waits=[f"qin.{c}.{g}"])
            op("T", f"denq.{c}.{g}",
               lambda e, gr=gr, g=g: e.matmul(pb[4 + gr][:], t["twos_sb"][:], t["eq2"][g % 2], start=True, stop=True),
               waits=[f"eq.{c}.{g}", "c_twos"] +
                     ([f"rs.{c}.{g-2}"] if g >= 2 else [f"sk.{c}"]))
            w.seq("S", f"p4rq.{c}.{g}", [
                   lambda e, gr=gr: e.activation(t["lnscr"], pb[4 + gr][:], AF.Ln),
                   lambda e, g=g: e.activation(t["rq2"][g % 2], t["lnscr"], AF.Exp, scale=-1.0)],
               waits=[f"denq.{c}.{g}"])
            expl_g = _ins(t["expl"][:, g * 128:(g + 1) * 128], 0, 0, 4)
            w.seq("V", f"qt.{c}.{g}", [
                   lambda e, expl_g=expl_g, g=g: e.tensor_tensor(
                       t["qt2"].rearrange("p (r tt) -> p r tt", r=4),
                       t["eq2"][g % 2].rearrange("p (r tt) -> p r tt", r=4), expl_g, ALU.mult),
                   lambda e, g=g: e.tensor_tensor(t["qtil3"][g % 3], t["qt2"], t["rq2"][g % 2], ALU.mult)],
               waits=[f"p4rq.{c}.{g}", f"expl.{c}"])
            op("T", f"rotq.{c}.{g}",
               lambda e, gr=gr, g=g: e.matmul(pb[g % 2][:], t["rmat_sb"][:], t["qg"][gr], start=True, stop=True),
               waits=[f"qin.{c}.{g}", "c_rmat"] +
                     ([f"am.{c}.{g-2}"] if g >= 2 else
                      ([f"am.{c-1}.{6+g}"] if c else [])) +
                     ([f"khTev.{c}"] if g == 0 else []))
            cos4 = _ins(t["cos_sb"][:, c * 128:(c + 1) * 128], 0, 0, 4)
            sin4 = _ins(t["sin_sb"][:, c * 128:(c + 1) * 128], 0, 0, 4)
            w.seq("V", f"sq.{c}.{g}", [
                   lambda e, gr=gr, cos4=cos4: e.tensor_tensor(
                       t["sq2"].rearrange("p (r tt) -> p r tt", r=4),
                       t["qg"][gr].rearrange("p (r tt) -> p r tt", r=4), cos4, ALU.mult),
                   lambda e, g=g, sin4=sin4: e.tensor_tensor(
                       t["sq3"].rearrange("p (r tt) -> p r tt", r=4),
                       pb[g % 2][:].rearrange("p (r tt) -> p r tt", r=4), sin4, ALU.mult),
                   lambda e, g=g: e.tensor_tensor(t["sq1"][g % 2], t["sq2"], t["sq3"], ALU.add)],
               waits=[f"rotq.{c}.{g}", f"qt.{c}.{g}"])
            op("V", f"p4qfree.{c}.{g}", lambda e, g=g: e.tensor_copy(t["qt2"][:, 0:1], t["sq1"][g % 2][:, 0:1]),
               waits=[f"sq.{c}.{g}"])

            def mm_sc(e, cr=cr, hs=hs, g=g):
                e.matmul(pb[g % 2][:], t["skring"][cr][:, hs], t["sq1"][g % 2], start=True, stop=False)
                return e.matmul(pb[g % 2][:], t["mcur_sb"][:], t["i4rep_sb"][:],
                                start=False, stop=True)
            op("T", f"sc.{c}.{g}", mm_sc,
               waits=[f"sq.{c}.{g}", f"sk.{c}", "c_mcur", "c_i4rep"])
            if c > 0:
                def mm_scp(e, c=c, g=g, gr=gr):
                    e.matmul(pb[4 + gr][64:128, :],
                             t["skring"][(c - 1) % 2][:, g * 128 + 64:g * 128 + 128],
                             t["sq1"][g % 2], start=True, stop=False, tile_position=(0, 64))
                    return e.matmul(pb[4 + gr][64:128, :], t["mprev_sb"][:, 0:64],
                                    t["i4rep_sb"][:], start=False, stop=True,
                                    tile_position=(0, 64))
                op("T", f"scp.{c}.{g}", mm_scp,
                   waits=[f"sq.{c}.{g}", f"p4rq.{c}.{g}", "c_mprev", "c_i4rep"])
            op("S", f"esc.{c}.{g}",
               lambda e, g=g: e.activation(t["esc2"][g % 2], pb[g % 2][:], AF.Exp, scale=SSCALE),
               waits=[f"sc.{c}.{g}"] + ([f"pcn.{c}.{g-2}"] if g >= 2 else []))
            if c > 0:
                op("S", f"esp.{c}.{g}",
                   lambda e, gr=gr, g=g: e.activation(t["esp2"][g % 2][64:128, :], pb[4 + gr][64:128, :], AF.Exp, scale=SSCALE),
                   waits=[f"scp.{c}.{g}"] + ([f"ppn.{c}.{g-2}"] if g >= 2 else []))

            def mm_dsw(e, c=c, gr=gr, g=g):
                ii = e.matmul(pb[4 + gr][:], t["twos_sb"][:], t["esc2"][g % 2], start=True,
                              stop=(c == 0))
                if c > 0:
                    ii = e.matmul(pb[4 + gr][:], t["twos_sb"][64:128, :], t["esp2"][g % 2][64:128, :],
                                  start=False, stop=True)
                return ii
            op("T", f"dsw.{c}.{g}", mm_dsw,
               waits=[f"esc.{c}.{g}", f"p4rq.{c}.{g}"] + ([f"esp.{c}.{g}"] if c else []))
            w.seq("S", f"rs.{c}.{g}", [
                   lambda e, gr=gr: e.activation(t["lnscr"], pb[4 + gr][:], AF.Ln),
                   lambda e, g=g: e.activation(t["rs2"][g % 2], t["lnscr"], AF.Exp, scale=-1.0)],
               waits=[f"dsw.{c}.{g}"])
            op("V", f"pcn.{c}.{g}",
               lambda e, g=g: e.tensor_tensor(t["pcm3"][g % 3], t["esc2"][g % 2], t["rs2"][g % 2], ALU.mult),
               waits=[f"rs.{c}.{g}"])
            if c > 0:
                op("V", f"ppn.{c}.{g}",
                   lambda e, g=g: e.tensor_tensor(t["ppm3"][g % 3][64:128, :], t["esp2"][g % 2][64:128, :],
                                                  t["rs2"][g % 2][64:128, :], ALU.mult),
                   waits=[f"rs.{c}.{g}"])
            op("T", f"at.{c}.{g}",
               lambda e, hs=hs, g=g: e.matmul(pb[g % 2][:], t["ktil"][:, hs], t["qtil3"][g % 3],
                                              start=True, stop=True),
               waits=[f"qt.{c}.{g}", f"kt.{c}", f"esc.{c}.{g}"])
            tril4 = _ins(t["tril_sb"][:], 0, 0, 4)
            op("V", f"am.{c}.{g}",
               lambda e, g=g, tril4=tril4: e.tensor_tensor(
                   t["am3"][g % 3].rearrange("p (r tt) -> p r tt", r=4),
                   pb[g % 2][:].rearrange("p (r tt) -> p r tt", r=4), tril4, ALU.mult),
               waits=[f"at.{c}.{g}", "c_tril"])
            if g >= 2:
                emit_pass2(c, g - 2, cr)
        emit_pass2(c, KVH - 2, cr)
        emit_pass2(c, KVH - 1, cr)
        op("S", f"sbf.{c}", lambda e: e.copy(t["Sbf"], t["S"]),
           waits=[f"sadd.{c}.{KVH-1}"])
        op("V", f"p4odone.{c}", lambda e: e.tensor_copy(t["qt2"][:, 0:1], t["qtil3"][0][:, 0:1]),
           waits=[f"oev.{c}.{KVH-1}", f"oev.{c}.{KVH-2}"])
        if c >= 4:
            ts = (c - 4) // 4
            j0 = 8 * ((c - 4) % 4)
            for j in range(j0, j0 + 8):
                emit_p5_jtile(ts, j)
    for j in range(32):
        emit_p5_jtile(3, j)
    P4_END = [f"p4odone.{NCH-1}", f"sadd.{NCH-1}.{KVH-1}", f"sbf.{NCH-1}",
              f"p4kdone.{NCH-1}", f"khTev.{NCH-1}", f"p4qfree.{NCH-1}.{KVH-1}"]

    for i2 in range(2 * NTT):
        i, hf = i2 // 2, i2 % 2
        dma("SY", f"p6sg.{i2}",
            lambda e, i=i, hf=hf, i2=i2: e.dma_start(
                out=t["sgh"][i2 % 2],
                in_=t["sg_sp"][i * 128:(i + 1) * 128, hf * 2048:(hf + 1) * 2048]),
            waits=[f"sgsp.{j*4+3}" for j in range(26, 32)] + [f"gld.{j*4+3}" for j in range(29, 32)]
                  + ([f"p6add.{i2-2}"] if i2 >= 2 else []))
        dma("SY", f"p6x.{i2}",
            lambda e, i=i, hf=hf, i2=i2: e.dma_start(
                out=t["xf"][i2 % 2],
                in_=t["x_h"][i * 128:(i + 1) * 128, hf * 2048:(hf + 1) * 2048]),
            waits=([f"p6add.{i2-2}"] if i2 >= 2 else [f"sg.{127}"]))
        op("V", f"p6add.{i2}",
           lambda e, i2=i2: e.tensor_tensor(t["of"][i2 % 2], t["xf"][i2 % 2],
                                            t["sgh"][i2 % 2], ALU.add),
           waits=[f"p6sg.{i2}", f"p6x.{i2}"] + ([f"p6st.{i2-2}"] if i2 >= 2 else [f"sgsp.{127}"]))
        dma("SY", f"p6st.{i2}",
            lambda e, i=i, hf=hf, i2=i2: e.dma_start(
                out=t["out_h"][i * 128:(i + 1) * 128, hf * 2048:(hf + 1) * 2048],
                in_=t["of"][i2 % 2]),
            waits=[f"p6add.{i2}"])
    op("SY", "fin", lambda e: e.nop(), waits=[f"p6st.{2*NTT-1}"])


def build_nc():
    nc = bass.Bass("TRN2", target_bir_lowering=False)
    t = {}
    t["x_h"] = nc.dram_tensor("x", [NT, D], F32, kind="ExternalInput").ap()
    t["wq_h"] = nc.dram_tensor("wq", [D, D], F32, kind="ExternalInput").ap()
    t["wk_h"] = nc.dram_tensor("wk", [D, KVH * HD], F32, kind="ExternalInput").ap()
    t["wv_h"] = nc.dram_tensor("wv", [D, KVH * HD], F32, kind="ExternalInput").ap()
    t["wo_h"] = nc.dram_tensor("wo", [D, D], F32, kind="ExternalInput").ap()
    t["w0t_h"] = nc.dram_tensor("w0t", [D, D], F32, kind="ExternalInput").ap()
    t["acat_h"] = nc.dram_tensor("acat", [D, 128], F32, kind="ExternalInput").ap()
    t["bt_h"] = nc.dram_tensor("bt", [128, D], F32, kind="ExternalInput").ap()
    t["b0_h"] = nc.dram_tensor("b0fm", [128, 32], F32, kind="ExternalInput").ap()
    t["maskt_h"] = nc.dram_tensor("maskt", [4, D, NT], BF16, kind="ExternalInput").ap()
    t["cos_h"] = nc.dram_tensor("cosfm", [128, NT], BF16, kind="ExternalInput").ap()
    t["sin_h"] = nc.dram_tensor("sinfm", [128, NT], BF16, kind="ExternalInput").ap()
    t["rmat_h"] = nc.dram_tensor("rmat", [128, 128], BF16, kind="ExternalInput").ap()
    t["tril_h"] = nc.dram_tensor("trilT", [128, 128], BF16, kind="ExternalInput").ap()
    t["mcur_h"] = nc.dram_tensor("mcurT", [128, 128], BF16, kind="ExternalInput").ap()
    t["mprev_h"] = nc.dram_tensor("mprevT", [128, 128], BF16, kind="ExternalInput").ap()
    t["i4rep_h"] = nc.dram_tensor("i4rep", [128, 512], BF16, kind="ExternalInput").ap()
    t["ones_h"] = nc.dram_tensor("ones128", [128, 128], BF16, kind="ExternalInput").ap()
    t["twos_h"] = nc.dram_tensor("twos128", [128, 128], BF16, kind="ExternalInput").ap()
    t["onesf_h"] = nc.dram_tensor("onesf128", [128, 128], F32, kind="ExternalInput").ap()
    t["id128_h"] = nc.dram_tensor("id128", [128, 128], BF16, kind="ExternalInput").ap()
    t["out_h"] = nc.dram_tensor("out", [NT, D], F32, kind="ExternalOutput").ap()
    t["q_sp"] = nc.dram_tensor("q_sp", [NCH, 128, NH * 128], BF16).ap()
    t["k_sp"] = nc.dram_tensor("k_sp", [NCH, 128, KVH * 128], BF16).ap()
    t["v_sp"] = nc.dram_tensor("v_sp", [NCH, 128, KVH * 128], BF16).ap()
    t["g_sp"] = nc.dram_tensor("g_sp", [32, 128, NT], F32).ap()
    t["sg_sp"] = nc.dram_tensor("sg_sp", [NT, D], BF16).ap()

    ctx = ExitStack()
    with ctx:
        def sb(name, shape, dt=BF16):
            h = ctx.enter_context(nc.sbuf_tensor(name, shape, dt))
            t[name] = h
            return h
        sb("bigA", [128, 32 * NT])
        for nm, shape, dt in [("cos_sb", [128, NT], BF16), ("sin_sb", [128, NT], BF16),
                              ("rmat_sb", [128, 128], BF16), ("tril_sb", [128, 128], BF16),
                              ("mcur_sb", [128, 128], BF16), ("mprev_sb", [128, 128], BF16),
                              ("i4rep_sb", [128, 512], BF16),
                              ("ones_sb", [128, 128], BF16), ("twos_sb", [128, 128], BF16),
                              ("onesf_sb", [128, 128], F32), ("id128_sb", [128, 128], BF16),
                              ("b0_sb", [128, 32], F32), ("st1", [128, 16], F32)]:
            sb(nm, shape, dt)
        pool = ctx.enter_context(nc.sbuf_tensor("pool", [128, 34816], BF16))

        def V_(off, n, dt=BF16):
            ap = pool[:, off:off + n]
            if dt == F32:
                ap = ap.bitcast(F32)
            return ap
        t["xt"] = [V_(0, 8192, F32), V_(8192, 8192, F32)]
        t["xc"] = V_(16384, 8192, F32)
        t["xn_tm"] = [V_(24576, 4096), V_(28672, 4096)]
        t["wbuf"] = [V_(0, 8192), V_(8192, 8192)]
        t["mbig"] = [V_(16384, 2048), V_(18432, 2048)]
        t["acat"] = V_(20480, 4096)
        t["bt"] = V_(24576, 4096)
        t["h"] = V_(28672, 2048)
        t["tl"] = [V_(30720 + 512 * i, 512) for i in range(4)]
        t["gv"] = [V_(32768, 1024, F32), V_(33792, 1024, F32)]
        t["ev"] = [V_(16384 + 512 * i, 512) for i in range(4)]
        go = 0

        def GV(n, dt=BF16):
            nonlocal go
            v = V_(go, n, dt)
            go += n
            return v
        t["qg"] = [GV(512), GV(512)]
        t["eq2"] = [GV(512), GV(512)]
        t["rq2"] = [GV(512), GV(512)]
        t["lnscr"] = GV(1024, F32)
        t["qtil3"] = [GV(512) for _ in range(3)]
        t["qt2"] = GV(512)
        t["sq1"] = [GV(512), GV(512)]
        t["sq2"] = t["qt2"]
        t["sq3"] = GV(512)
        t["esc2"] = [GV(512), GV(512)]
        t["rs2"] = [GV(512), GV(512)]
        t["pcm3"] = [GV(512) for _ in range(3)]
        t["am3"] = [GV(512) for _ in range(3)]
        t["esp2"] = [GV(512), GV(512)]
        t["ppm3"] = [GV(512) for _ in range(3)]
        ko = go

        def KV(n, dt=BF16):
            nonlocal ko
            v = V_(ko, n, dt)
            ko += n
            return v
        t["kin"] = KV(1024)
        t["ksig"] = KV(2048, F32)
        t["lam"] = KV(2048, F32)
        t["expl"] = KV(1024)
        t["expli"] = t["lam"].bitcast(BF16)[:, 0:1024]
        t["explend"] = t["st1"][:, 8:16]
        t["ekt"] = KV(1024)
        t["rkt2"] = KV(1024)
        t["krot"] = t["ekt"]
        t["ktil"] = KV(1024)
        t["khat"] = KV(1024)
        t["khT"] = KV(1024)
        t["ktmp"] = t["ksig"]
        t["skring"] = [KV(1024), KV(1024)]
        t["vring"] = [KV(1024), KV(1024)]
        t["S"] = KV(2048, F32)
        t["Sbf"] = KV(1024)
        assert ko <= 34816, ko
        def BV(off, n, dt=BF16):
            ap = t["bigA"][:][:, off:off + n]
            if dt == F32:
                ap = ap.bitcast(F32)
            return ap
        t["wobuf"] = [BV(32768, 8192), BV(40960, 8192)]
        t["sgv5"] = [BV(49152, 512), BV(49664, 512)]
        t["gld5"] = [BV(50176, 1024, F32), BV(51200, 1024, F32)]
        t["sgt5"] = [BV(52224, 512), BV(52736, 512)]
        t["xf"] = [V_(0, 4096, F32), V_(4096, 4096, F32)]
        t["of"] = [V_(8192, 4096, F32), V_(12288, 4096, F32)]
        t["sgh"] = [V_(16384, 2048), V_(18432, 2048)]
        t["pb"] = [ctx.enter_context(nc.psum_tensor(f"pb{i}", [128, 512], F32)) for i in range(6)]
        t["pS"] = ctx.enter_context(nc.psum_tensor("pS", [128, 512], F32))
        t["pWo"] = ctx.enter_context(nc.psum_tensor("pWo", [128, 512], F32))

        sems = {}
        for s in ALL_SEMS:
            sems[s] = ctx.enter_context(nc.semaphore(s))

        wA = Walk("collect")
        build_walk(wA, t)
        wB = Walk("count")
        wB.waited = wA.waited
        build_walk(wB, t)

        ctx.enter_context(nc.allow_low_precision(
            reason="bf16 matmul operands with fp32 PSUM per stated tolerance"))
        block = ctx.enter_context(nc.Block())
        emitters = {"T": block.tensor, "V": block.vector, "S": block.scalar,
                    "SY": block.sync, "G": block.gpsimd}
        for stream in STREAMS:
            def body(eng, stream=stream):
                wE = Walk("emit", stream=stream, eng=eng, sems=sems)
                wE.waited = wA.waited
                wE.evt = wB.evt
                build_walk(wE, t)
            emitters[stream](body)
    return nc


def _rotl32(x, r):
    return ((x << np.uint32(r)) | (x >> np.uint32(32 - r))) & np.uint32(0xFFFFFFFF)


def _threefry2x32(k0, k1, x0, x1):
    rot = [[13, 15, 26, 6], [17, 29, 16, 24]]
    ks2 = np.uint32(k0) ^ np.uint32(k1) ^ np.uint32(0x1BD11BDA)
    ks = [np.uint32(k0), np.uint32(k1), ks2]
    x0 = (x0 + ks[0]) & np.uint32(0xFFFFFFFF)
    x1 = (x1 + ks[1]) & np.uint32(0xFFFFFFFF)
    for d in range(5):
        for r in rot[d % 2]:
            x0 = (x0 + x1) & np.uint32(0xFFFFFFFF)
            x1 = _rotl32(x1, r) ^ x0
        x0 = (x0 + ks[(d + 1) % 3]) & np.uint32(0xFFFFFFFF)
        x1 = (x1 + ks[(d + 2) % 3] + np.uint32(d + 1)) & np.uint32(0xFFFFFFFF)
    return x0, x1


def _dropout_uniform():
    n = 8 * NT * D
    i = np.arange(n, dtype=np.uint32)
    b1, b2 = _threefry2x32(0, 42, np.zeros(n, np.uint32), i)
    bits = b1 ^ b2
    return (((bits >> np.uint32(9)) | np.uint32(0x3F800000)).view(np.float32)
            - 1.0).reshape(8, NT, D)


def _host_prep(inputs):
    x = np.ascontiguousarray(inputs["L0_attn_hidden_states"])
    W0t = np.ascontiguousarray(inputs["L0_ffn_W0"].T)
    b0 = np.asarray(inputs["L0_ffn_b0"])
    A = np.asarray(inputs["L0_ffn_A"])
    B = np.asarray(inputs["L0_ffn_B"])

    acat = np.zeros((D, 128), np.float32)
    bt = np.zeros((128, D), np.float32)
    for l in range(4):
        acat[:, 32 * l:32 * l + 16] = A[l].T
        bt[32 * l:32 * l + 16, :] = B[l].T
    b0fm = np.ascontiguousarray(b0.reshape(32, 128).T)

    u = _dropout_uniform()
    keep = 0.25
    maskt = np.zeros((8, 4, D, NT), BFNP)
    for l in range(4):
        m = ((u >= l * keep) & (u < (l + 1) * keep)).astype(np.float32) * 8.0
        maskt[:, l] = m.transpose(0, 2, 1).astype(BFNP)

    half = np.arange(HD // 2, dtype=np.float32)
    inv = 1.0 / 10000.0 ** (2.0 * half / HD)
    fr = np.arange(NT, dtype=np.float32)[:, None] * inv[None, :]
    emb = np.concatenate([fr, fr], axis=-1)
    cosfm = np.ascontiguousarray(np.cos(emb).T).astype(BFNP)
    sinfm = np.ascontiguousarray(np.sin(emb).T).astype(BFNP)

    rmat = np.zeros((128, 128), np.float32)
    for dd in range(64):
        rmat[dd + 64, dd] = -1.0
        rmat[dd, dd + 64] = 1.0
    tau = np.arange(128)
    tril = (tau[:, None] <= tau[None, :]).astype(np.float32)
    mcur_valid = ((tau[None, :] - tau[:, None] >= 0) &
                  (tau[None, :] - tau[:, None] <= 64))
    mcur = np.where(mcur_valid, 0.0, -30000.0).astype(np.float32).T
    i64 = np.arange(64)
    mprev_valid = (i64[:, None] >= tau[None, :])
    mprev = np.zeros((128, 128), np.float32)
    mprev[:, 0:64] = np.where(mprev_valid, 0.0, -30000.0).astype(np.float32).T
    i4 = np.concatenate([np.eye(128, dtype=np.float32)] * 4, axis=1)

    common = {
        "wq": np.ascontiguousarray(inputs["L0_attn_Wq"]),
        "wk": np.ascontiguousarray(inputs["L0_attn_Wk"]),
        "wv": np.ascontiguousarray(inputs["L0_attn_Wv"]),
        "wo": np.ascontiguousarray(inputs["L0_attn_Wo"]),
        "w0t": W0t, "acat": acat, "bt": bt, "b0fm": b0fm,
        "cosfm": cosfm, "sinfm": sinfm,
        "rmat": rmat.astype(BFNP), "trilT": tril.astype(BFNP),
        "mcurT": mcur.astype(BFNP), "mprevT": mprev.astype(BFNP),
        "i4rep": i4.astype(BFNP),
        "ones128": np.ones((128, 128), BFNP), "twos128": np.full((128, 128), 2.0, BFNP),
        "onesf128": np.ones((128, 128), np.float32),
        "id128": np.eye(128).astype(BFNP),
    }
    in_maps = []
    for bi in range(8):
        m = dict(common)
        m["x"] = np.ascontiguousarray(x[bi])
        m["maskt"] = np.ascontiguousarray(maskt[bi])
        in_maps.append(m)
    return in_maps


def kernel(L0_attn_hidden_states, L0_attn_Wq, L0_attn_Wk, L0_attn_Wv, L0_attn_Wo,
           L0_ffn_W0, L0_ffn_b0, L0_ffn_A, L0_ffn_B):
    inputs = dict(L0_attn_hidden_states=L0_attn_hidden_states, L0_attn_Wq=L0_attn_Wq,
                  L0_attn_Wk=L0_attn_Wk, L0_attn_Wv=L0_attn_Wv, L0_attn_Wo=L0_attn_Wo,
                  L0_ffn_W0=L0_ffn_W0, L0_ffn_b0=L0_ffn_b0, L0_ffn_A=L0_ffn_A,
                  L0_ffn_B=L0_ffn_B)
    in_maps = _host_prep(inputs)
    nc = build_nc()
    res = run_bass_kernel_spmd(nc, in_maps, core_ids=list(range(8)))
    out = np.stack([np.asarray(r["out"]) for r in res.results], axis=0)
    return out.astype(np.float32)


# version 09
# speedup vs baseline: 1.0453x; 1.0453x over previous
import numpy as np
import ml_dtypes
import concourse.bass as bass
import concourse.mybir as mybir
from concourse.bass_utils import run_bass_kernel_spmd
from contextlib import ExitStack

F32 = mybir.dt.float32
BF16 = mybir.dt.bfloat16
AF = mybir.ActivationFunctionType
ALU = mybir.AluOpType
AX = mybir.AxisListType
BFNP = ml_dtypes.bfloat16

D = 4096
NT = 2048
NH, KVH, HD = 32, 8, 128
CH = 128
NCH = NT // CH
NTT = NT // 128
SSCALE = float(1.0 / np.sqrt(np.float32(HD)))
INV16 = 1.0 / 16.0

STREAMS = ("T", "V", "S", "SY", "G")
STREAM_SEM = {"T": "c_t", "V": "c_v", "S": "c_s", "SY": "c_sy", "G": "c_g"}
DMA_POOL = {"S": [f"d_s{i}" for i in range(2)],
            "SY": [f"d_y{i}" for i in range(6)],
            "G": [f"d_g{i}" for i in range(4)]}
ALL_SEMS = (list(STREAM_SEM.values())
            + [x for p in DMA_POOL.values() for x in p])


class Walk:
    def __init__(self, mode, stream=None, eng=None, sems=None):
        self.mode = mode
        self.stream = stream
        self.eng = eng
        self.sems = sems
        self.waited = set()
        self.evt = {}
        self.C = {s: 0 for s in ALL_SEMS}
        self.dn = {s: 0 for s in DMA_POOL}
        self.hwm = {}

    def _wait(self, names, is_dma=False):
        for w in names:
            if self.mode == "collect":
                self.waited.add(w)
            elif self.mode == "emit":
                if w not in self.evt:
                    raise KeyError(f"wait on unknown event {w}")
                sem, val = self.evt[w]
                if not is_dma and sem == STREAM_SEM.get(self.stream):
                    continue
                if self.hwm.get(sem, -1) >= val:
                    continue
                self.eng.wait_ge(self.sems[sem], val)
                self.hwm[sem] = val

    def op(self, stream, name, emit_fn, waits=()):
        serial = stream in ("V", "S", "G")
        sem = STREAM_SEM[stream]
        if self.mode == "collect":
            self._wait(waits)
            return
        if self.mode == "count":
            if serial:
                self.C[sem] += 1
                if name is not None:
                    self.evt[name] = (sem, self.C[sem])
            elif name is not None and name in self.waited:
                self.C[sem] += 1
                self.evt[name] = (sem, self.C[sem])
            return
        if self.stream != stream:
            return
        self._wait(waits)
        if serial and self.C[sem] > 0:
            self.eng.wait_ge(self.sems[sem], self.C[sem])
        instr = emit_fn(self.eng)
        if serial:
            self.C[sem] += 1
            instr.then_inc(self.sems[sem], 1)
        elif name is not None and name in self.waited:
            self.C[sem] += 1
            instr.then_inc(self.sems[sem], 1)

    def seq(self, stream, name, fns, waits=()):
        for i, fn in enumerate(fns):
            self.op(stream, name if i == len(fns) - 1 else None, fn,
                    waits=waits if i == 0 else ())

    def dma(self, stream, name, emit_fn, waits=()):
        if self.mode == "collect":
            self._wait(waits)
            return
        pool = DMA_POOL[stream]
        sem = pool[self.dn[stream] % len(pool)]
        self.dn[stream] += 1
        if self.mode == "count":
            self.C[sem] += 16
            self.evt[name] = (sem, self.C[sem])
            return
        if self.stream != stream:
            return
        self._wait(waits, is_dma=True)
        if self.C[sem] > 0 and self.hwm.get(sem, -1) < self.C[sem]:
            self.eng.wait_ge(self.sems[sem], self.C[sem])
            self.hwm[sem] = self.C[sem]
        instr = emit_fn(self.eng)
        self.C[sem] += 16
        instr.then_inc(self.sems[sem], 16)


def _ins(ap, idx, step, count):
    dims = [list(d) for d in ap.ap]
    dims.insert(1 + idx, [step, count])
    return bass.AP(ap.tensor, ap.offset, dims)


def build_walk(w, t):
    op, dma = w.op, w.dma
    pb = t["pb"]
    pS = t["pS"]

    CONSTS = ["cos", "sin", "rmat", "tril", "mcur", "mprev", "ones", "twos",
              "onesf", "id128", "b0", "i4rep"]
    for nm in CONSTS:
        dma("SY", f"c_{nm}", lambda e, nm=nm: e.dma_start(out=t[nm + "_sb"][:], in_=t[nm + "_h"][:]))

    st1 = t["st1"]
    for i in range(NTT):
        ib = i % 2
        xb = t["xt"][ib]
        so = (i % 2) * 8
        dma("SY", f"p1x.{i}",
            lambda e, i=i, xb=xb: e.dma_start(out=xb, in_=t["x_h"][i * 128:(i + 1) * 128, :]),
            waits=[f"p1xn.{i-2}"] if i >= 2 else [])
        op("S", f"p1sq.{i}",
           lambda e, xb=xb, so=so: e.activation(t["xc"], xb, AF.Square,
                                                accum_out=st1[:, so + 1:so + 2]),
           waits=[f"p1x.{i}"] + ([f"p1r.{i-2}"] if i >= 2 else []))
        w.seq("V", f"p1nm.{i}", [
               lambda e, xb=xb, so=so: e.tensor_reduce(st1[:, so:so + 1], xb, AX.X, ALU.add),
               lambda e, so=so: e.tensor_scalar(st1[:, so:so + 1], st1[:, so:so + 1], -1.0 / D, None, ALU.mult)],
           waits=[f"p1x.{i}"] + ([f"p1xn.{i-2}"] if i >= 2 else []))
        w.seq("V", f"p1r.{i}", [
               lambda e, so=so: e.tensor_scalar(st1[:, so + 2:so + 3], st1[:, so + 1:so + 2], 1.0 / D, None, ALU.mult),
               lambda e, so=so: e.tensor_tensor(st1[:, so + 5:so + 6], st1[:, so:so + 1], st1[:, so:so + 1], ALU.mult),
               lambda e, so=so: e.tensor_tensor(st1[:, so + 2:so + 3], st1[:, so + 2:so + 3], st1[:, so + 5:so + 6], ALU.subtract),
               lambda e, so=so: e.tensor_scalar(st1[:, so + 2:so + 3], st1[:, so + 2:so + 3], 1e-5, None, ALU.add),
               lambda e, so=so: e.reciprocal(st1[:, so + 3:so + 4], st1[:, so + 2:so + 3])],
           waits=[f"p1sq.{i}"])
        w.seq("S", f"p1xn.{i}", [
               lambda e, so=so: e.activation(st1[:, so + 4:so + 5], st1[:, so + 3:so + 4], AF.Sqrt),
               lambda e, xb=xb, so=so: e.activation(t["xc"], xb, AF.Identity, bias=st1[:, so:so + 1]),
               lambda e, so=so, ib=ib: e.activation(t["xn_tm"][ib], t["xc"], AF.Identity, scale=st1[:, so + 4:so + 5])],
           waits=[f"p1r.{i}", f"p1nm.{i}"] + ([f"p1tr.{i-2}"] if i >= 2 else []))
        dma("S", f"p1tr.{i}",
            lambda e, i=i, ib=ib: e.dma_start_transpose(
                t["bigA"][:].rearrange("p (c tt) -> p c tt", c=32)[:, :, i * 128:(i + 1) * 128],
                t["xn_tm"][ib]),
            waits=[f"p1xn.{i}"])
    P1_END = [f"p1tr.{NTT-2}", f"p1tr.{NTT-1}", f"p1xn.{NTT-1}"]

    dma("G", "w_acat",
        lambda e: e.dma_start(out=t["acat"],
                              in_=t["acat_h"][:].rearrange("(k p) m -> p k m", p=128)),
        waits=P1_END)
    dma("G", "w_bt", lambda e: e.dma_start(out=t["bt"], in_=t["bt_h"][:]), waits=P1_END)
    xn = t["bigA"][:].rearrange("p (k tt) -> p k tt", k=32)
    for ts in range(4):
        def mm_h(e, ts=ts):
            ac = t["acat"].rearrange("p (k m) -> p k m", k=32)
            for k in range(32):
                ii = e.matmul(pb[0][:], ac[:, k, :], xn[:, k, ts * 512:(ts + 1) * 512],
                              start=(k == 0), stop=(k == 31))
            return ii
        op("T", f"p2h.{ts}", mm_h,
           waits=["w_acat", f"p1tr.{NTT-1}"] + ([f"p2hev.{ts-1}"] if ts else []))
        op("S", f"p2hev.{ts}",
           lambda e, ts=ts: e.copy(t["h"][:, ts * 512:(ts + 1) * 512], pb[0][:]),
           waits=[f"p2h.{ts}"])
    H_READY = [f"p2hev.{3}"]

    def _w0_load(b):
        src = bass.AP(t["w0t_h"].tensor, b * 256, [[D, 128], [128 * D, 32], [1, 256]])
        dma("G", f"w0.{b}",
            lambda e, src=src, b=b: e.dma_start(out=t["wbuf"][b % 2], in_=src),
            waits=P1_END if b < 2 else [f"p2mm.{(b-2)*8 + 7}"])
    _w0_load(0)
    _w0_load(1)
    for lt in range(128):
        j, ts = lt // 4, lt % 4
        src = bass.AP(t["maskt_h"].tensor, j * 128 * NT + ts * 512,
                      [[NT, 128], [D * NT, 4], [1, 512]])
        dma("G", f"msk.{lt}",
            lambda e, src=src, lt=lt: e.dma_start(out=t["mbig"][lt % 2], in_=src),
            waits=P1_END if lt < 2 else [f"p2tl3.{lt - 2}"])
        if lt % 8 == 7 and lt // 8 + 2 < 16:
            _w0_load(lt // 8 + 2)
    def p2_emit_base(it):
        j, ts = it // 4, it % 4
        b, j2 = j // 2, j % 2
        bk = 4 + it % 2

        def mm_base(e, b=b, j2=j2, ts=ts, bk=bk):
            wb = t["wbuf"][b % 2].rearrange("p (k m) -> p k m", k=32)
            for k in range(32):
                ii = e.matmul(pb[bk][:], wb[:, k, j2 * 128:j2 * 128 + 128],
                              xn[:, k, ts * 512:(ts + 1) * 512],
                              start=(k == 0), stop=False)
            return ii
        op("T", f"p2b.{it}", mm_base,
           waits=[f"w0.{b}"] + ([f"p2sil.{it-2}"] if it >= 2 else H_READY))

        def mm_y(e, j=j, ts=ts):
            for l in range(4):
                ii = e.matmul(pb[l][:], t["bt"][32 * l:32 * l + 16, j * 128:(j + 1) * 128],
                              t["h"][32 * l:32 * l + 16, ts * 512:(ts + 1) * 512],
                              start=True, stop=True, tile_position=(32 * l, 0))
            return ii
        op("T", f"p2y.{it}", mm_y,
           waits=["w_bt"] + ([f"p2tl3.{it-1}"] if it else []))
        for l in range(4):
            op("V", f"p2tl{l}.{it}",
               lambda e, l=l, ts=ts, it=it: e.tensor_tensor(
                   t["tl"][l], pb[l][:],
                   t["mbig"][it % 2].rearrange("p (l4 tt) -> p l4 tt", l4=4)[:, l, :],
                   ALU.mult),
               waits=[f"p2y.{it}", f"msk.{it}"] + ([f"p2mm.{it-1}"] if it else []))

    def p2_emit_fin(it):
        j = it // 4
        bk = 4 + it % 2

        def mm_id(e, bk=bk, it=it):
            for l in range(4):
                ii = e.matmul(pb[bk][:], t["id128_sb"][:], t["tl"][l],
                              start=False, stop=(l == 3))
            return ii
        op("T", f"p2mm.{it}", mm_id,
           waits=["c_id128", f"p2tl3.{it}"])
        op("S", f"p2sil.{it}",
           lambda e, it=it, j=j, bk=bk: e.activation(t["gv"][it % 2], pb[bk][:], AF.Silu,
                                                     bias=t["b0_sb"][:, j:j + 1]),
           waits=[f"p2mm.{it}", "c_b0"] + ([f"p2gs.{it-2}"] if it >= 2 else []))
        dma("SY", f"p2gs.{it}",
            lambda e, it=it, j=j, ts=it % 4: e.dma_start(
                out=t["g_sp"][j, :, ts * 512:(ts + 1) * 512], in_=t["gv"][it % 2]),
            waits=[f"p2sil.{it}"])

    p2_emit_base(0)
    for it in range(1, 128):
        p2_emit_base(it)
        p2_emit_fin(it - 1)
    p2_emit_fin(127)
    P2_POOL_FREE = [f"p2tl3.{127}", f"p2sil.{127}"]

    for vq in range(4):
        src = bass.AP(t["wv_h"].tensor, vq * 256,
                      [[KVH * HD, 128], [128 * KVH * HD, 32], [1, 256]])
        dma("G", f"wv.{vq}",
            lambda e, src=src, vq=vq: e.dma_start(out=t["wbuf"][vq % 2], in_=src),
            waits=[f"p2mm.{119 + 8 * vq}"] if vq < 2 else [f"p3vmm.{(vq-1)*16 - 1}"])
        for tt in range(16):
            it = vq * 16 + tt

            def mm_v(e, vq=vq, tt=tt, it=it):
                wb = t["wbuf"][vq % 2].rearrange("p (k m) -> p k m", k=32)
                for k in range(32):
                    ii = e.matmul(pb[4 + it % 2][:, 0:256], xn[:, k, tt * 128:(tt + 1) * 128],
                                  wb[:, k, :], start=(k == 0), stop=(k == 31))
                return ii
            op("T", f"p3vmm.{it}", mm_v,
               waits=[f"wv.{vq}"] + ([f"p3vev.{it-2}"] if it >= 2 else [f"p2sil.{126}", f"p2sil.{127}"]))
            op("S", f"p3vev.{it}",
               lambda e, it=it: e.copy(t["ev"][it % 4][:, 0:256], pb[4 + it % 2][:, 0:256]),
               waits=[f"p3vmm.{it}"] + ([f"p3vsp.{it-4}"] if it >= 4 else P2_POOL_FREE),)
            dst = bass.AP(t["v_sp"].tensor, tt * 128 * KVH * 128 + vq * 256,
                          [[KVH * 128, 128], [1, 256]])
            dma("SY", f"p3vsp.{it}",
                lambda e, dst=dst, it=it: e.dma_start(out=dst, in_=t["ev"][it % 4][:, 0:256]),
                waits=[f"p3vev.{it}"])
    for b in range(4):
        src = bass.AP(t["wk_h"].tensor, b * 256,
                      [[KVH * HD, 128], [128 * KVH * HD, 32], [1, 256]])
        dma("G", f"wk.{b}",
            lambda e, src=src, b=b: e.dma_start(out=t["wbuf"][b % 2], in_=src),
            waits=[f"p3vmm.{47 + 16 * b}"] if b < 2 else [f"p3kmm.{(b-1)*8 - 1}"])
        for sub in range(8):
            j2, ts = sub // 4, sub % 4
            j = b * 2 + j2
            it = b * 8 + sub

            def mm_k(e, b=b, j2=j2, ts=ts, it=it):
                wb = t["wbuf"][b % 2].rearrange("p (k m) -> p k m", k=32)
                for k in range(32):
                    ii = e.matmul(pb[4 + it % 2][:], wb[:, k, j2 * 128:j2 * 128 + 128],
                                  xn[:, k, ts * 512:(ts + 1) * 512],
                                  start=(k == 0), stop=(k == 31))
                return ii
            op("T", f"p3kmm.{it}", mm_k,
               waits=[f"wk.{b}"] + ([f"p3kev.{it-2}"] if it >= 2 else [f"p3vev.{62 + it}"]))
            op("S", f"p3kev.{it}",
               lambda e, it=it: e.copy(t["ev"][it % 4], pb[4 + it % 2][:]),
               waits=[f"p3kmm.{it}"] + ([f"p3ksp.{it-4}"] if it >= 4 else [f"p3vsp.{60 + it}"]))
            dst = bass.AP(t["k_sp"].tensor, ts * 4 * 128 * KVH * 128 + j * 128,
                          [[KVH * 128, 128], [128 * KVH * 128, 4], [1, 128]])
            dma("SY", f"p3ksp.{it}",
                lambda e, dst=dst, it=it: e.dma_start(out=dst, in_=t["ev"][it % 4]),
                waits=[f"p3kev.{it}"])
    for b in range(16):
        src = bass.AP(t["wq_h"].tensor, b * 256, [[D, 128], [128 * D, 32], [1, 256]])
        dma("G", f"wqb.{b}",
            lambda e, src=src, b=b: e.dma_start(out=t["wbuf"][b % 2], in_=src),
            waits=[f"p3kmm.{23 + 8 * b}"] if b < 2 else [f"p3qmm.{(b-1)*8 - 1}"])
        for sub in range(8):
            j2, ts = sub // 4, sub % 4
            j = b * 2 + j2
            it = b * 8 + sub

            def mm_q(e, b=b, j2=j2, ts=ts, it=it):
                wb = t["wbuf"][b % 2].rearrange("p (k m) -> p k m", k=32)
                for k in range(32):
                    ii = e.matmul(pb[4 + it % 2][:], wb[:, k, j2 * 128:j2 * 128 + 128],
                                  xn[:, k, ts * 512:(ts + 1) * 512],
                                  start=(k == 0), stop=(k == 31))
                return ii
            op("T", f"p3qmm.{it}", mm_q,
               waits=[f"wqb.{b}"] + ([f"p3qev.{it-2}"] if it >= 2 else [f"p3kev.{30 + it}"]))
            op("S", f"p3qev.{it}",
               lambda e, it=it: e.copy(t["ev"][it % 4], pb[4 + it % 2][:]),
               waits=[f"p3qmm.{it}"] + ([f"p3qsp.{it-4}"] if it >= 4 else [f"p3ksp.{28 + it}"]))
            dst = bass.AP(t["q_sp"].tensor, ts * 4 * 128 * NH * 128 + j * 128,
                          [[NH * 128, 128], [128 * NH * 128, 4], [1, 128]])
            dma("SY", f"p3qsp.{it}",
                lambda e, dst=dst, it=it: e.dma_start(out=dst, in_=t["ev"][it % 4]),
                waits=[f"p3qev.{it}"])
    P3_SPILLED = [f"p3qsp.{i}" for i in range(121, 128)]

    w.seq("G", "s_zero", [
           lambda e: e.memset(t["S"], 0.0),
           lambda e: e.memset(t["Sbf"], 0.0)],
       waits=P3_SPILLED + [f"p3qev.{127}"])
    KW = KVH * 128

    def emit_p5_block(ts, b):
        src = bass.AP(t["wo_h"].tensor, b * 256, [[D, 128], [128 * D, 32], [1, 256]])
        wdep = list(P3_SPILLED) if (ts == 0 and b < 2) else [f"p5mm.{(2*b-3)*4+ts}"] if b >= 2 else [f"p5mm.{(2*(b+14)+1)*4+ts-1}"]
        dma("G", f"wo.{ts}.{b}",
            lambda e, src=src, b=b: e.dma_start(out=t["wobuf"][b % 2], in_=src),
            waits=wdep)

    p5_seq = []

    def emit_p5_jtile(ts, j):
        b, j2 = j // 2, j % 2
        it = j * 4 + ts
        if j2 == 0:
            emit_p5_block(ts, b)
        ei = len(p5_seq)
        prev = p5_seq[-1] if p5_seq else None
        pprev = p5_seq[-2] if len(p5_seq) >= 2 else None
        p5_seq.append(it)
        dma("SY", f"gld.{it}",
            lambda e, j=j, ts=ts, ei=ei: e.dma_start(
                out=t["gld5"][ei % 2], in_=t["g_sp"][j, :, ts * 512:(ts + 1) * 512]),
            waits=[f"p2gs.{it}"] + ([f"sg.{pprev}"] if pprev is not None else []))

        def mm_s(e, b=b, j2=j2, ts=ts):
            wb = t["wobuf"][b % 2].rearrange("p (k m) -> p k m", k=32)
            off = (4 * ts % 8) * 128
            for k in range(32):
                ii = e.matmul(t["pWo"][:], wb[:, k, j2 * 128:j2 * 128 + 128],
                              bass.AP(t["bigA"][:].tensor, k * 1024 + off,
                                      [[32 * NT, 128], [1, 512]]),
                              start=(k == 0), stop=(k == 31))
            return ii
        op("T", f"p5mm.{it}", mm_s,
           waits=[f"wo.{ts}.{b}", f"oev.{4*ts+3}.{KVH-1}"] +
                 ([f"sg.{prev}"] if prev is not None else []))
        op("V", f"sg.{it}",
           lambda e, it=it, ei=ei: e.tensor_tensor(t["sgv5"][ei % 2], t["pWo"][:],
                                                   t["gld5"][ei % 2], ALU.mult),
           waits=[f"p5mm.{it}", f"gld.{it}"] + ([f"sgt.{pprev}"] if pprev is not None else []))
        dma("S", f"sgt.{it}",
            lambda e, it=it, ei=ei: e.dma_start_transpose(
                t["sgt5"][ei % 2].rearrange("p (r q) -> p r q", r=4),
                t["sgv5"][ei % 2]),
            waits=[f"sg.{it}"] + ([f"sgsp.{pprev}"] if pprev is not None else []))
        dst = bass.AP(t["sg_sp"].tensor, ts * 512 * D + j * 128,
                      [[D, 128], [128 * D, 4], [1, 128]])
        dma("SY", f"sgsp.{it}",
            lambda e, dst=dst, it=it, ei=ei: e.dma_start(
                out=dst, in_=t["sgt5"][ei % 2].rearrange("p (r q) -> p r q", r=4)),
            waits=[f"sgt.{it}"])

    def emit_pass2(c, gg, cr):
        hs = slice(gg * 128, (gg + 1) * 128)
        gr = gg % 2

        def mm_o(e, c=c, g=gg, gr=gr, hs=hs):
            ii = e.matmul(pb[2 + gr][:], t["Sbf"][:, hs], t["qtil3"][g % 3], start=True, stop=False)
            ii = e.matmul(pb[2 + gr][:], t["vring"][c % 2][:, hs], t["am3"][g % 3], start=False, stop=False)
            ii = e.matmul(pb[2 + gr][:], t["vring"][c % 2][:, hs], t["pcm3"][g % 3], start=False,
                          stop=(c == 0))
            if c > 0:
                ii = e.matmul(pb[2 + gr][:], t["vring"][(c - 1) % 2][64:128, hs],
                              t["ppm3"][g % 3][64:128, :], start=False, stop=True)
            return ii
        ow = [f"qt.{c}.{gg}", f"am.{c}.{gg}", f"pcn.{c}.{gg}", f"vin.{c}"]
        if c > 0:
            ow += [f"ppn.{c}.{gg}"]
        else:
            ow += ["s_zero"]
        ow += [f"oev.{c}.{gg-2}"] if gg >= 2 else ([f"oev.{c-1}.{KVH-2+gg}"] if c else [])
        if gg == 0 and c >= 8:
            ow += [f"p5mm.{124 + (c - 8) // 4}"]
        op("T", f"o.{c}.{gg}", mm_o, waits=ow)
        op("V", f"oev.{c}.{gg}",
           lambda e, c=c, g=gg, gr=gr: e.tensor_copy(
               bass.AP(t["bigA"][:].tensor, 4 * g * 1024 + (c % 8) * 128,
                       [list(t["bigA"][:].ap[0]), [1024, 4], [1, 128]]),
               pb[2 + gr][:].rearrange("p (r tt) -> p r tt", r=4)),
           waits=[f"o.{c}.{gg}"])

        def mm_su(e, c=c, g=gg, hs=hs):
            return e.matmul(pS[:, (g % 4) * 128:(g % 4) * 128 + 128],
                            t["khT"][:, hs], t["vring"][c % 2][:, hs],
                            start=(g % 4 == 0), stop=(g % 4 == 3))
        op("T", f"su.{c}.{gg}", mm_su,
           waits=[f"khTev.{c}", f"vin.{c}", f"o.{c}.{gg}"] +
                 ([f"sadd.{c}.{gg-1}"] if (gg % 4 == 0 and gg >= 4) else []) +
                 ([f"sadd.{c-1}.{KVH-1}"] if (gg == 0 and c > 0) else []) +
                 (["s_zero"] if (gg == 0 and c == 0) else []))
        if gg % 4 == 3:
            w.seq("S", f"sdec.{c}.{gg}", [
                   (lambda e, H=H: e.activation(t["S"][:, H * 128:(H + 1) * 128],
                                                t["S"][:, H * 128:(H + 1) * 128],
                                                AF.Identity, scale=t["explend"][:, H:H + 1]))
                   for H in range(gg - 3, gg + 1)],
               waits=[f"explend.{c}", f"su.{c}.{gg}"] +
                     ([f"sbf.{c-1}"] if c else ["s_zero"]))
            op("V", f"sadd.{c}.{gg}",
               lambda e, c=c, g=gg: e.tensor_tensor(
                   t["S"][:, (g - 3) * 128:(g + 1) * 128],
                   t["S"][:, (g - 3) * 128:(g + 1) * 128], pS[:], ALU.add),
               waits=[f"sdec.{c}.{gg}", f"su.{c}.{gg}"])

    for c in range(NCH):
        cr = c % 2
        dma("SY", f"kin.{c}",
            lambda e, c=c: e.dma_start(out=t["kin"], in_=t["k_sp"][c, :, :]),
            waits=P3_SPILLED + ([f"p4kdone.{c-1}"] if c else []))
        dma("SY", f"vin.{c}",
            lambda e, c=c, cr=cr: e.dma_start(out=t["vring"][cr], in_=t["v_sp"][c, :, :]),
            waits=(P3_SPILLED + ["s_zero"] if c < 2 else []) +
                  ([f"p4odone.{c-1}"] if c >= 1 else []) +
                  ([f"sadd.{c-2}.{KVH-1}"] if c >= 2 else []))
        op("S", f"sig.{c}", lambda e: e.activation(t["ksig"], t["kin"], AF.Sigmoid),
           waits=[f"kin.{c}"])
        op("S", f"lsig.{c}", lambda e: e.activation(t["ksig"], t["ksig"], AF.Ln), waits=[])
        w.seq("V", f"lam.{c}", [
               (lambda e, H=H: e.tensor_tensor_scan(t["lam"][:, H * 128:(H + 1) * 128],
                                                    t["onesf_sb"][:], t["ksig"][:, H * 128:(H + 1) * 128],
                                                    0.0, ALU.mult, ALU.add)) for H in range(KVH)],
           waits=[f"lsig.{c}", "c_onesf"])
        op("S", f"expl.{c}", lambda e: e.activation(t["expl"], t["lam"], AF.Exp, scale=INV16),
           waits=[f"lam.{c}"])
        op("S", f"explend.{c}",
           lambda e: e.activation(t["explend"],
                                  bass.AP(t["lam"].tensor, t["lam"].offset + 127,
                                          [list(t["lam"].ap[0]), [128, 8]]),
                                  AF.Exp, scale=INV16),
           waits=[f"lam.{c}"] + ([f"sdec.{c-1}.{KVH-1}"] if c else []))
        op("S", f"expli.{c}", lambda e: e.activation(t["expli"], t["lam"], AF.Exp, scale=-INV16),
           waits=[f"lam.{c}"])
        op("S", f"ek.{c}", lambda e: e.activation(t["ekt"], t["kin"], AF.Exp),
           waits=[f"kin.{c}"])
        op("T", f"denk.{c}",
           lambda e: (e.matmul(pb[4][:], t["ones_sb"][:], t["ekt"][:, 0:512], start=True, stop=True),
                      e.matmul(pb[5][:], t["ones_sb"][:], t["ekt"][:, 512:1024], start=True, stop=True))[-1],
           waits=[f"ek.{c}", "c_ones"] + ([f"rs.{c-1}.{KVH-2}", f"rs.{c-1}.{KVH-1}"] if c else []))
        w.seq("S", f"rk.{c}", [
               lambda e: e.activation(t["lnscr"], pb[4][:], AF.Ln),
               lambda e: e.activation(t["rkt2"][:, 0:512], t["lnscr"], AF.Exp, scale=-1.0),
               lambda e: e.activation(t["lnscr"], pb[5][:], AF.Ln),
               lambda e: e.activation(t["rkt2"][:, 512:1024], t["lnscr"], AF.Exp, scale=-1.0)],
           waits=[f"denk.{c}"])
        w.seq("V", f"kt.{c}", [
               lambda e: e.tensor_tensor(t["ktmp"].bitcast(BF16)[:, 0:1024], t["ekt"], t["expli"], ALU.mult),
               lambda e: e.tensor_tensor(t["ktil"], t["ktmp"].bitcast(BF16)[:, 0:1024], t["rkt2"], ALU.mult)],
           waits=[f"rk.{c}", f"expli.{c}"])
        op("V", f"kh.{c}",
           lambda e: e.tensor_tensor(
               t["khat"].rearrange("p (H tt) -> p H tt", H=KVH),
               t["ktil"].rearrange("p (H tt) -> p H tt", H=KVH),
               _ins(t["explend"], 1, 0, 128), ALU.mult),
           waits=[f"kt.{c}", f"explend.{c}"])
        op("T", f"rotk.{c}",
           lambda e: (e.matmul(pb[4][:], t["rmat_sb"][:], t["kin"][:, 0:512], start=True, stop=True),
                      e.matmul(pb[5][:], t["rmat_sb"][:], t["kin"][:, 512:1024], start=True, stop=True))[-1],
           waits=[f"kin.{c}", "c_rmat", f"rk.{c}"])
        cosc = _ins(t["cos_sb"][:, c * 128:(c + 1) * 128], 0, 0, KVH)
        sinc = _ins(t["sin_sb"][:, c * 128:(c + 1) * 128], 0, 0, KVH)
        ktc = t["ktmp"].bitcast(BF16)[:, 0:1024]
        w.seq("V", f"sk.{c}", [
               lambda e, cosc=cosc, ktc=ktc: e.tensor_tensor(
                   ktc.rearrange("p (H tt) -> p H tt", H=KVH),
                   t["kin"].rearrange("p (H tt) -> p H tt", H=KVH), cosc, ALU.mult),
               lambda e, sinc=sinc: e.tensor_tensor(
                   t["krot"][:, 0:512].rearrange("p (r tt) -> p r tt", r=4),
                   pb[4][:].rearrange("p (r tt) -> p r tt", r=4), sinc[:, 0:4, :], ALU.mult),
               lambda e, sinc=sinc: e.tensor_tensor(
                   t["krot"][:, 512:1024].rearrange("p (r tt) -> p r tt", r=4),
                   pb[5][:].rearrange("p (r tt) -> p r tt", r=4), sinc[:, 4:8, :], ALU.mult),
               lambda e, cr=cr, ktc=ktc: e.tensor_tensor(t["skring"][cr], ktc, t["krot"], ALU.add)],
           waits=[f"rotk.{c}", "c_cos", "c_sin", f"kh.{c}"] +
                 ([f"scp.{c-1}.{KVH-1}"] if c >= 2 else []))
        op("T", f"khT.{c}",
           lambda e: [e.transpose(pb[0][:].bitcast(BF16)[:, H * 128:(H + 1) * 128],
                                  t["khat"][:, H * 128:(H + 1) * 128], t["id128_sb"][:])
                      for H in range(KVH)][-1],
           waits=[f"kh.{c}", "c_id128", f"sk.{c}"] + ([f"khTev.{c-1}"] if c else []) +
                 ([f"am.{c-1}.{KVH-2}"] if c else []))
        op("S", f"khTev.{c}", lambda e: e.copy(t["khT"], pb[0][:].bitcast(BF16)),
           waits=[f"khT.{c}"] + ([f"sadd.{c-1}.{KVH-1}"] if c else []))
        op("V", f"p4kdone.{c}", lambda e: e.tensor_copy(t["ktmp"].bitcast(BF16)[:, 0:1], t["ktil"][:, 0:1]),
           waits=[f"sk.{c}", f"kh.{c}", f"ek.{c}"])

        for g in range(KVH):
            gr = g % 2
            hs = slice(g * 128, (g + 1) * 128)
            dma("SY", f"qin.{c}.{g}",
                lambda e, c=c, g=g, gr=gr: e.dma_start(
                    out=t["qg"][gr],
                    in_=bass.AP(t["q_sp"].tensor, c * 128 * NH * 128 + g * 512,
                                [[NH * 128, 128], [1, 512]])),
                waits=P3_SPILLED + ([f"p4qfree.{c}.{g-2}"] if g >= 2 else
                                    ([f"p4qfree.{c-1}.{KVH-2+g}"] if c else [])))
            op("S", f"eq.{c}.{g}", lambda e, gr=gr, g=g: e.activation(t["eq2"][g % 2], t["qg"][gr], AF.Exp),
               waits=[f"qin.{c}.{g}"])
            op("T", f"denq.{c}.{g}",
               lambda e, gr=gr, g=g: e.matmul(pb[4 + gr][:], t["twos_sb"][:], t["eq2"][g % 2], start=True, stop=True),
               waits=[f"eq.{c}.{g}", "c_twos"] +
                     ([f"rs.{c}.{g-2}"] if g >= 2 else [f"sk.{c}"]))
            w.seq("S", f"p4rq.{c}.{g}", [
                   lambda e, gr=gr: e.activation(t["lnscr"], pb[4 + gr][:], AF.Ln),
                   lambda e, g=g: e.activation(t["rq2"][g % 2], t["lnscr"], AF.Exp, scale=-1.0)],
               waits=[f"denq.{c}.{g}"])
            expl_g = _ins(t["expl"][:, g * 128:(g + 1) * 128], 0, 0, 4)
            w.seq("V", f"qt.{c}.{g}", [
                   lambda e, expl_g=expl_g, g=g: e.tensor_tensor(
                       t["qt2"].rearrange("p (r tt) -> p r tt", r=4),
                       t["eq2"][g % 2].rearrange("p (r tt) -> p r tt", r=4), expl_g, ALU.mult),
                   lambda e, g=g: e.tensor_tensor(t["qtil3"][g % 3], t["qt2"], t["rq2"][g % 2], ALU.mult)],
               waits=[f"p4rq.{c}.{g}", f"expl.{c}"])
            op("T", f"rotq.{c}.{g}",
               lambda e, gr=gr, g=g: e.matmul(pb[g % 2][:], t["rmat_sb"][:], t["qg"][gr], start=True, stop=True),
               waits=[f"qin.{c}.{g}", "c_rmat"] +
                     ([f"am.{c}.{g-2}"] if g >= 2 else
                      ([f"am.{c-1}.{6+g}"] if c else [])) +
                     ([f"khTev.{c}"] if g == 0 else []))
            cos4 = _ins(t["cos_sb"][:, c * 128:(c + 1) * 128], 0, 0, 4)
            sin4 = _ins(t["sin_sb"][:, c * 128:(c + 1) * 128], 0, 0, 4)
            w.seq("V", f"sq.{c}.{g}", [
                   lambda e, gr=gr, cos4=cos4: e.tensor_tensor(
                       t["sq2"].rearrange("p (r tt) -> p r tt", r=4),
                       t["qg"][gr].rearrange("p (r tt) -> p r tt", r=4), cos4, ALU.mult),
                   lambda e, g=g, sin4=sin4: e.tensor_tensor(
                       t["sq3"].rearrange("p (r tt) -> p r tt", r=4),
                       pb[g % 2][:].rearrange("p (r tt) -> p r tt", r=4), sin4, ALU.mult),
                   lambda e, g=g: e.tensor_tensor(t["sq1"][g % 2], t["sq2"], t["sq3"], ALU.add)],
               waits=[f"rotq.{c}.{g}", f"qt.{c}.{g}"])
            op("V", f"p4qfree.{c}.{g}", lambda e, g=g: e.tensor_copy(t["qt2"][:, 0:1], t["sq1"][g % 2][:, 0:1]),
               waits=[f"sq.{c}.{g}"])

            def mm_sc(e, cr=cr, hs=hs, g=g):
                e.matmul(pb[g % 2][:], t["skring"][cr][:, hs], t["sq1"][g % 2], start=True, stop=False)
                return e.matmul(pb[g % 2][:], t["mcur_sb"][:], t["i4rep_sb"][:],
                                start=False, stop=True)
            op("T", f"sc.{c}.{g}", mm_sc,
               waits=[f"sq.{c}.{g}", f"sk.{c}", "c_mcur", "c_i4rep"])
            if c > 0:
                def mm_scp(e, c=c, g=g, gr=gr):
                    e.matmul(pb[4 + gr][64:128, :],
                             t["skring"][(c - 1) % 2][:, g * 128 + 64:g * 128 + 128],
                             t["sq1"][g % 2], start=True, stop=False, tile_position=(0, 64))
                    return e.matmul(pb[4 + gr][64:128, :], t["mprev_sb"][:, 0:64],
                                    t["i4rep_sb"][:], start=False, stop=True,
                                    tile_position=(0, 64))
                op("T", f"scp.{c}.{g}", mm_scp,
                   waits=[f"sq.{c}.{g}", f"p4rq.{c}.{g}", "c_mprev", "c_i4rep"])
            op("S", f"esc.{c}.{g}",
               lambda e, g=g: e.activation(t["esc2"][g % 2], pb[g % 2][:], AF.Exp, scale=SSCALE),
               waits=[f"sc.{c}.{g}"] + ([f"pcn.{c}.{g-2}"] if g >= 2 else []))
            if c > 0:
                op("S", f"esp.{c}.{g}",
                   lambda e, gr=gr, g=g: e.activation(t["esp2"][g % 2][64:128, :], pb[4 + gr][64:128, :], AF.Exp, scale=SSCALE),
                   waits=[f"scp.{c}.{g}"] + ([f"ppn.{c}.{g-2}"] if g >= 2 else []))

            def mm_dsw(e, c=c, gr=gr, g=g):
                ii = e.matmul(pb[4 + gr][:], t["twos_sb"][:], t["esc2"][g % 2], start=True,
                              stop=(c == 0))
                if c > 0:
                    ii = e.matmul(pb[4 + gr][:], t["twos_sb"][64:128, :], t["esp2"][g % 2][64:128, :],
                                  start=False, stop=True)
                return ii
            op("T", f"dsw.{c}.{g}", mm_dsw,
               waits=[f"esc.{c}.{g}", f"p4rq.{c}.{g}"] + ([f"esp.{c}.{g}"] if c else []))
            w.seq("S", f"rs.{c}.{g}", [
                   lambda e, gr=gr: e.activation(t["lnscr"], pb[4 + gr][:], AF.Ln),
                   lambda e, g=g: e.activation(t["rs2"][g % 2], t["lnscr"], AF.Exp, scale=-1.0)],
               waits=[f"dsw.{c}.{g}"])
            op("V", f"pcn.{c}.{g}",
               lambda e, g=g: e.tensor_tensor(t["pcm3"][g % 3], t["esc2"][g % 2], t["rs2"][g % 2], ALU.mult),
               waits=[f"rs.{c}.{g}"])
            if c > 0:
                op("V", f"ppn.{c}.{g}",
                   lambda e, g=g: e.tensor_tensor(t["ppm3"][g % 3][64:128, :], t["esp2"][g % 2][64:128, :],
                                                  t["rs2"][g % 2][64:128, :], ALU.mult),
                   waits=[f"rs.{c}.{g}"])
            op("T", f"at.{c}.{g}",
               lambda e, hs=hs, g=g: e.matmul(pb[g % 2][:], t["ktil"][:, hs], t["qtil3"][g % 3],
                                              start=True, stop=True),
               waits=[f"qt.{c}.{g}", f"kt.{c}", f"esc.{c}.{g}"])
            tril4 = _ins(t["tril_sb"][:], 0, 0, 4)
            op("V", f"am.{c}.{g}",
               lambda e, g=g, tril4=tril4: e.tensor_tensor(
                   t["am3"][g % 3].rearrange("p (r tt) -> p r tt", r=4),
                   pb[g % 2][:].rearrange("p (r tt) -> p r tt", r=4), tril4, ALU.mult),
               waits=[f"at.{c}.{g}", "c_tril"])
            if g >= 2:
                emit_pass2(c, g - 2, cr)
        emit_pass2(c, KVH - 2, cr)
        emit_pass2(c, KVH - 1, cr)
        op("V", f"sbf.{c}", lambda e: e.tensor_copy(t["Sbf"], t["S"]),
           waits=[f"sadd.{c}.{KVH-1}"])
        op("V", f"p4odone.{c}", lambda e: e.tensor_copy(t["qt2"][:, 0:1], t["qtil3"][0][:, 0:1]),
           waits=[f"oev.{c}.{KVH-1}", f"oev.{c}.{KVH-2}"])
        if c >= 4:
            ts = (c - 4) // 4
            j0 = 8 * ((c - 4) % 4)
            for j in range(j0, j0 + 8):
                emit_p5_jtile(ts, j)
    for j in range(32):
        emit_p5_jtile(3, j)
    P4_END = [f"p4odone.{NCH-1}", f"sadd.{NCH-1}.{KVH-1}", f"sbf.{NCH-1}",
              f"p4kdone.{NCH-1}", f"khTev.{NCH-1}", f"p4qfree.{NCH-1}.{KVH-1}"]

    for i2 in range(2 * NTT):
        i, hf = i2 // 2, i2 % 2
        dma("SY", f"p6sg.{i2}",
            lambda e, i=i, hf=hf, i2=i2: e.dma_start(
                out=t["sgh"][i2 % 2],
                in_=t["sg_sp"][i * 128:(i + 1) * 128, hf * 2048:(hf + 1) * 2048]),
            waits=[f"sgsp.{j*4+3}" for j in range(26, 32)] + [f"gld.{j*4+3}" for j in range(29, 32)]
                  + ([f"p6add.{i2-2}"] if i2 >= 2 else []))
        dma("G", f"p6x.{i2}",
            lambda e, i=i, hf=hf, i2=i2: e.dma_start(
                out=t["xf"][i2 % 2],
                in_=t["x_h"][i * 128:(i + 1) * 128, hf * 2048:(hf + 1) * 2048]),
            waits=([f"p6add.{i2-2}"] if i2 >= 2 else [f"sg.{127}"]))
        op("V", f"p6add.{i2}",
           lambda e, i2=i2: e.tensor_tensor(t["of"][i2 % 2], t["xf"][i2 % 2],
                                            t["sgh"][i2 % 2], ALU.add),
           waits=[f"p6sg.{i2}", f"p6x.{i2}"] + ([f"p6st.{i2-2}"] if i2 >= 2 else [f"sgsp.{127}"]))
        dma("SY", f"p6st.{i2}",
            lambda e, i=i, hf=hf, i2=i2: e.dma_start(
                out=t["out_h"][i * 128:(i + 1) * 128, hf * 2048:(hf + 1) * 2048],
                in_=t["of"][i2 % 2]),
            waits=[f"p6add.{i2}"])
    op("SY", "fin", lambda e: e.nop(), waits=[f"p6st.{2*NTT-1}"])


def build_nc():
    nc = bass.Bass("TRN2", target_bir_lowering=False)
    t = {}
    t["x_h"] = nc.dram_tensor("x", [NT, D], F32, kind="ExternalInput").ap()
    t["wq_h"] = nc.dram_tensor("wq", [D, D], F32, kind="ExternalInput").ap()
    t["wk_h"] = nc.dram_tensor("wk", [D, KVH * HD], F32, kind="ExternalInput").ap()
    t["wv_h"] = nc.dram_tensor("wv", [D, KVH * HD], F32, kind="ExternalInput").ap()
    t["wo_h"] = nc.dram_tensor("wo", [D, D], F32, kind="ExternalInput").ap()
    t["w0t_h"] = nc.dram_tensor("w0t", [D, D], F32, kind="ExternalInput").ap()
    t["acat_h"] = nc.dram_tensor("acat", [D, 128], F32, kind="ExternalInput").ap()
    t["bt_h"] = nc.dram_tensor("bt", [128, D], F32, kind="ExternalInput").ap()
    t["b0_h"] = nc.dram_tensor("b0fm", [128, 32], F32, kind="ExternalInput").ap()
    t["maskt_h"] = nc.dram_tensor("maskt", [4, D, NT], BF16, kind="ExternalInput").ap()
    t["cos_h"] = nc.dram_tensor("cosfm", [128, NT], BF16, kind="ExternalInput").ap()
    t["sin_h"] = nc.dram_tensor("sinfm", [128, NT], BF16, kind="ExternalInput").ap()
    t["rmat_h"] = nc.dram_tensor("rmat", [128, 128], BF16, kind="ExternalInput").ap()
    t["tril_h"] = nc.dram_tensor("trilT", [128, 128], BF16, kind="ExternalInput").ap()
    t["mcur_h"] = nc.dram_tensor("mcurT", [128, 128], BF16, kind="ExternalInput").ap()
    t["mprev_h"] = nc.dram_tensor("mprevT", [128, 128], BF16, kind="ExternalInput").ap()
    t["i4rep_h"] = nc.dram_tensor("i4rep", [128, 512], BF16, kind="ExternalInput").ap()
    t["ones_h"] = nc.dram_tensor("ones128", [128, 128], BF16, kind="ExternalInput").ap()
    t["twos_h"] = nc.dram_tensor("twos128", [128, 128], BF16, kind="ExternalInput").ap()
    t["onesf_h"] = nc.dram_tensor("onesf128", [128, 128], F32, kind="ExternalInput").ap()
    t["id128_h"] = nc.dram_tensor("id128", [128, 128], BF16, kind="ExternalInput").ap()
    t["out_h"] = nc.dram_tensor("out", [NT, D], F32, kind="ExternalOutput").ap()
    t["q_sp"] = nc.dram_tensor("q_sp", [NCH, 128, NH * 128], BF16).ap()
    t["k_sp"] = nc.dram_tensor("k_sp", [NCH, 128, KVH * 128], BF16).ap()
    t["v_sp"] = nc.dram_tensor("v_sp", [NCH, 128, KVH * 128], BF16).ap()
    t["g_sp"] = nc.dram_tensor("g_sp", [32, 128, NT], F32).ap()
    t["sg_sp"] = nc.dram_tensor("sg_sp", [NT, D], BF16).ap()

    ctx = ExitStack()
    with ctx:
        def sb(name, shape, dt=BF16):
            h = ctx.enter_context(nc.sbuf_tensor(name, shape, dt))
            t[name] = h
            return h
        sb("bigA", [128, 32 * NT])
        for nm, shape, dt in [("cos_sb", [128, NT], BF16), ("sin_sb", [128, NT], BF16),
                              ("rmat_sb", [128, 128], BF16), ("tril_sb", [128, 128], BF16),
                              ("mcur_sb", [128, 128], BF16), ("mprev_sb", [128, 128], BF16),
                              ("i4rep_sb", [128, 512], BF16),
                              ("ones_sb", [128, 128], BF16), ("twos_sb", [128, 128], BF16),
                              ("onesf_sb", [128, 128], F32), ("id128_sb", [128, 128], BF16),
                              ("b0_sb", [128, 32], F32), ("st1", [128, 16], F32)]:
            sb(nm, shape, dt)
        pool = ctx.enter_context(nc.sbuf_tensor("pool", [128, 34816], BF16))

        def V_(off, n, dt=BF16):
            ap = pool[:, off:off + n]
            if dt == F32:
                ap = ap.bitcast(F32)
            return ap
        t["xt"] = [V_(0, 8192, F32), V_(8192, 8192, F32)]
        t["xc"] = V_(16384, 8192, F32)
        t["xn_tm"] = [V_(24576, 4096), V_(28672, 4096)]
        t["wbuf"] = [V_(0, 8192), V_(8192, 8192)]
        t["mbig"] = [V_(16384, 2048), V_(18432, 2048)]
        t["acat"] = V_(20480, 4096)
        t["bt"] = V_(24576, 4096)
        t["h"] = V_(28672, 2048)
        t["tl"] = [V_(30720 + 512 * i, 512) for i in range(4)]
        t["gv"] = [V_(32768, 1024, F32), V_(33792, 1024, F32)]
        t["ev"] = [V_(16384 + 512 * i, 512) for i in range(4)]
        go = 0

        def GV(n, dt=BF16):
            nonlocal go
            v = V_(go, n, dt)
            go += n
            return v
        t["qg"] = [GV(512), GV(512)]
        t["eq2"] = [GV(512), GV(512)]
        t["rq2"] = [GV(512), GV(512)]
        t["lnscr"] = GV(1024, F32)
        t["qtil3"] = [GV(512) for _ in range(3)]
        t["qt2"] = GV(512)
        t["sq1"] = [GV(512), GV(512)]
        t["sq2"] = t["qt2"]
        t["sq3"] = GV(512)
        t["esc2"] = [GV(512), GV(512)]
        t["rs2"] = [GV(512), GV(512)]
        t["pcm3"] = [GV(512) for _ in range(3)]
        t["am3"] = [GV(512) for _ in range(3)]
        t["esp2"] = [GV(512), GV(512)]
        t["ppm3"] = [GV(512) for _ in range(3)]
        ko = go

        def KV(n, dt=BF16):
            nonlocal ko
            v = V_(ko, n, dt)
            ko += n
            return v
        t["kin"] = KV(1024)
        t["ksig"] = KV(2048, F32)
        t["lam"] = KV(2048, F32)
        t["expl"] = KV(1024)
        t["expli"] = t["lam"].bitcast(BF16)[:, 0:1024]
        t["explend"] = t["st1"][:, 8:16]
        t["ekt"] = KV(1024)
        t["rkt2"] = KV(1024)
        t["krot"] = t["ekt"]
        t["ktil"] = KV(1024)
        t["khat"] = KV(1024)
        t["khT"] = KV(1024)
        t["ktmp"] = t["ksig"]
        t["skring"] = [KV(1024), KV(1024)]
        t["vring"] = [KV(1024), KV(1024)]
        t["S"] = KV(2048, F32)
        t["Sbf"] = KV(1024)
        assert ko <= 34816, ko
        def BV(off, n, dt=BF16):
            ap = t["bigA"][:][:, off:off + n]
            if dt == F32:
                ap = ap.bitcast(F32)
            return ap
        t["wobuf"] = [BV(32768, 8192), BV(40960, 8192)]
        t["sgv5"] = [BV(49152, 512), BV(49664, 512)]
        t["gld5"] = [BV(50176, 1024, F32), BV(51200, 1024, F32)]
        t["sgt5"] = [BV(52224, 512), BV(52736, 512)]
        t["xf"] = [V_(0, 4096, F32), V_(4096, 4096, F32)]
        t["of"] = [V_(8192, 4096, F32), V_(12288, 4096, F32)]
        t["sgh"] = [V_(16384, 2048), V_(18432, 2048)]
        t["pb"] = [ctx.enter_context(nc.psum_tensor(f"pb{i}", [128, 512], F32)) for i in range(6)]
        t["pS"] = ctx.enter_context(nc.psum_tensor("pS", [128, 512], F32))
        t["pWo"] = ctx.enter_context(nc.psum_tensor("pWo", [128, 512], F32))

        sems = {}
        for s in ALL_SEMS:
            sems[s] = ctx.enter_context(nc.semaphore(s))

        wA = Walk("collect")
        build_walk(wA, t)
        wB = Walk("count")
        wB.waited = wA.waited
        build_walk(wB, t)

        ctx.enter_context(nc.allow_low_precision(
            reason="bf16 matmul operands with fp32 PSUM per stated tolerance"))
        block = ctx.enter_context(nc.Block())
        emitters = {"T": block.tensor, "V": block.vector, "S": block.scalar,
                    "SY": block.sync, "G": block.gpsimd}
        for stream in STREAMS:
            def body(eng, stream=stream):
                wE = Walk("emit", stream=stream, eng=eng, sems=sems)
                wE.waited = wA.waited
                wE.evt = wB.evt
                build_walk(wE, t)
            emitters[stream](body)
    return nc


def _rotl32(x, r):
    return ((x << np.uint32(r)) | (x >> np.uint32(32 - r))) & np.uint32(0xFFFFFFFF)


def _threefry2x32(k0, k1, x0, x1):
    rot = [[13, 15, 26, 6], [17, 29, 16, 24]]
    ks2 = np.uint32(k0) ^ np.uint32(k1) ^ np.uint32(0x1BD11BDA)
    ks = [np.uint32(k0), np.uint32(k1), ks2]
    x0 = (x0 + ks[0]) & np.uint32(0xFFFFFFFF)
    x1 = (x1 + ks[1]) & np.uint32(0xFFFFFFFF)
    for d in range(5):
        for r in rot[d % 2]:
            x0 = (x0 + x1) & np.uint32(0xFFFFFFFF)
            x1 = _rotl32(x1, r) ^ x0
        x0 = (x0 + ks[(d + 1) % 3]) & np.uint32(0xFFFFFFFF)
        x1 = (x1 + ks[(d + 2) % 3] + np.uint32(d + 1)) & np.uint32(0xFFFFFFFF)
    return x0, x1


def _dropout_uniform():
    n = 8 * NT * D
    i = np.arange(n, dtype=np.uint32)
    b1, b2 = _threefry2x32(0, 42, np.zeros(n, np.uint32), i)
    bits = b1 ^ b2
    return (((bits >> np.uint32(9)) | np.uint32(0x3F800000)).view(np.float32)
            - 1.0).reshape(8, NT, D)


def _host_prep(inputs):
    x = np.ascontiguousarray(inputs["L0_attn_hidden_states"])
    W0t = np.ascontiguousarray(inputs["L0_ffn_W0"].T)
    b0 = np.asarray(inputs["L0_ffn_b0"])
    A = np.asarray(inputs["L0_ffn_A"])
    B = np.asarray(inputs["L0_ffn_B"])

    acat = np.zeros((D, 128), np.float32)
    bt = np.zeros((128, D), np.float32)
    for l in range(4):
        acat[:, 32 * l:32 * l + 16] = A[l].T
        bt[32 * l:32 * l + 16, :] = B[l].T
    b0fm = np.ascontiguousarray(b0.reshape(32, 128).T)

    u = _dropout_uniform()
    keep = 0.25
    maskt = np.zeros((8, 4, D, NT), BFNP)
    for l in range(4):
        m = ((u >= l * keep) & (u < (l + 1) * keep)).astype(np.float32) * 8.0
        maskt[:, l] = m.transpose(0, 2, 1).astype(BFNP)

    half = np.arange(HD // 2, dtype=np.float32)
    inv = 1.0 / 10000.0 ** (2.0 * half / HD)
    fr = np.arange(NT, dtype=np.float32)[:, None] * inv[None, :]
    emb = np.concatenate([fr, fr], axis=-1)
    cosfm = np.ascontiguousarray(np.cos(emb).T).astype(BFNP)
    sinfm = np.ascontiguousarray(np.sin(emb).T).astype(BFNP)

    rmat = np.zeros((128, 128), np.float32)
    for dd in range(64):
        rmat[dd + 64, dd] = -1.0
        rmat[dd, dd + 64] = 1.0
    tau = np.arange(128)
    tril = (tau[:, None] <= tau[None, :]).astype(np.float32)
    mcur_valid = ((tau[None, :] - tau[:, None] >= 0) &
                  (tau[None, :] - tau[:, None] <= 64))
    mcur = np.where(mcur_valid, 0.0, -30000.0).astype(np.float32).T
    i64 = np.arange(64)
    mprev_valid = (i64[:, None] >= tau[None, :])
    mprev = np.zeros((128, 128), np.float32)
    mprev[:, 0:64] = np.where(mprev_valid, 0.0, -30000.0).astype(np.float32).T
    i4 = np.concatenate([np.eye(128, dtype=np.float32)] * 4, axis=1)

    common = {
        "wq": np.ascontiguousarray(inputs["L0_attn_Wq"]),
        "wk": np.ascontiguousarray(inputs["L0_attn_Wk"]),
        "wv": np.ascontiguousarray(inputs["L0_attn_Wv"]),
        "wo": np.ascontiguousarray(inputs["L0_attn_Wo"]),
        "w0t": W0t, "acat": acat, "bt": bt, "b0fm": b0fm,
        "cosfm": cosfm, "sinfm": sinfm,
        "rmat": rmat.astype(BFNP), "trilT": tril.astype(BFNP),
        "mcurT": mcur.astype(BFNP), "mprevT": mprev.astype(BFNP),
        "i4rep": i4.astype(BFNP),
        "ones128": np.ones((128, 128), BFNP), "twos128": np.full((128, 128), 2.0, BFNP),
        "onesf128": np.ones((128, 128), np.float32),
        "id128": np.eye(128).astype(BFNP),
    }
    in_maps = []
    for bi in range(8):
        m = dict(common)
        m["x"] = np.ascontiguousarray(x[bi])
        m["maskt"] = np.ascontiguousarray(maskt[bi])
        in_maps.append(m)
    return in_maps


def kernel(L0_attn_hidden_states, L0_attn_Wq, L0_attn_Wk, L0_attn_Wv, L0_attn_Wo,
           L0_ffn_W0, L0_ffn_b0, L0_ffn_A, L0_ffn_B):
    inputs = dict(L0_attn_hidden_states=L0_attn_hidden_states, L0_attn_Wq=L0_attn_Wq,
                  L0_attn_Wk=L0_attn_Wk, L0_attn_Wv=L0_attn_Wv, L0_attn_Wo=L0_attn_Wo,
                  L0_ffn_W0=L0_ffn_W0, L0_ffn_b0=L0_ffn_b0, L0_ffn_A=L0_ffn_A,
                  L0_ffn_B=L0_ffn_B)
    in_maps = _host_prep(inputs)
    nc = build_nc()
    res = run_bass_kernel_spmd(nc, in_maps, core_ids=list(range(8)))
    out = np.stack([np.asarray(r["out"]) for r in res.results], axis=0)
    return out.astype(np.float32)


# version 10
# speedup vs baseline: 1.0469x; 1.0016x over previous
import numpy as np
import ml_dtypes
import concourse.bass as bass
import concourse.mybir as mybir
from concourse.bass_utils import run_bass_kernel_spmd
from contextlib import ExitStack

F32 = mybir.dt.float32
BF16 = mybir.dt.bfloat16
AF = mybir.ActivationFunctionType
ALU = mybir.AluOpType
AX = mybir.AxisListType
BFNP = ml_dtypes.bfloat16

D = 4096
NT = 2048
NH, KVH, HD = 32, 8, 128
CH = 128
NCH = NT // CH
NTT = NT // 128
SSCALE = float(1.0 / np.sqrt(np.float32(HD)))
INV16 = 1.0 / 16.0

STREAMS = ("T", "V", "S", "SY", "G")
STREAM_SEM = {"T": "c_t", "V": "c_v", "S": "c_s", "SY": "c_sy", "G": "c_g"}
DMA_POOL = {"S": [f"d_s{i}" for i in range(2)],
            "SY": [f"d_y{i}" for i in range(6)],
            "G": [f"d_g{i}" for i in range(4)]}
ALL_SEMS = (list(STREAM_SEM.values())
            + [x for p in DMA_POOL.values() for x in p])


class Walk:
    def __init__(self, mode, stream=None, eng=None, sems=None):
        self.mode = mode
        self.stream = stream
        self.eng = eng
        self.sems = sems
        self.waited = set()
        self.evt = {}
        self.C = {s: 0 for s in ALL_SEMS}
        self.dn = {s: 0 for s in DMA_POOL}
        self.hwm = {}

    def _wait(self, names, is_dma=False):
        for w in names:
            if self.mode == "collect":
                self.waited.add(w)
            elif self.mode == "emit":
                if w not in self.evt:
                    raise KeyError(f"wait on unknown event {w}")
                sem, val = self.evt[w]
                if not is_dma and sem == STREAM_SEM.get(self.stream):
                    continue
                if self.hwm.get(sem, -1) >= val:
                    continue
                self.eng.wait_ge(self.sems[sem], val)
                self.hwm[sem] = val

    def op(self, stream, name, emit_fn, waits=()):
        serial = stream in ("V", "S", "G")
        sem = STREAM_SEM[stream]
        if self.mode == "collect":
            self._wait(waits)
            return
        if self.mode == "count":
            if serial:
                self.C[sem] += 1
                if name is not None:
                    self.evt[name] = (sem, self.C[sem])
            elif name is not None and name in self.waited:
                self.C[sem] += 1
                self.evt[name] = (sem, self.C[sem])
            return
        if self.stream != stream:
            return
        self._wait(waits)
        if serial and self.C[sem] > 0:
            self.eng.wait_ge(self.sems[sem], self.C[sem])
        instr = emit_fn(self.eng)
        if serial:
            self.C[sem] += 1
            instr.then_inc(self.sems[sem], 1)
        elif name is not None and name in self.waited:
            self.C[sem] += 1
            instr.then_inc(self.sems[sem], 1)

    def seq(self, stream, name, fns, waits=()):
        for i, fn in enumerate(fns):
            self.op(stream, name if i == len(fns) - 1 else None, fn,
                    waits=waits if i == 0 else ())

    def dma(self, stream, name, emit_fn, waits=()):
        if self.mode == "collect":
            self._wait(waits)
            return
        pool = DMA_POOL[stream]
        sem = pool[self.dn[stream] % len(pool)]
        self.dn[stream] += 1
        if self.mode == "count":
            self.C[sem] += 16
            self.evt[name] = (sem, self.C[sem])
            return
        if self.stream != stream:
            return
        self._wait(waits, is_dma=True)
        if self.C[sem] > 0 and self.hwm.get(sem, -1) < self.C[sem]:
            self.eng.wait_ge(self.sems[sem], self.C[sem])
            self.hwm[sem] = self.C[sem]
        instr = emit_fn(self.eng)
        self.C[sem] += 16
        instr.then_inc(self.sems[sem], 16)


def _ins(ap, idx, step, count):
    dims = [list(d) for d in ap.ap]
    dims.insert(1 + idx, [step, count])
    return bass.AP(ap.tensor, ap.offset, dims)


def build_walk(w, t):
    op, dma = w.op, w.dma
    pb = t["pb"]
    pS = t["pS"]

    CONSTS = ["cos", "sin", "rmat", "tril", "mcur", "mprev", "ones", "twos",
              "onesf", "id128", "b0", "i4rep"]
    for nm in CONSTS:
        dma("SY", f"c_{nm}", lambda e, nm=nm: e.dma_start(out=t[nm + "_sb"][:], in_=t[nm + "_h"][:]))

    st1 = t["st1"]
    for i in range(NTT):
        ib = i % 2
        xb = t["xt"][ib]
        so = (i % 2) * 8
        dma("SY", f"p1x.{i}",
            lambda e, i=i, xb=xb: e.dma_start(out=xb, in_=t["x_h"][i * 128:(i + 1) * 128, :]),
            waits=[f"p1xn.{i-2}"] if i >= 2 else [])
        op("S", f"p1sq.{i}",
           lambda e, xb=xb, so=so: e.activation(t["xc"], xb, AF.Square,
                                                accum_out=st1[:, so + 1:so + 2]),
           waits=[f"p1x.{i}"] + ([f"p1r.{i-2}"] if i >= 2 else []))
        w.seq("V", f"p1nm.{i}", [
               lambda e, xb=xb, so=so: e.tensor_reduce(st1[:, so:so + 1], xb, AX.X, ALU.add),
               lambda e, so=so: e.tensor_scalar(st1[:, so:so + 1], st1[:, so:so + 1], -1.0 / D, None, ALU.mult)],
           waits=[f"p1x.{i}"] + ([f"p1xn.{i-2}"] if i >= 2 else []))
        w.seq("V", f"p1r.{i}", [
               lambda e, so=so: e.tensor_scalar(st1[:, so + 2:so + 3], st1[:, so + 1:so + 2], 1.0 / D, None, ALU.mult),
               lambda e, so=so: e.tensor_tensor(st1[:, so + 5:so + 6], st1[:, so:so + 1], st1[:, so:so + 1], ALU.mult),
               lambda e, so=so: e.tensor_tensor(st1[:, so + 2:so + 3], st1[:, so + 2:so + 3], st1[:, so + 5:so + 6], ALU.subtract),
               lambda e, so=so: e.tensor_scalar(st1[:, so + 2:so + 3], st1[:, so + 2:so + 3], 1e-5, None, ALU.add),
               lambda e, so=so: e.reciprocal(st1[:, so + 3:so + 4], st1[:, so + 2:so + 3])],
           waits=[f"p1sq.{i}"])
        w.seq("S", f"p1xn.{i}", [
               lambda e, so=so: e.activation(st1[:, so + 4:so + 5], st1[:, so + 3:so + 4], AF.Sqrt),
               lambda e, xb=xb, so=so: e.activation(t["xc"], xb, AF.Identity, bias=st1[:, so:so + 1]),
               lambda e, so=so, ib=ib: e.activation(t["xn_tm"][ib], t["xc"], AF.Identity, scale=st1[:, so + 4:so + 5])],
           waits=[f"p1r.{i}", f"p1nm.{i}"] + ([f"p1tr.{i-2}"] if i >= 2 else []))
        dma("S", f"p1tr.{i}",
            lambda e, i=i, ib=ib: e.dma_start_transpose(
                t["bigA"][:].rearrange("p (c tt) -> p c tt", c=32)[:, :, i * 128:(i + 1) * 128],
                t["xn_tm"][ib]),
            waits=[f"p1xn.{i}"])
    P1_END = [f"p1tr.{NTT-2}", f"p1tr.{NTT-1}", f"p1xn.{NTT-1}"]

    dma("G", "w_acat",
        lambda e: e.dma_start(out=t["acat"],
                              in_=t["acat_h"][:].rearrange("(k p) m -> p k m", p=128)),
        waits=P1_END)
    dma("G", "w_bt", lambda e: e.dma_start(out=t["bt"], in_=t["bt_h"][:]), waits=P1_END)
    xn = t["bigA"][:].rearrange("p (k tt) -> p k tt", k=32)
    for ts in range(4):
        def mm_h(e, ts=ts):
            ac = t["acat"].rearrange("p (k m) -> p k m", k=32)
            for k in range(32):
                ii = e.matmul(pb[0][:], ac[:, k, :], xn[:, k, ts * 512:(ts + 1) * 512],
                              start=(k == 0), stop=(k == 31))
            return ii
        op("T", f"p2h.{ts}", mm_h,
           waits=["w_acat", f"p1tr.{NTT-1}"] + ([f"p2hev.{ts-1}"] if ts else []))
        op("S", f"p2hev.{ts}",
           lambda e, ts=ts: e.copy(t["h"][:, ts * 512:(ts + 1) * 512], pb[0][:]),
           waits=[f"p2h.{ts}"])
    H_READY = [f"p2hev.{3}"]

    def _w0_load(b):
        src = bass.AP(t["w0t_h"].tensor, b * 256, [[D, 128], [128 * D, 32], [1, 256]])
        dma("G", f"w0.{b}",
            lambda e, src=src, b=b: e.dma_start(out=t["wbuf"][b % 2], in_=src),
            waits=P1_END if b < 2 else [f"p2mm.{(b-2)*8 + 7}"])
    _w0_load(0)
    _w0_load(1)
    for lt in range(128):
        j, ts = lt // 4, lt % 4
        src = bass.AP(t["maskt_h"].tensor, j * 128 * NT + ts * 512,
                      [[NT, 128], [D * NT, 4], [1, 512]])
        dma("G", f"msk.{lt}",
            lambda e, src=src, lt=lt: e.dma_start(out=t["mbig"][lt % 2], in_=src),
            waits=P1_END if lt < 2 else [f"p2tl3.{lt - 2}"])
        if lt % 8 == 7 and lt // 8 + 2 < 16:
            _w0_load(lt // 8 + 2)
    def p2_emit_base(it):
        j, ts = it // 4, it % 4
        b, j2 = j // 2, j % 2
        bk = 4 + it % 2

        def mm_base(e, b=b, j2=j2, ts=ts, bk=bk):
            wb = t["wbuf"][b % 2].rearrange("p (k m) -> p k m", k=32)
            for k in range(32):
                ii = e.matmul(pb[bk][:], wb[:, k, j2 * 128:j2 * 128 + 128],
                              xn[:, k, ts * 512:(ts + 1) * 512],
                              start=(k == 0), stop=False)
            return ii
        op("T", f"p2b.{it}", mm_base,
           waits=[f"w0.{b}"] + ([f"p2sil.{it-2}"] if it >= 2 else H_READY))

        def mm_y(e, j=j, ts=ts):
            for l in range(4):
                ii = e.matmul(pb[l][:], t["bt"][32 * l:32 * l + 16, j * 128:(j + 1) * 128],
                              t["h"][32 * l:32 * l + 16, ts * 512:(ts + 1) * 512],
                              start=True, stop=True, tile_position=(32 * l, 0))
            return ii
        op("T", f"p2y.{it}", mm_y,
           waits=["w_bt"] + ([f"p2tl3.{it-1}"] if it else []))
        for l in range(4):
            op("V", f"p2tl{l}.{it}",
               lambda e, l=l, ts=ts, it=it: e.tensor_tensor(
                   t["tl"][l], pb[l][:],
                   t["mbig"][it % 2].rearrange("p (l4 tt) -> p l4 tt", l4=4)[:, l, :],
                   ALU.mult),
               waits=[f"p2y.{it}", f"msk.{it}"] + ([f"p2mm.{it-1}"] if it else []))

    def p2_emit_fin(it):
        j = it // 4
        bk = 4 + it % 2

        def mm_id(e, bk=bk, it=it):
            for l in range(4):
                ii = e.matmul(pb[bk][:], t["id128_sb"][:], t["tl"][l],
                              start=False, stop=(l == 3))
            return ii
        op("T", f"p2mm.{it}", mm_id,
           waits=["c_id128", f"p2tl3.{it}"])
        op("S", f"p2sil.{it}",
           lambda e, it=it, j=j, bk=bk: e.activation(t["gv"][it % 2], pb[bk][:], AF.Silu,
                                                     bias=t["b0_sb"][:, j:j + 1]),
           waits=[f"p2mm.{it}", "c_b0"] + ([f"p2gs.{it-2}"] if it >= 2 else []))
        dma("SY", f"p2gs.{it}",
            lambda e, it=it, j=j, ts=it % 4: e.dma_start(
                out=t["g_sp"][j, :, ts * 512:(ts + 1) * 512], in_=t["gv"][it % 2]),
            waits=[f"p2sil.{it}"])

    p2_emit_base(0)
    for it in range(1, 128):
        p2_emit_base(it)
        p2_emit_fin(it - 1)
    p2_emit_fin(127)
    P2_POOL_FREE = [f"p2tl3.{127}", f"p2sil.{127}"]

    for vq in range(4):
        src = bass.AP(t["wv_h"].tensor, vq * 256,
                      [[KVH * HD, 128], [128 * KVH * HD, 32], [1, 256]])
        dma("G", f"wv.{vq}",
            lambda e, src=src, vq=vq: e.dma_start(out=t["wbuf"][vq % 2], in_=src),
            waits=[f"p2mm.{119 + 8 * vq}"] if vq < 2 else [f"p3vmm.{(vq-1)*16 - 1}"])
        for tt in range(16):
            it = vq * 16 + tt

            def mm_v(e, vq=vq, tt=tt, it=it):
                wb = t["wbuf"][vq % 2].rearrange("p (k m) -> p k m", k=32)
                for k in range(32):
                    ii = e.matmul(pb[4 + it % 2][:, 0:256], xn[:, k, tt * 128:(tt + 1) * 128],
                                  wb[:, k, :], start=(k == 0), stop=(k == 31))
                return ii
            op("T", f"p3vmm.{it}", mm_v,
               waits=[f"wv.{vq}"] + ([f"p3vev.{it-2}"] if it >= 2 else [f"p2sil.{126}", f"p2sil.{127}"]))
            op("S", f"p3vev.{it}",
               lambda e, it=it: e.copy(t["ev"][it % 4][:, 0:256], pb[4 + it % 2][:, 0:256]),
               waits=[f"p3vmm.{it}"] + ([f"p3vsp.{it-4}"] if it >= 4 else P2_POOL_FREE),)
            dst = bass.AP(t["v_sp"].tensor, tt * 128 * KVH * 128 + vq * 256,
                          [[KVH * 128, 128], [1, 256]])
            dma("SY", f"p3vsp.{it}",
                lambda e, dst=dst, it=it: e.dma_start(out=dst, in_=t["ev"][it % 4][:, 0:256]),
                waits=[f"p3vev.{it}"])
    for b in range(4):
        src = bass.AP(t["wk_h"].tensor, b * 256,
                      [[KVH * HD, 128], [128 * KVH * HD, 32], [1, 256]])
        dma("G", f"wk.{b}",
            lambda e, src=src, b=b: e.dma_start(out=t["wbuf"][b % 2], in_=src),
            waits=[f"p3vmm.{47 + 16 * b}"] if b < 2 else [f"p3kmm.{(b-1)*8 - 1}"])
        for sub in range(8):
            j2, ts = sub // 4, sub % 4
            j = b * 2 + j2
            it = b * 8 + sub

            def mm_k(e, b=b, j2=j2, ts=ts, it=it):
                wb = t["wbuf"][b % 2].rearrange("p (k m) -> p k m", k=32)
                for k in range(32):
                    ii = e.matmul(pb[4 + it % 2][:], wb[:, k, j2 * 128:j2 * 128 + 128],
                                  xn[:, k, ts * 512:(ts + 1) * 512],
                                  start=(k == 0), stop=(k == 31))
                return ii
            op("T", f"p3kmm.{it}", mm_k,
               waits=[f"wk.{b}"] + ([f"p3kev.{it-2}"] if it >= 2 else [f"p3vev.{62 + it}"]))
            op("S", f"p3kev.{it}",
               lambda e, it=it: e.copy(t["ev"][it % 4], pb[4 + it % 2][:]),
               waits=[f"p3kmm.{it}"] + ([f"p3ksp.{it-4}"] if it >= 4 else [f"p3vsp.{60 + it}"]))
            dst = bass.AP(t["k_sp"].tensor, ts * 4 * 128 * KVH * 128 + j * 128,
                          [[KVH * 128, 128], [128 * KVH * 128, 4], [1, 128]])
            dma("SY", f"p3ksp.{it}",
                lambda e, dst=dst, it=it: e.dma_start(out=dst, in_=t["ev"][it % 4]),
                waits=[f"p3kev.{it}"])
    for b in range(16):
        src = bass.AP(t["wq_h"].tensor, b * 256, [[D, 128], [128 * D, 32], [1, 256]])
        dma("G", f"wqb.{b}",
            lambda e, src=src, b=b: e.dma_start(out=t["wbuf"][b % 2], in_=src),
            waits=[f"p3kmm.{23 + 8 * b}"] if b < 2 else [f"p3qmm.{(b-1)*8 - 1}"])
        for sub in range(8):
            j2, ts = sub // 4, sub % 4
            j = b * 2 + j2
            it = b * 8 + sub

            def mm_q(e, b=b, j2=j2, ts=ts, it=it):
                wb = t["wbuf"][b % 2].rearrange("p (k m) -> p k m", k=32)
                for k in range(32):
                    ii = e.matmul(pb[4 + it % 2][:], wb[:, k, j2 * 128:j2 * 128 + 128],
                                  xn[:, k, ts * 512:(ts + 1) * 512],
                                  start=(k == 0), stop=(k == 31))
                return ii
            op("T", f"p3qmm.{it}", mm_q,
               waits=[f"wqb.{b}"] + ([f"p3qev.{it-2}"] if it >= 2 else [f"p3kev.{30 + it}"]))
            op("S", f"p3qev.{it}",
               lambda e, it=it: e.copy(t["ev"][it % 4], pb[4 + it % 2][:]),
               waits=[f"p3qmm.{it}"] + ([f"p3qsp.{it-4}"] if it >= 4 else [f"p3ksp.{28 + it}"]))
            dst = bass.AP(t["q_sp"].tensor, ts * 4 * 128 * NH * 128 + j * 128,
                          [[NH * 128, 128], [128 * NH * 128, 4], [1, 128]])
            dma("SY", f"p3qsp.{it}",
                lambda e, dst=dst, it=it: e.dma_start(out=dst, in_=t["ev"][it % 4]),
                waits=[f"p3qev.{it}"])
    P3_SPILLED = [f"p3qsp.{i}" for i in range(121, 128)]

    w.seq("G", "s_zero", [
           lambda e: e.memset(t["S"], 0.0),
           lambda e: e.memset(t["Sbf"], 0.0)],
       waits=P3_SPILLED + [f"p3qev.{127}"])
    KW = KVH * 128

    def emit_p5_block(ts, b):
        src = bass.AP(t["wo_h"].tensor, b * 256, [[D, 128], [128 * D, 32], [1, 256]])
        wdep = list(P3_SPILLED) if (ts == 0 and b < 2) else [f"p5mm.{(2*b-3)*4+ts}"] if b >= 2 else [f"p5mm.{(2*(b+14)+1)*4+ts-1}"]
        dma("G", f"wo.{ts}.{b}",
            lambda e, src=src, b=b: e.dma_start(out=t["wobuf"][b % 2], in_=src),
            waits=wdep)

    p5_seq = []

    def emit_p5_jtile(ts, j):
        b, j2 = j // 2, j % 2
        it = j * 4 + ts
        if j2 == 0:
            emit_p5_block(ts, b)
        ei = len(p5_seq)
        prev = p5_seq[-1] if p5_seq else None
        pprev = p5_seq[-2] if len(p5_seq) >= 2 else None
        p5_seq.append(it)
        dma("SY", f"gld.{it}",
            lambda e, j=j, ts=ts, ei=ei: e.dma_start(
                out=t["gld5"][ei % 2], in_=t["g_sp"][j, :, ts * 512:(ts + 1) * 512]),
            waits=[f"p2gs.{it}"] + ([f"sg.{pprev}"] if pprev is not None else []))

        def mm_s(e, b=b, j2=j2, ts=ts):
            wb = t["wobuf"][b % 2].rearrange("p (k m) -> p k m", k=32)
            off = (4 * ts % 8) * 128
            for k in range(32):
                ii = e.matmul(t["pWo"][:], wb[:, k, j2 * 128:j2 * 128 + 128],
                              bass.AP(t["bigA"][:].tensor, k * 1024 + off,
                                      [[32 * NT, 128], [1, 512]]),
                              start=(k == 0), stop=(k == 31))
            return ii
        op("T", f"p5mm.{it}", mm_s,
           waits=[f"wo.{ts}.{b}", f"oev.{4*ts+3}.{KVH-1}"] +
                 ([f"sg.{prev}"] if prev is not None else []))
        op("V", f"sg.{it}",
           lambda e, it=it, ei=ei: e.tensor_tensor(t["sgv5"][ei % 2], t["pWo"][:],
                                                   t["gld5"][ei % 2], ALU.mult),
           waits=[f"p5mm.{it}", f"gld.{it}"] + ([f"sgt.{pprev}"] if pprev is not None else []))
        dma("S", f"sgt.{it}",
            lambda e, it=it, ei=ei: e.dma_start_transpose(
                t["sgt5"][ei % 2].rearrange("p (r q) -> p r q", r=4),
                t["sgv5"][ei % 2]),
            waits=[f"sg.{it}"] + ([f"sgsp.{pprev}"] if pprev is not None else []))
        dst = bass.AP(t["sg_sp"].tensor, ts * 512 * D + j * 128,
                      [[D, 128], [128 * D, 4], [1, 128]])
        dma("SY", f"sgsp.{it}",
            lambda e, dst=dst, it=it, ei=ei: e.dma_start(
                out=dst, in_=t["sgt5"][ei % 2].rearrange("p (r q) -> p r q", r=4)),
            waits=[f"sgt.{it}"])

    def emit_pass2(c, gg, cr):
        hs = slice(gg * 128, (gg + 1) * 128)
        gr = gg % 2

        def mm_o(e, c=c, g=gg, gr=gr, hs=hs):
            ii = e.matmul(pb[2 + gr][:], t["Sbf"][:, hs], t["qtil3"][g % 3], start=True, stop=False)
            ii = e.matmul(pb[2 + gr][:], t["vring"][c % 2][:, hs], t["am3"][g % 3], start=False, stop=False)
            ii = e.matmul(pb[2 + gr][:], t["vring"][c % 2][:, hs], t["pcm3"][g % 3], start=False,
                          stop=(c == 0))
            if c > 0:
                ii = e.matmul(pb[2 + gr][:], t["vring"][(c - 1) % 2][64:128, hs],
                              t["ppm3"][g % 3][64:128, :], start=False, stop=True)
            return ii
        ow = [f"qt.{c}.{gg}", f"am.{c}.{gg}", f"pcn.{c}.{gg}", f"vin.{c}"]
        if c > 0:
            ow += [f"ppn.{c}.{gg}"]
        else:
            ow += ["s_zero"]
        ow += [f"oev.{c}.{gg-2}"] if gg >= 2 else ([f"oev.{c-1}.{KVH-2+gg}"] if c else [])
        if gg == 0 and c >= 8:
            ow += [f"p5mm.{124 + (c - 8) // 4}"]
        op("T", f"o.{c}.{gg}", mm_o, waits=ow)
        op("V", f"oev.{c}.{gg}",
           lambda e, c=c, g=gg, gr=gr: e.tensor_copy(
               bass.AP(t["bigA"][:].tensor, 4 * g * 1024 + (c % 8) * 128,
                       [list(t["bigA"][:].ap[0]), [1024, 4], [1, 128]]),
               pb[2 + gr][:].rearrange("p (r tt) -> p r tt", r=4)),
           waits=[f"o.{c}.{gg}"])

        def mm_su(e, c=c, g=gg, hs=hs):
            return e.matmul(pS[:, (g % 4) * 128:(g % 4) * 128 + 128],
                            t["khT"][:, hs], t["vring"][c % 2][:, hs],
                            start=(g % 4 == 0), stop=(g % 4 == 3))
        op("T", f"su.{c}.{gg}", mm_su,
           waits=[f"khTev.{c}", f"vin.{c}", f"o.{c}.{gg}"] +
                 ([f"sadd.{c}.{gg-1}"] if (gg % 4 == 0 and gg >= 4) else []) +
                 ([f"sadd.{c-1}.{KVH-1}"] if (gg == 0 and c > 0) else []) +
                 (["s_zero"] if (gg == 0 and c == 0) else []))
        if gg % 4 == 3:
            w.seq("S", f"sdec.{c}.{gg}", [
                   (lambda e, H=H: e.activation(t["S"][:, H * 128:(H + 1) * 128],
                                                t["S"][:, H * 128:(H + 1) * 128],
                                                AF.Identity, scale=t["explend"][:, H:H + 1]))
                   for H in range(gg - 3, gg + 1)],
               waits=[f"explend.{c}", f"su.{c}.{gg}"] +
                     ([f"sbf.{c-1}"] if c else ["s_zero"]))
            op("V", f"sadd.{c}.{gg}",
               lambda e, c=c, g=gg: e.tensor_tensor(
                   t["S"][:, (g - 3) * 128:(g + 1) * 128],
                   t["S"][:, (g - 3) * 128:(g + 1) * 128], pS[:], ALU.add),
               waits=[f"sdec.{c}.{gg}", f"su.{c}.{gg}"])

    for c in range(NCH):
        cr = c % 2
        dma("SY", f"kin.{c}",
            lambda e, c=c: e.dma_start(out=t["kin"], in_=t["k_sp"][c, :, :]),
            waits=P3_SPILLED + ([f"p4kdone.{c-1}"] if c else []))
        dma("SY", f"vin.{c}",
            lambda e, c=c, cr=cr: e.dma_start(out=t["vring"][cr], in_=t["v_sp"][c, :, :]),
            waits=(P3_SPILLED + ["s_zero"] if c < 2 else []) +
                  ([f"p4odone.{c-1}"] if c >= 1 else []) +
                  ([f"sadd.{c-2}.{KVH-1}"] if c >= 2 else []))
        op("S", f"sig.{c}", lambda e: e.activation(t["ksig"], t["kin"], AF.Sigmoid),
           waits=[f"kin.{c}"])
        op("S", f"lsig.{c}", lambda e: e.activation(t["ksig"], t["ksig"], AF.Ln), waits=[])
        w.seq("V", f"lam.{c}", [
               (lambda e, H=H: e.tensor_tensor_scan(t["lam"][:, H * 128:(H + 1) * 128],
                                                    t["onesf_sb"][:], t["ksig"][:, H * 128:(H + 1) * 128],
                                                    0.0, ALU.mult, ALU.add)) for H in range(KVH)],
           waits=[f"lsig.{c}", "c_onesf"])
        op("S", f"expl.{c}", lambda e: e.activation(t["expl"], t["lam"], AF.Exp, scale=INV16),
           waits=[f"lam.{c}"])
        op("S", f"explend.{c}",
           lambda e: e.activation(t["explend"],
                                  bass.AP(t["lam"].tensor, t["lam"].offset + 127,
                                          [list(t["lam"].ap[0]), [128, 8]]),
                                  AF.Exp, scale=INV16),
           waits=[f"lam.{c}"] + ([f"sdec.{c-1}.{KVH-1}"] if c else []))
        op("S", f"expli.{c}", lambda e: e.activation(t["expli"], t["lam"], AF.Exp, scale=-INV16),
           waits=[f"lam.{c}"])
        op("S", f"ek.{c}", lambda e: e.activation(t["ekt"], t["kin"], AF.Exp),
           waits=[f"kin.{c}"])
        op("T", f"denk.{c}",
           lambda e: (e.matmul(pb[4][:], t["ones_sb"][:], t["ekt"][:, 0:512], start=True, stop=True),
                      e.matmul(pb[5][:], t["ones_sb"][:], t["ekt"][:, 512:1024], start=True, stop=True))[-1],
           waits=[f"ek.{c}", "c_ones"] + ([f"rs.{c-1}.{KVH-2}", f"rs.{c-1}.{KVH-1}"] if c else []))
        w.seq("S", f"rk.{c}", [
               lambda e: e.activation(t["lnscr"], pb[4][:], AF.Ln),
               lambda e: e.activation(t["rkt2"][:, 0:512], t["lnscr"], AF.Exp, scale=-1.0),
               lambda e: e.activation(t["lnscr"], pb[5][:], AF.Ln),
               lambda e: e.activation(t["rkt2"][:, 512:1024], t["lnscr"], AF.Exp, scale=-1.0)],
           waits=[f"denk.{c}"])
        w.seq("V", f"kt.{c}", [
               lambda e: e.tensor_tensor(t["ktmp"].bitcast(BF16)[:, 0:1024], t["ekt"], t["expli"], ALU.mult),
               lambda e: e.tensor_tensor(t["ktil"], t["ktmp"].bitcast(BF16)[:, 0:1024], t["rkt2"], ALU.mult)],
           waits=[f"rk.{c}", f"expli.{c}"])
        op("V", f"kh.{c}",
           lambda e: e.tensor_tensor(
               t["khat"].rearrange("p (H tt) -> p H tt", H=KVH),
               t["ktil"].rearrange("p (H tt) -> p H tt", H=KVH),
               _ins(t["explend"], 1, 0, 128), ALU.mult),
           waits=[f"kt.{c}", f"explend.{c}"])
        op("T", f"rotk.{c}",
           lambda e: (e.matmul(pb[4][:], t["rmat_sb"][:], t["kin"][:, 0:512], start=True, stop=True),
                      e.matmul(pb[5][:], t["rmat_sb"][:], t["kin"][:, 512:1024], start=True, stop=True))[-1],
           waits=[f"kin.{c}", "c_rmat", f"rk.{c}"])
        cosc = _ins(t["cos_sb"][:, c * 128:(c + 1) * 128], 0, 0, KVH)
        sinc = _ins(t["sin_sb"][:, c * 128:(c + 1) * 128], 0, 0, KVH)
        ktc = t["ktmp"].bitcast(BF16)[:, 0:1024]
        w.seq("V", f"sk.{c}", [
               lambda e, cosc=cosc, ktc=ktc: e.tensor_tensor(
                   ktc.rearrange("p (H tt) -> p H tt", H=KVH),
                   t["kin"].rearrange("p (H tt) -> p H tt", H=KVH), cosc, ALU.mult),
               lambda e, sinc=sinc: e.tensor_tensor(
                   t["krot"][:, 0:512].rearrange("p (r tt) -> p r tt", r=4),
                   pb[4][:].rearrange("p (r tt) -> p r tt", r=4), sinc[:, 0:4, :], ALU.mult),
               lambda e, sinc=sinc: e.tensor_tensor(
                   t["krot"][:, 512:1024].rearrange("p (r tt) -> p r tt", r=4),
                   pb[5][:].rearrange("p (r tt) -> p r tt", r=4), sinc[:, 4:8, :], ALU.mult),
               lambda e, cr=cr, ktc=ktc: e.tensor_tensor(t["skring"][cr], ktc, t["krot"], ALU.add)],
           waits=[f"rotk.{c}", "c_cos", "c_sin", f"kh.{c}"] +
                 ([f"scp.{c-1}.{KVH-1}"] if c >= 2 else []))
        op("T", f"khT.{c}",
           lambda e: [e.transpose(pb[0][:].bitcast(BF16)[:, H * 128:(H + 1) * 128],
                                  t["khat"][:, H * 128:(H + 1) * 128], t["id128_sb"][:])
                      for H in range(KVH)][-1],
           waits=[f"kh.{c}", "c_id128", f"sk.{c}"] + ([f"khTev.{c-1}"] if c else []) +
                 ([f"am.{c-1}.{KVH-2}"] if c else []))
        op("S", f"khTev.{c}", lambda e: e.copy(t["khT"], pb[0][:].bitcast(BF16)),
           waits=[f"khT.{c}"] + ([f"sadd.{c-1}.{KVH-1}"] if c else []))
        op("V", f"p4kdone.{c}", lambda e: e.tensor_copy(t["ktmp"].bitcast(BF16)[:, 0:1], t["ktil"][:, 0:1]),
           waits=[f"sk.{c}", f"kh.{c}", f"ek.{c}"])

        for g in range(KVH):
            gr = g % 2
            dma("SY", f"qin.{c}.{g}",
                lambda e, c=c, g=g, gr=gr: e.dma_start(
                    out=t["qg"][gr],
                    in_=bass.AP(t["q_sp"].tensor, c * 128 * NH * 128 + g * 512,
                                [[NH * 128, 128], [1, 512]])),
                waits=P3_SPILLED + ([f"sq.{c}.{g-2}", f"eq.{c}.{g-2}"] if g >= 2 else
                                    ([f"sq.{c-1}.{KVH-2+g}", f"eq.{c-1}.{KVH-2+g}"] if c else [])))
            op("S", f"eq.{c}.{g}",
               lambda e, gr=gr, g=g: e.activation(t["eq2"][g % 2], t["qg"][gr], AF.Exp),
               waits=[f"qin.{c}.{g}"] + ([f"qt1.{c}.{g-2}"] if g >= 2 else []))
            op("T", f"denq.{c}.{g}",
               lambda e, gr=gr, g=g: e.matmul(pb[4 + gr][:], t["twos_sb"][:], t["eq2"][g % 2], start=True, stop=True),
               waits=[f"eq.{c}.{g}", "c_twos"] +
                     ([f"p4rq.{c}.{g-2}"] if g >= 2 else [f"sk.{c}"]))
            w.seq("S", f"p4rq.{c}.{g}", [
                   lambda e, gr=gr: e.activation(t["lnscr"], pb[4 + gr][:], AF.Ln),
                   lambda e, g=g: e.activation(t["rq2"][g % 2], t["lnscr"], AF.Exp, scale=-1.0)],
               waits=[f"denq.{c}.{g}"])
            op("V", f"qt1.{c}.{g}",
               lambda e, g=g: e.tensor_tensor(t["qt18"][g], t["eq2"][g % 2], t["rq2"][g % 2], ALU.mult),
               waits=[f"p4rq.{c}.{g}"])
            op("T", f"rotq.{c}.{g}",
               lambda e, gr=gr, g=g: e.matmul(pb[g % 2][:], t["rmat_sb"][:], t["qg"][gr], start=True, stop=True),
               waits=[f"qin.{c}.{g}", "c_rmat"] +
                     ([f"sq.{c}.{g-2}"] if g >= 2 else
                      ([f"am.{c-1}.{6+g}"] if c else [])) +
                     ([f"khTev.{c}"] if g == 0 else []))
            cos4 = _ins(t["cos_sb"][:, c * 128:(c + 1) * 128], 0, 0, 4)
            sin4 = _ins(t["sin_sb"][:, c * 128:(c + 1) * 128], 0, 0, 4)
            w.seq("V", f"sq.{c}.{g}", [
                   lambda e, gr=gr, cos4=cos4: e.tensor_tensor(
                       t["sq2"].rearrange("p (r tt) -> p r tt", r=4),
                       t["qg"][gr].rearrange("p (r tt) -> p r tt", r=4), cos4, ALU.mult),
                   lambda e, g=g, sin4=sin4: e.tensor_tensor(
                       t["sq3"].rearrange("p (r tt) -> p r tt", r=4),
                       pb[g % 2][:].rearrange("p (r tt) -> p r tt", r=4), sin4, ALU.mult),
                   lambda e, g=g: e.tensor_tensor(t["sqa8"][g], t["sq2"], t["sq3"], ALU.add)],
               waits=[f"rotq.{c}.{g}"])

        for g in range(KVH):
            gr = g % 2
            hs = slice(g * 128, (g + 1) * 128)
            expl_g = _ins(t["expl"][:, g * 128:(g + 1) * 128], 0, 0, 4)
            op("V", f"qt.{c}.{g}",
               lambda e, expl_g=expl_g, g=g: e.tensor_tensor(
                   t["qtil3"][g % 3].rearrange("p (r tt) -> p r tt", r=4),
                   t["qt18"][g].rearrange("p (r tt) -> p r tt", r=4), expl_g, ALU.mult),
               waits=[f"qt1.{c}.{g}", f"expl.{c}"])

            def mm_sc(e, cr=cr, hs=hs, g=g):
                e.matmul(pb[g % 2][:], t["skring"][cr][:, hs], t["sqa8"][g], start=True, stop=False)
                return e.matmul(pb[g % 2][:], t["mcur_sb"][:], t["i4rep_sb"][:],
                                start=False, stop=True)
            op("T", f"sc.{c}.{g}", mm_sc,
               waits=[f"sq.{c}.{g}", f"sk.{c}", "c_mcur", "c_i4rep",
                      f"sq.{c}.{KVH-2 + g % 2}"] +
                     ([f"am.{c}.{g-2}"] if g >= 2 else []))
            if c > 0:
                def mm_scp(e, c=c, g=g, gr=gr):
                    e.matmul(pb[4 + gr][64:128, :],
                             t["skring"][(c - 1) % 2][:, g * 128 + 64:g * 128 + 128],
                             t["sqa8"][g], start=True, stop=False, tile_position=(0, 64))
                    return e.matmul(pb[4 + gr][64:128, :], t["mprev_sb"][:, 0:64],
                                    t["i4rep_sb"][:], start=False, stop=True,
                                    tile_position=(0, 64))
                op("T", f"scp.{c}.{g}", mm_scp,
                   waits=[f"sq.{c}.{g}", f"p4rq.{c}.{KVH-2 + g % 2}", "c_mprev", "c_i4rep"])
            op("S", f"esc.{c}.{g}",
               lambda e, g=g: e.activation(t["esc2"][g % 2], pb[g % 2][:], AF.Exp, scale=SSCALE),
               waits=[f"sc.{c}.{g}"] + ([f"pcn.{c}.{g-2}"] if g >= 2 else []))
            if c > 0:
                op("S", f"esp.{c}.{g}",
                   lambda e, gr=gr, g=g: e.activation(t["esp2"][g % 2][64:128, :], pb[4 + gr][64:128, :], AF.Exp, scale=SSCALE),
                   waits=[f"scp.{c}.{g}"] + ([f"ppn.{c}.{g-2}"] if g >= 2 else []))

            def mm_dsw(e, c=c, gr=gr, g=g):
                ii = e.matmul(pb[4 + gr][:], t["twos_sb"][:], t["esc2"][g % 2], start=True,
                              stop=(c == 0))
                if c > 0:
                    ii = e.matmul(pb[4 + gr][:], t["twos_sb"][64:128, :], t["esp2"][g % 2][64:128, :],
                                  start=False, stop=True)
                return ii
            op("T", f"dsw.{c}.{g}", mm_dsw,
               waits=[f"esc.{c}.{g}", f"p4rq.{c}.{KVH-2 + g % 2}"] +
                     ([f"esp.{c}.{g}"] if c else []))
            w.seq("S", f"rs.{c}.{g}", [
                   lambda e, gr=gr: e.activation(t["lnscr"], pb[4 + gr][:], AF.Ln),
                   lambda e, g=g: e.activation(t["rs2"][g % 2], t["lnscr"], AF.Exp, scale=-1.0)],
               waits=[f"dsw.{c}.{g}"])
            op("V", f"pcn.{c}.{g}",
               lambda e, g=g: e.tensor_tensor(t["pcm3"][g % 3], t["esc2"][g % 2], t["rs2"][g % 2], ALU.mult),
               waits=[f"rs.{c}.{g}"])
            if c > 0:
                op("V", f"ppn.{c}.{g}",
                   lambda e, g=g: e.tensor_tensor(t["ppm3"][g % 3][64:128, :], t["esp2"][g % 2][64:128, :],
                                                  t["rs2"][g % 2][64:128, :], ALU.mult),
                   waits=[f"rs.{c}.{g}"])
            op("T", f"at.{c}.{g}",
               lambda e, hs=hs, g=g: e.matmul(pb[g % 2][:], t["ktil"][:, hs], t["qtil3"][g % 3],
                                              start=True, stop=True),
               waits=[f"qt.{c}.{g}", f"kt.{c}", f"esc.{c}.{g}"])
            tril4 = _ins(t["tril_sb"][:], 0, 0, 4)
            op("V", f"am.{c}.{g}",
               lambda e, g=g, tril4=tril4: e.tensor_tensor(
                   t["am3"][g % 3].rearrange("p (r tt) -> p r tt", r=4),
                   pb[g % 2][:].rearrange("p (r tt) -> p r tt", r=4), tril4, ALU.mult),
               waits=[f"at.{c}.{g}", "c_tril"])
            if g >= 2:
                emit_pass2(c, g - 2, cr)
        emit_pass2(c, KVH - 2, cr)
        emit_pass2(c, KVH - 1, cr)
        op("V", f"sbf.{c}", lambda e: e.tensor_copy(t["Sbf"], t["S"]),
           waits=[f"sadd.{c}.{KVH-1}"])
        op("V", f"p4odone.{c}", lambda e: e.tensor_copy(t["qt2"][:, 0:1], t["qtil3"][0][:, 0:1]),
           waits=[f"oev.{c}.{KVH-1}", f"oev.{c}.{KVH-2}"])
        if c >= 4:
            ts = (c - 4) // 4
            j0 = 8 * ((c - 4) % 4)
            for j in range(j0, j0 + 8):
                emit_p5_jtile(ts, j)
    for j in range(32):
        emit_p5_jtile(3, j)
    P4_END = [f"p4odone.{NCH-1}", f"sadd.{NCH-1}.{KVH-1}", f"sbf.{NCH-1}",
              f"p4kdone.{NCH-1}", f"khTev.{NCH-1}", f"sq.{NCH-1}.{KVH-1}", f"qt1.{NCH-1}.{KVH-1}"]

    for i2 in range(2 * NTT):
        i, hf = i2 // 2, i2 % 2
        dma("SY", f"p6sg.{i2}",
            lambda e, i=i, hf=hf, i2=i2: e.dma_start(
                out=t["sgh"][i2 % 2],
                in_=t["sg_sp"][i * 128:(i + 1) * 128, hf * 2048:(hf + 1) * 2048]),
            waits=[f"sgsp.{j*4+3}" for j in range(26, 32)] + [f"gld.{j*4+3}" for j in range(29, 32)]
                  + ([f"p6add.{i2-2}"] if i2 >= 2 else []))
        dma("G", f"p6x.{i2}",
            lambda e, i=i, hf=hf, i2=i2: e.dma_start(
                out=t["xf"][i2 % 2],
                in_=t["x_h"][i * 128:(i + 1) * 128, hf * 2048:(hf + 1) * 2048]),
            waits=([f"p6add.{i2-2}"] if i2 >= 2 else [f"sg.{127}"]))
        op("V", f"p6add.{i2}",
           lambda e, i2=i2: e.tensor_tensor(t["of"][i2 % 2], t["xf"][i2 % 2],
                                            t["sgh"][i2 % 2], ALU.add),
           waits=[f"p6sg.{i2}", f"p6x.{i2}"] + ([f"p6st.{i2-2}"] if i2 >= 2 else [f"sgsp.{127}"]))
        dma("SY", f"p6st.{i2}",
            lambda e, i=i, hf=hf, i2=i2: e.dma_start(
                out=t["out_h"][i * 128:(i + 1) * 128, hf * 2048:(hf + 1) * 2048],
                in_=t["of"][i2 % 2]),
            waits=[f"p6add.{i2}"])
    op("SY", "fin", lambda e: e.nop(), waits=[f"p6st.{2*NTT-1}"])


def build_nc():
    nc = bass.Bass("TRN2", target_bir_lowering=False)
    t = {}
    t["x_h"] = nc.dram_tensor("x", [NT, D], F32, kind="ExternalInput").ap()
    t["wq_h"] = nc.dram_tensor("wq", [D, D], F32, kind="ExternalInput").ap()
    t["wk_h"] = nc.dram_tensor("wk", [D, KVH * HD], F32, kind="ExternalInput").ap()
    t["wv_h"] = nc.dram_tensor("wv", [D, KVH * HD], F32, kind="ExternalInput").ap()
    t["wo_h"] = nc.dram_tensor("wo", [D, D], F32, kind="ExternalInput").ap()
    t["w0t_h"] = nc.dram_tensor("w0t", [D, D], F32, kind="ExternalInput").ap()
    t["acat_h"] = nc.dram_tensor("acat", [D, 128], F32, kind="ExternalInput").ap()
    t["bt_h"] = nc.dram_tensor("bt", [128, D], F32, kind="ExternalInput").ap()
    t["b0_h"] = nc.dram_tensor("b0fm", [128, 32], F32, kind="ExternalInput").ap()
    t["maskt_h"] = nc.dram_tensor("maskt", [4, D, NT], BF16, kind="ExternalInput").ap()
    t["cos_h"] = nc.dram_tensor("cosfm", [128, NT], BF16, kind="ExternalInput").ap()
    t["sin_h"] = nc.dram_tensor("sinfm", [128, NT], BF16, kind="ExternalInput").ap()
    t["rmat_h"] = nc.dram_tensor("rmat", [128, 128], BF16, kind="ExternalInput").ap()
    t["tril_h"] = nc.dram_tensor("trilT", [128, 128], BF16, kind="ExternalInput").ap()
    t["mcur_h"] = nc.dram_tensor("mcurT", [128, 128], BF16, kind="ExternalInput").ap()
    t["mprev_h"] = nc.dram_tensor("mprevT", [128, 128], BF16, kind="ExternalInput").ap()
    t["i4rep_h"] = nc.dram_tensor("i4rep", [128, 512], BF16, kind="ExternalInput").ap()
    t["ones_h"] = nc.dram_tensor("ones128", [128, 128], BF16, kind="ExternalInput").ap()
    t["twos_h"] = nc.dram_tensor("twos128", [128, 128], BF16, kind="ExternalInput").ap()
    t["onesf_h"] = nc.dram_tensor("onesf128", [128, 128], F32, kind="ExternalInput").ap()
    t["id128_h"] = nc.dram_tensor("id128", [128, 128], BF16, kind="ExternalInput").ap()
    t["out_h"] = nc.dram_tensor("out", [NT, D], F32, kind="ExternalOutput").ap()
    t["q_sp"] = nc.dram_tensor("q_sp", [NCH, 128, NH * 128], BF16).ap()
    t["k_sp"] = nc.dram_tensor("k_sp", [NCH, 128, KVH * 128], BF16).ap()
    t["v_sp"] = nc.dram_tensor("v_sp", [NCH, 128, KVH * 128], BF16).ap()
    t["g_sp"] = nc.dram_tensor("g_sp", [32, 128, NT], F32).ap()
    t["sg_sp"] = nc.dram_tensor("sg_sp", [NT, D], BF16).ap()

    ctx = ExitStack()
    with ctx:
        def sb(name, shape, dt=BF16):
            h = ctx.enter_context(nc.sbuf_tensor(name, shape, dt))
            t[name] = h
            return h
        sb("bigA", [128, 32 * NT])
        for nm, shape, dt in [("cos_sb", [128, NT], BF16), ("sin_sb", [128, NT], BF16),
                              ("rmat_sb", [128, 128], BF16), ("tril_sb", [128, 128], BF16),
                              ("mcur_sb", [128, 128], BF16), ("mprev_sb", [128, 128], BF16),
                              ("i4rep_sb", [128, 512], BF16),
                              ("ones_sb", [128, 128], BF16), ("twos_sb", [128, 128], BF16),
                              ("onesf_sb", [128, 128], F32), ("id128_sb", [128, 128], BF16),
                              ("b0_sb", [128, 32], F32), ("st1", [128, 16], F32)]:
            sb(nm, shape, dt)
        pool = ctx.enter_context(nc.sbuf_tensor("pool", [128, 34816], BF16))

        def V_(off, n, dt=BF16):
            ap = pool[:, off:off + n]
            if dt == F32:
                ap = ap.bitcast(F32)
            return ap
        t["xt"] = [V_(0, 8192, F32), V_(8192, 8192, F32)]
        t["xc"] = V_(16384, 8192, F32)
        t["xn_tm"] = [V_(24576, 4096), V_(28672, 4096)]
        t["wbuf"] = [V_(0, 8192), V_(8192, 8192)]
        t["mbig"] = [V_(16384, 2048), V_(18432, 2048)]
        t["acat"] = V_(20480, 4096)
        t["bt"] = V_(24576, 4096)
        t["h"] = V_(28672, 2048)
        t["tl"] = [V_(30720 + 512 * i, 512) for i in range(4)]
        t["gv"] = [V_(32768, 1024, F32), V_(33792, 1024, F32)]
        t["ev"] = [V_(16384 + 512 * i, 512) for i in range(4)]
        go = 0

        def GV(n, dt=BF16):
            nonlocal go
            v = V_(go, n, dt)
            go += n
            return v
        t["qg"] = [GV(512), GV(512)]
        t["eq2"] = [GV(512), GV(512)]
        t["rq2"] = [GV(512), GV(512)]
        t["lnscr"] = GV(1024, F32)
        t["qtil3"] = [GV(512) for _ in range(3)]
        t["qt2"] = GV(512)
        t["sq1"] = [GV(512), GV(512)]
        t["sq2"] = t["qt2"]
        t["sq3"] = GV(512)
        t["esc2"] = [GV(512), GV(512)]
        t["rs2"] = [GV(512), GV(512)]
        t["pcm3"] = [GV(512) for _ in range(3)]
        t["am3"] = [GV(512) for _ in range(3)]
        t["esp2"] = [GV(512), GV(512)]
        t["ppm3"] = [GV(512) for _ in range(3)]
        ko = go

        def KV(n, dt=BF16):
            nonlocal ko
            v = V_(ko, n, dt)
            ko += n
            return v
        t["kin"] = KV(1024)
        t["ksig"] = KV(2048, F32)
        t["lam"] = KV(2048, F32)
        t["expl"] = KV(1024)
        t["expli"] = t["lam"].bitcast(BF16)[:, 0:1024]
        t["explend"] = t["st1"][:, 8:16]
        t["ekt"] = KV(1024)
        t["rkt2"] = KV(1024)
        t["krot"] = t["ekt"]
        t["ktil"] = KV(1024)
        t["khat"] = KV(1024)
        t["khT"] = KV(1024)
        t["ktmp"] = t["ksig"]
        t["skring"] = [KV(1024), KV(1024)]
        t["vring"] = [KV(1024), KV(1024)]
        t["S"] = KV(2048, F32)
        t["Sbf"] = KV(1024)
        assert ko <= 34816, ko
        def BV(off, n, dt=BF16):
            ap = t["bigA"][:][:, off:off + n]
            if dt == F32:
                ap = ap.bitcast(F32)
            return ap
        t["wobuf"] = [BV(32768, 8192), BV(40960, 8192)]
        t["sgv5"] = [BV(49152, 512), BV(49664, 512)]
        t["gld5"] = [BV(50176, 1024, F32), BV(51200, 1024, F32)]
        t["sgt5"] = [BV(52224, 512), BV(52736, 512)]
        t["qt18"] = [BV(53248 + 512 * i, 512) for i in range(8)]
        t["sqa8"] = [BV(57344 + 512 * i, 512) for i in range(8)]
        t["xf"] = [V_(0, 4096, F32), V_(4096, 4096, F32)]
        t["of"] = [V_(8192, 4096, F32), V_(12288, 4096, F32)]
        t["sgh"] = [V_(16384, 2048), V_(18432, 2048)]
        t["pb"] = [ctx.enter_context(nc.psum_tensor(f"pb{i}", [128, 512], F32)) for i in range(6)]
        t["pS"] = ctx.enter_context(nc.psum_tensor("pS", [128, 512], F32))
        t["pWo"] = ctx.enter_context(nc.psum_tensor("pWo", [128, 512], F32))

        sems = {}
        for s in ALL_SEMS:
            sems[s] = ctx.enter_context(nc.semaphore(s))

        wA = Walk("collect")
        build_walk(wA, t)
        wB = Walk("count")
        wB.waited = wA.waited
        build_walk(wB, t)

        ctx.enter_context(nc.allow_low_precision(
            reason="bf16 matmul operands with fp32 PSUM per stated tolerance"))
        block = ctx.enter_context(nc.Block())
        emitters = {"T": block.tensor, "V": block.vector, "S": block.scalar,
                    "SY": block.sync, "G": block.gpsimd}
        for stream in STREAMS:
            def body(eng, stream=stream):
                wE = Walk("emit", stream=stream, eng=eng, sems=sems)
                wE.waited = wA.waited
                wE.evt = wB.evt
                build_walk(wE, t)
            emitters[stream](body)
    return nc


def _rotl32(x, r):
    return ((x << np.uint32(r)) | (x >> np.uint32(32 - r))) & np.uint32(0xFFFFFFFF)


def _threefry2x32(k0, k1, x0, x1):
    rot = [[13, 15, 26, 6], [17, 29, 16, 24]]
    ks2 = np.uint32(k0) ^ np.uint32(k1) ^ np.uint32(0x1BD11BDA)
    ks = [np.uint32(k0), np.uint32(k1), ks2]
    x0 = (x0 + ks[0]) & np.uint32(0xFFFFFFFF)
    x1 = (x1 + ks[1]) & np.uint32(0xFFFFFFFF)
    for d in range(5):
        for r in rot[d % 2]:
            x0 = (x0 + x1) & np.uint32(0xFFFFFFFF)
            x1 = _rotl32(x1, r) ^ x0
        x0 = (x0 + ks[(d + 1) % 3]) & np.uint32(0xFFFFFFFF)
        x1 = (x1 + ks[(d + 2) % 3] + np.uint32(d + 1)) & np.uint32(0xFFFFFFFF)
    return x0, x1


def _dropout_uniform():
    n = 8 * NT * D
    i = np.arange(n, dtype=np.uint32)
    b1, b2 = _threefry2x32(0, 42, np.zeros(n, np.uint32), i)
    bits = b1 ^ b2
    return (((bits >> np.uint32(9)) | np.uint32(0x3F800000)).view(np.float32)
            - 1.0).reshape(8, NT, D)


def _host_prep(inputs):
    x = np.ascontiguousarray(inputs["L0_attn_hidden_states"])
    W0t = np.ascontiguousarray(inputs["L0_ffn_W0"].T)
    b0 = np.asarray(inputs["L0_ffn_b0"])
    A = np.asarray(inputs["L0_ffn_A"])
    B = np.asarray(inputs["L0_ffn_B"])

    acat = np.zeros((D, 128), np.float32)
    bt = np.zeros((128, D), np.float32)
    for l in range(4):
        acat[:, 32 * l:32 * l + 16] = A[l].T
        bt[32 * l:32 * l + 16, :] = B[l].T
    b0fm = np.ascontiguousarray(b0.reshape(32, 128).T)

    u = _dropout_uniform()
    keep = 0.25
    maskt = np.zeros((8, 4, D, NT), BFNP)
    for l in range(4):
        m = ((u >= l * keep) & (u < (l + 1) * keep)).astype(np.float32) * 8.0
        maskt[:, l] = m.transpose(0, 2, 1).astype(BFNP)

    half = np.arange(HD // 2, dtype=np.float32)
    inv = 1.0 / 10000.0 ** (2.0 * half / HD)
    fr = np.arange(NT, dtype=np.float32)[:, None] * inv[None, :]
    emb = np.concatenate([fr, fr], axis=-1)
    cosfm = np.ascontiguousarray(np.cos(emb).T).astype(BFNP)
    sinfm = np.ascontiguousarray(np.sin(emb).T).astype(BFNP)

    rmat = np.zeros((128, 128), np.float32)
    for dd in range(64):
        rmat[dd + 64, dd] = -1.0
        rmat[dd, dd + 64] = 1.0
    tau = np.arange(128)
    tril = (tau[:, None] <= tau[None, :]).astype(np.float32)
    mcur_valid = ((tau[None, :] - tau[:, None] >= 0) &
                  (tau[None, :] - tau[:, None] <= 64))
    mcur = np.where(mcur_valid, 0.0, -30000.0).astype(np.float32).T
    i64 = np.arange(64)
    mprev_valid = (i64[:, None] >= tau[None, :])
    mprev = np.zeros((128, 128), np.float32)
    mprev[:, 0:64] = np.where(mprev_valid, 0.0, -30000.0).astype(np.float32).T
    i4 = np.concatenate([np.eye(128, dtype=np.float32)] * 4, axis=1)

    common = {
        "wq": np.ascontiguousarray(inputs["L0_attn_Wq"]),
        "wk": np.ascontiguousarray(inputs["L0_attn_Wk"]),
        "wv": np.ascontiguousarray(inputs["L0_attn_Wv"]),
        "wo": np.ascontiguousarray(inputs["L0_attn_Wo"]),
        "w0t": W0t, "acat": acat, "bt": bt, "b0fm": b0fm,
        "cosfm": cosfm, "sinfm": sinfm,
        "rmat": rmat.astype(BFNP), "trilT": tril.astype(BFNP),
        "mcurT": mcur.astype(BFNP), "mprevT": mprev.astype(BFNP),
        "i4rep": i4.astype(BFNP),
        "ones128": np.ones((128, 128), BFNP), "twos128": np.full((128, 128), 2.0, BFNP),
        "onesf128": np.ones((128, 128), np.float32),
        "id128": np.eye(128).astype(BFNP),
    }
    in_maps = []
    for bi in range(8):
        m = dict(common)
        m["x"] = np.ascontiguousarray(x[bi])
        m["maskt"] = np.ascontiguousarray(maskt[bi])
        in_maps.append(m)
    return in_maps


def kernel(L0_attn_hidden_states, L0_attn_Wq, L0_attn_Wk, L0_attn_Wv, L0_attn_Wo,
           L0_ffn_W0, L0_ffn_b0, L0_ffn_A, L0_ffn_B):
    inputs = dict(L0_attn_hidden_states=L0_attn_hidden_states, L0_attn_Wq=L0_attn_Wq,
                  L0_attn_Wk=L0_attn_Wk, L0_attn_Wv=L0_attn_Wv, L0_attn_Wo=L0_attn_Wo,
                  L0_ffn_W0=L0_ffn_W0, L0_ffn_b0=L0_ffn_b0, L0_ffn_A=L0_ffn_A,
                  L0_ffn_B=L0_ffn_B)
    in_maps = _host_prep(inputs)
    nc = build_nc()
    res = run_bass_kernel_spmd(nc, in_maps, core_ids=list(range(8)))
    out = np.stack([np.asarray(r["out"]) for r in res.results], axis=0)
    return out.astype(np.float32)


# version 11
# speedup vs baseline: 1.1775x; 1.1247x over previous
import numpy as np
import ml_dtypes
import concourse.bass as bass
import concourse.mybir as mybir
from concourse.bass_utils import run_bass_kernel_spmd
from contextlib import ExitStack

F32 = mybir.dt.float32
BF16 = mybir.dt.bfloat16
AF = mybir.ActivationFunctionType
ALU = mybir.AluOpType
AX = mybir.AxisListType
BFNP = ml_dtypes.bfloat16

D = 4096
NT = 2048
NH, KVH, HD = 32, 8, 128
CH = 128
NCH = NT // CH
NTT = NT // 128
SSCALE = float(1.0 / np.sqrt(np.float32(HD)))
INV16 = 1.0 / 16.0

STREAMS = ("T", "V", "S", "SY", "G")
STREAM_SEM = {"T": "c_t", "V": "c_v", "S": "c_s", "SY": "c_sy", "G": "c_g"}
DMA_POOL = {"S": [f"d_s{i}" for i in range(2)],
            "SY": [f"d_y{i}" for i in range(6)],
            "G": [f"d_g{i}" for i in range(4)]}
ALL_SEMS = (list(STREAM_SEM.values())
            + [x for p in DMA_POOL.values() for x in p])


class Walk:
    def __init__(self, mode, stream=None, eng=None, sems=None):
        self.mode = mode
        self.stream = stream
        self.eng = eng
        self.sems = sems
        self.waited = set()
        self.evt = {}
        self.C = {s: 0 for s in ALL_SEMS}
        self.dn = {s: 0 for s in DMA_POOL}
        self.hwm = {}

    def _wait(self, names, is_dma=False):
        for w in names:
            if self.mode == "collect":
                self.waited.add(w)
            elif self.mode == "emit":
                if w not in self.evt:
                    raise KeyError(f"wait on unknown event {w}")
                sem, val = self.evt[w]
                if not is_dma and sem == STREAM_SEM.get(self.stream):
                    continue
                if self.hwm.get(sem, -1) >= val:
                    continue
                self.eng.wait_ge(self.sems[sem], val)
                self.hwm[sem] = val

    def op(self, stream, name, emit_fn, waits=()):
        serial = stream in ("V", "S", "G")
        sem = STREAM_SEM[stream]
        if self.mode == "collect":
            self._wait(waits)
            return
        if self.mode == "count":
            if serial:
                self.C[sem] += 1
                if name is not None:
                    self.evt[name] = (sem, self.C[sem])
            elif name is not None and name in self.waited:
                self.C[sem] += 1
                self.evt[name] = (sem, self.C[sem])
            return
        if self.stream != stream:
            return
        self._wait(waits)
        if serial and self.C[sem] > 0:
            self.eng.wait_ge(self.sems[sem], self.C[sem])
        instr = emit_fn(self.eng)
        if serial:
            self.C[sem] += 1
            instr.then_inc(self.sems[sem], 1)
        elif name is not None and name in self.waited:
            self.C[sem] += 1
            instr.then_inc(self.sems[sem], 1)

    def seq(self, stream, name, fns, waits=()):
        for i, fn in enumerate(fns):
            self.op(stream, name if i == len(fns) - 1 else None, fn,
                    waits=waits if i == 0 else ())

    def dma(self, stream, name, emit_fn, waits=()):
        if self.mode == "collect":
            self._wait(waits)
            return
        pool = DMA_POOL[stream]
        sem = pool[self.dn[stream] % len(pool)]
        self.dn[stream] += 1
        if self.mode == "count":
            self.C[sem] += 16
            self.evt[name] = (sem, self.C[sem])
            return
        if self.stream != stream:
            return
        self._wait(waits, is_dma=True)
        if self.C[sem] > 0 and self.hwm.get(sem, -1) < self.C[sem]:
            self.eng.wait_ge(self.sems[sem], self.C[sem])
            self.hwm[sem] = self.C[sem]
        instr = emit_fn(self.eng)
        self.C[sem] += 16
        instr.then_inc(self.sems[sem], 16)


def _ins(ap, idx, step, count):
    dims = [list(d) for d in ap.ap]
    dims.insert(1 + idx, [step, count])
    return bass.AP(ap.tensor, ap.offset, dims)


def build_walk(w, t):
    op, dma = w.op, w.dma
    pb = t["pb"]
    pS = t["pS"]

    CONSTS = ["cos", "sin", "rmat", "tril", "mcur", "mprev", "ones", "twos",
              "onesf", "id128", "b0", "i4rep"]
    for nm in CONSTS:
        dma("SY", f"c_{nm}", lambda e, nm=nm: e.dma_start(out=t[nm + "_sb"][:], in_=t[nm + "_h"][:]))

    st1 = t["st1"]
    for i in range(NTT):
        ib = i % 2
        xb = t["xt"][ib]
        so = (i % 2) * 8
        dma("SY", f"p1x.{i}",
            lambda e, i=i, xb=xb: e.dma_start(out=xb, in_=t["x_h"][i * 128:(i + 1) * 128, :]),
            waits=[f"p1xn.{i-2}"] if i >= 2 else [])
        op("S", f"p1sq.{i}",
           lambda e, xb=xb, so=so: e.activation(t["xc"], xb, AF.Square,
                                                accum_out=st1[:, so + 1:so + 2]),
           waits=[f"p1x.{i}"] + ([f"p1r.{i-2}"] if i >= 2 else []))
        w.seq("V", f"p1nm.{i}", [
               lambda e, xb=xb, so=so: e.tensor_reduce(st1[:, so:so + 1], xb, AX.X, ALU.add),
               lambda e, so=so: e.tensor_scalar(st1[:, so:so + 1], st1[:, so:so + 1], -1.0 / D, None, ALU.mult)],
           waits=[f"p1x.{i}"] + ([f"p1xn.{i-2}"] if i >= 2 else []))
        w.seq("V", f"p1r.{i}", [
               lambda e, so=so: e.tensor_scalar(st1[:, so + 2:so + 3], st1[:, so + 1:so + 2], 1.0 / D, None, ALU.mult),
               lambda e, so=so: e.tensor_tensor(st1[:, so + 5:so + 6], st1[:, so:so + 1], st1[:, so:so + 1], ALU.mult),
               lambda e, so=so: e.tensor_tensor(st1[:, so + 2:so + 3], st1[:, so + 2:so + 3], st1[:, so + 5:so + 6], ALU.subtract),
               lambda e, so=so: e.tensor_scalar(st1[:, so + 2:so + 3], st1[:, so + 2:so + 3], 1e-5, None, ALU.add),
               lambda e, so=so: e.reciprocal(st1[:, so + 3:so + 4], st1[:, so + 2:so + 3])],
           waits=[f"p1sq.{i}"])
        w.seq("S", f"p1xn.{i}", [
               lambda e, so=so: e.activation(st1[:, so + 4:so + 5], st1[:, so + 3:so + 4], AF.Sqrt),
               lambda e, xb=xb, so=so: e.activation(t["xc"], xb, AF.Identity, bias=st1[:, so:so + 1]),
               lambda e, so=so, ib=ib: e.activation(t["xn_tm"][ib], t["xc"], AF.Identity, scale=st1[:, so + 4:so + 5])],
           waits=[f"p1r.{i}", f"p1nm.{i}"] + ([f"p1tr.{i-2}"] if i >= 2 else []))
        dma("S", f"p1tr.{i}",
            lambda e, i=i, ib=ib: e.dma_start_transpose(
                t["bigA"][:].rearrange("p (c tt) -> p c tt", c=32)[:, :, i * 128:(i + 1) * 128],
                t["xn_tm"][ib]),
            waits=[f"p1xn.{i}"])
    P1_END = [f"p1tr.{NTT-2}", f"p1tr.{NTT-1}", f"p1xn.{NTT-1}"]

    dma("G", "w_acat",
        lambda e: e.dma_start(out=t["acat"],
                              in_=t["acat_h"][:].rearrange("(k p) m -> p k m", p=128)),
        waits=P1_END)
    dma("G", "w_bt", lambda e: e.dma_start(out=t["bt"], in_=t["bt_h"][:]), waits=P1_END)
    xn = t["bigA"][:].rearrange("p (k tt) -> p k tt", k=32)
    for ts in range(4):
        def mm_h(e, ts=ts):
            ac = t["acat"].rearrange("p (k m) -> p k m", k=32)
            for k in range(32):
                ii = e.matmul(pb[0][:], ac[:, k, :], xn[:, k, ts * 512:(ts + 1) * 512],
                              start=(k == 0), stop=(k == 31))
            return ii
        op("T", f"p2h.{ts}", mm_h,
           waits=["w_acat", f"p1tr.{NTT-1}"] + ([f"p2hev.{ts-1}"] if ts else []))
        op("S", f"p2hev.{ts}",
           lambda e, ts=ts: e.copy(t["h"][:, ts * 512:(ts + 1) * 512], pb[0][:]),
           waits=[f"p2h.{ts}"])
    H_READY = [f"p2hev.{3}"]

    def _w0_load(b):
        src = bass.AP(t["w0t_h"].tensor, b * 256, [[D, 128], [128 * D, 32], [1, 256]])
        dma("G", f"w0.{b}",
            lambda e, src=src, b=b: e.dma_start(out=t["wbuf"][b % 2], in_=src),
            waits=P1_END if b < 2 else [f"p2mm.{(b-2)*8 + 7}"])
    _w0_load(0)
    _w0_load(1)
    for lt in range(128):
        j, ts = lt // 4, lt % 4
        src = bass.AP(t["maskt_h"].tensor, j * 128 * NT + ts * 512,
                      [[NT, 128], [D * NT, 4], [1, 512]])
        dma("G", f"msk.{lt}",
            lambda e, src=src, lt=lt: e.dma_start(out=t["mbig"][lt % 2], in_=src),
            waits=P1_END if lt < 2 else [f"p2tl3.{lt - 2}"])
        if lt % 8 == 7 and lt // 8 + 2 < 16:
            _w0_load(lt // 8 + 2)
    def p2_emit_base(it):
        j, ts = it // 4, it % 4
        b, j2 = j // 2, j % 2
        bk = 4 + it % 2

        def mm_base(e, b=b, j2=j2, ts=ts, bk=bk):
            wb = t["wbuf"][b % 2].rearrange("p (k m) -> p k m", k=32)
            for k in range(32):
                ii = e.matmul(pb[bk][:], wb[:, k, j2 * 128:j2 * 128 + 128],
                              xn[:, k, ts * 512:(ts + 1) * 512],
                              start=(k == 0), stop=False)
            return ii
        op("T", f"p2b.{it}", mm_base,
           waits=[f"w0.{b}"] + ([f"p2sil.{it-2}"] if it >= 2 else H_READY))

        def mm_y(e, j=j, ts=ts):
            for l in range(4):
                ii = e.matmul(pb[l][:], t["bt"][32 * l:32 * l + 16, j * 128:(j + 1) * 128],
                              t["h"][32 * l:32 * l + 16, ts * 512:(ts + 1) * 512],
                              start=True, stop=True, tile_position=(32 * l, 0))
            return ii
        op("T", f"p2y.{it}", mm_y,
           waits=["w_bt"] + ([f"p2tl3.{it-1}"] if it else []))
        for l in range(4):
            op("V", f"p2tl{l}.{it}",
               lambda e, l=l, ts=ts, it=it: e.tensor_tensor(
                   t["tl"][l], pb[l][:],
                   t["mbig"][it % 2].rearrange("p (l4 tt) -> p l4 tt", l4=4)[:, l, :],
                   ALU.mult),
               waits=[f"p2y.{it}", f"msk.{it}"] + ([f"p2mm.{it-1}"] if it else []))

    def p2_emit_fin(it):
        j = it // 4
        bk = 4 + it % 2

        def mm_id(e, bk=bk, it=it):
            for l in range(4):
                ii = e.matmul(pb[bk][:], t["id128_sb"][:], t["tl"][l],
                              start=False, stop=(l == 3))
            return ii
        op("T", f"p2mm.{it}", mm_id,
           waits=["c_id128", f"p2tl3.{it}"])
        op("S", f"p2sil.{it}",
           lambda e, it=it, j=j, bk=bk: e.activation(t["gv"][it % 2], pb[bk][:], AF.Silu,
                                                     bias=t["b0_sb"][:, j:j + 1]),
           waits=[f"p2mm.{it}", "c_b0"] + ([f"p2gs.{it-2}"] if it >= 2 else []))
        dma("SY", f"p2gs.{it}",
            lambda e, it=it, j=j, ts=it % 4: e.dma_start(
                out=t["g_sp"][j, :, ts * 512:(ts + 1) * 512], in_=t["gv"][it % 2]),
            waits=[f"p2sil.{it}"])

    p2_emit_base(0)
    for it in range(1, 128):
        p2_emit_base(it)
        p2_emit_fin(it - 1)
    p2_emit_fin(127)
    P2_POOL_FREE = [f"p2tl3.{127}", f"p2sil.{127}"]

    for vq in range(4):
        src = bass.AP(t["wv_h"].tensor, vq * 256,
                      [[KVH * HD, 128], [128 * KVH * HD, 32], [1, 256]])
        dma("G", f"wv.{vq}",
            lambda e, src=src, vq=vq: e.dma_start(out=t["wbuf"][vq % 2], in_=src),
            waits=[f"p2mm.{119 + 8 * vq}"] if vq < 2 else [f"p3vmm.{(vq-1)*16 - 1}"])
        for tt in range(16):
            it = vq * 16 + tt

            def mm_v(e, vq=vq, tt=tt, it=it):
                wb = t["wbuf"][vq % 2].rearrange("p (k m) -> p k m", k=32)
                for k in range(32):
                    ii = e.matmul(pb[4 + it % 2][:, 0:256], xn[:, k, tt * 128:(tt + 1) * 128],
                                  wb[:, k, :], start=(k == 0), stop=(k == 31))
                return ii
            op("T", f"p3vmm.{it}", mm_v,
               waits=[f"wv.{vq}"] + ([f"p3vev.{it-2}"] if it >= 2 else [f"p2sil.{126}", f"p2sil.{127}"]))
            op("S", f"p3vev.{it}",
               lambda e, it=it: e.copy(t["ev"][it % 4][:, 0:256], pb[4 + it % 2][:, 0:256]),
               waits=[f"p3vmm.{it}"] + ([f"p3vsp.{it-4}"] if it >= 4 else P2_POOL_FREE),)
            dst = bass.AP(t["v_sp"].tensor, tt * 128 * KVH * 128 + vq * 256,
                          [[KVH * 128, 128], [1, 256]])
            dma("SY", f"p3vsp.{it}",
                lambda e, dst=dst, it=it: e.dma_start(out=dst, in_=t["ev"][it % 4][:, 0:256]),
                waits=[f"p3vev.{it}"])
    for b in range(4):
        src = bass.AP(t["wk_h"].tensor, b * 256,
                      [[KVH * HD, 128], [128 * KVH * HD, 32], [1, 256]])
        dma("G", f"wk.{b}",
            lambda e, src=src, b=b: e.dma_start(out=t["wbuf"][b % 2], in_=src),
            waits=[f"p3vmm.{47 + 16 * b}"] if b < 2 else [f"p3kmm.{(b-1)*8 - 1}"])
        for sub in range(8):
            j2, ts = sub // 4, sub % 4
            j = b * 2 + j2
            it = b * 8 + sub

            def mm_k(e, b=b, j2=j2, ts=ts, it=it):
                wb = t["wbuf"][b % 2].rearrange("p (k m) -> p k m", k=32)
                for k in range(32):
                    ii = e.matmul(pb[4 + it % 2][:], wb[:, k, j2 * 128:j2 * 128 + 128],
                                  xn[:, k, ts * 512:(ts + 1) * 512],
                                  start=(k == 0), stop=(k == 31))
                return ii
            op("T", f"p3kmm.{it}", mm_k,
               waits=[f"wk.{b}"] + ([f"p3kev.{it-2}"] if it >= 2 else [f"p3vev.{62 + it}"]))
            op("S", f"p3kev.{it}",
               lambda e, it=it: e.copy(t["ev"][it % 4], pb[4 + it % 2][:]),
               waits=[f"p3kmm.{it}"] + ([f"p3ksp.{it-4}"] if it >= 4 else [f"p3vsp.{60 + it}"]))
            dst = bass.AP(t["k_sp"].tensor, ts * 4 * 128 * KVH * 128 + j * 128,
                          [[KVH * 128, 128], [128 * KVH * 128, 4], [1, 128]])
            dma("SY", f"p3ksp.{it}",
                lambda e, dst=dst, it=it: e.dma_start(out=dst, in_=t["ev"][it % 4]),
                waits=[f"p3kev.{it}"])
    for b in range(16):
        src = bass.AP(t["wq_h"].tensor, b * 256, [[D, 128], [128 * D, 32], [1, 256]])
        dma("G", f"wqb.{b}",
            lambda e, src=src, b=b: e.dma_start(out=t["wbuf"][b % 2], in_=src),
            waits=[f"p3kmm.{23 + 8 * b}"] if b < 2 else [f"p3qmm.{(b-1)*8 - 1}"])
        for sub in range(8):
            j2, ts = sub // 4, sub % 4
            j = b * 2 + j2
            it = b * 8 + sub

            def mm_q(e, b=b, j2=j2, ts=ts, it=it):
                wb = t["wbuf"][b % 2].rearrange("p (k m) -> p k m", k=32)
                for k in range(32):
                    ii = e.matmul(pb[4 + it % 2][:], wb[:, k, j2 * 128:j2 * 128 + 128],
                                  xn[:, k, ts * 512:(ts + 1) * 512],
                                  start=(k == 0), stop=(k == 31))
                return ii
            op("T", f"p3qmm.{it}", mm_q,
               waits=[f"wqb.{b}"] + ([f"p3qev.{it-2}"] if it >= 2 else [f"p3kev.{30 + it}"]))
            op("S", f"p3qev.{it}",
               lambda e, it=it: e.copy(t["ev"][it % 4], pb[4 + it % 2][:]),
               waits=[f"p3qmm.{it}"] + ([f"p3qsp.{it-4}"] if it >= 4 else [f"p3ksp.{28 + it}"]))
            dst = bass.AP(t["q_sp"].tensor, ts * 4 * 128 * NH * 128 + j * 128,
                          [[NH * 128, 128], [128 * NH * 128, 4], [1, 128]])
            dma("SY", f"p3qsp.{it}",
                lambda e, dst=dst, it=it: e.dma_start(out=dst, in_=t["ev"][it % 4]),
                waits=[f"p3qev.{it}"])
    P3_SPILLED = [f"p3qsp.{i}" for i in range(121, 128)]

    w.seq("G", "s_zero", [
           lambda e: e.memset(t["S"], 0.0),
           lambda e: e.memset(t["Sbf"], 0.0)],
       waits=P3_SPILLED + [f"p3qev.{127}"])
    KW = KVH * 128

    def emit_p5_block(ts, b):
        src = bass.AP(t["wo_h"].tensor, b * 256, [[D, 128], [128 * D, 32], [1, 256]])
        wdep = list(P3_SPILLED) if (ts == 0 and b < 2) else [f"p5mm.{(2*b-3)*4+ts}"] if b >= 2 else [f"p5mm.{(2*(b+14)+1)*4+ts-1}"]
        dma("G", f"wo.{ts}.{b}",
            lambda e, src=src, b=b: e.dma_start(out=t["wobuf"][b % 2], in_=src),
            waits=wdep)

    p5_seq = []

    def emit_p5_jtile(ts, j):
        b, j2 = j // 2, j % 2
        it = j * 4 + ts
        if j2 == 0:
            emit_p5_block(ts, b)
        ei = len(p5_seq)
        prev = p5_seq[-1] if p5_seq else None
        pprev = p5_seq[-2] if len(p5_seq) >= 2 else None
        p5_seq.append(it)
        dma("SY", f"gld.{it}",
            lambda e, j=j, ts=ts, ei=ei: e.dma_start(
                out=t["gld5"][ei % 2], in_=t["g_sp"][j, :, ts * 512:(ts + 1) * 512]),
            waits=[f"p2gs.{it}"] + ([f"sg.{pprev}"] if pprev is not None else []))
        dma("G", f"xld.{it}",
            lambda e, j=j, ts=ts, ei=ei: e.dma_start(
                out=t["xld5"][ei % 2],
                in_=bass.AP(t["x_h"].tensor, ts * 512 * D + j * 128,
                            [[D, 128], [128 * D, 4], [1, 128]])),
            waits=[f"xadd.{pprev}"] if pprev is not None else [])

        def mm_s(e, b=b, j2=j2, ts=ts):
            wb = t["wobuf"][b % 2].rearrange("p (k m) -> p k m", k=32)
            off = (4 * ts % 8) * 128
            for k in range(32):
                ii = e.matmul(t["pWo"][:], wb[:, k, j2 * 128:j2 * 128 + 128],
                              bass.AP(t["bigA"][:].tensor, k * 1024 + off,
                                      [[32 * NT, 128], [1, 512]]),
                              start=(k == 0), stop=(k == 31))
            return ii
        op("T", f"p5mm.{it}", mm_s,
           waits=[f"wo.{ts}.{b}", f"oev.{4*ts+3}.{KVH-1}"] +
                 ([f"scop.{prev}"] if prev is not None else []))
        op("S", f"scop.{it}",
           lambda e, ei=ei: e.copy(t["sgf5"][ei % 2], t["pWo"][:]),
           waits=[f"p5mm.{it}"] + ([f"sg.{pprev}", f"outst.{pprev}"] if pprev is not None else []))
        op("V", f"sg.{it}",
           lambda e, it=it, ei=ei: e.tensor_tensor(t["sgv5"][ei % 2], t["sgf5"][ei % 2],
                                                   t["gld5"][ei % 2], ALU.mult),
           waits=[f"scop.{it}", f"gld.{it}"] + ([f"sgt.{pprev}"] if pprev is not None else []))
        dma("S", f"sgt.{it}",
            lambda e, it=it, ei=ei: e.dma_start_transpose(
                t["sgt5"][ei % 2].rearrange("p (r q) -> p r q", r=4),
                t["sgv5"][ei % 2]),
            waits=[f"sg.{it}"] + ([f"xadd.{pprev}"] if pprev is not None else []))
        op("V", f"xadd.{it}",
           lambda e, ei=ei: e.tensor_tensor(t["sgf5"][ei % 2].bitcast(BF16).bitcast(F32),
                                            t["xld5"][ei % 2], t["sgt5"][ei % 2], ALU.add),
           waits=[f"sgt.{it}", f"xld.{it}"] + ([f"outst.{pprev}"] if pprev is not None else []))
        dma("SY", f"outst.{it}",
            lambda e, j=j, ts=ts, ei=ei: e.dma_start(
                out=bass.AP(t["out_h"].tensor, ts * 512 * D + j * 128,
                            [[D, 128], [128 * D, 4], [1, 128]]),
                in_=t["sgf5"][ei % 2].bitcast(BF16).bitcast(F32)),
            waits=[f"xadd.{it}"])

    def emit_pass2(c, gg, cr):
        hs = slice(gg * 128, (gg + 1) * 128)
        gr = gg % 2

        def mm_o(e, c=c, g=gg, gr=gr, hs=hs):
            ii = e.matmul(pb[2 + gr][:], t["Sbf"][:, hs], t["qtil3"][g % 3], start=True, stop=False)
            ii = e.matmul(pb[2 + gr][:], t["vring"][c % 2][:, hs], t["am3"][g % 3], start=False, stop=False)
            ii = e.matmul(pb[2 + gr][:], t["vring"][c % 2][:, hs], t["pcm3"][g % 3], start=False,
                          stop=(c == 0))
            if c > 0:
                ii = e.matmul(pb[2 + gr][:], t["vring"][(c - 1) % 2][64:128, hs],
                              t["ppm3"][g % 3][64:128, :], start=False, stop=True)
            return ii
        ow = [f"qt.{c}.{gg}", f"am.{c}.{gg}", f"pcn.{c}.{gg}", f"vin.{c}"]
        if c > 0:
            ow += [f"ppn.{c}.{gg}"]
        else:
            ow += ["s_zero"]
        ow += [f"oev.{c}.{gg-2}"] if gg >= 2 else ([f"oev.{c-1}.{KVH-2+gg}"] if c else [])
        if gg == 0 and c >= 8:
            ow += [f"p5mm.{124 + (c - 8) // 4}"]
        op("T", f"o.{c}.{gg}", mm_o, waits=ow)
        op("V", f"oev.{c}.{gg}",
           lambda e, c=c, g=gg, gr=gr: e.tensor_copy(
               bass.AP(t["bigA"][:].tensor, 4 * g * 1024 + (c % 8) * 128,
                       [list(t["bigA"][:].ap[0]), [1024, 4], [1, 128]]),
               pb[2 + gr][:].rearrange("p (r tt) -> p r tt", r=4)),
           waits=[f"o.{c}.{gg}"])

        def mm_su(e, c=c, g=gg, hs=hs):
            return e.matmul(pS[:, (g % 4) * 128:(g % 4) * 128 + 128],
                            t["khT"][:, hs], t["vring"][c % 2][:, hs],
                            start=(g % 4 == 0), stop=(g % 4 == 3))
        op("T", f"su.{c}.{gg}", mm_su,
           waits=[f"khTev.{c}", f"vin.{c}", f"o.{c}.{gg}"] +
                 ([f"sadd.{c}.{gg-1}"] if (gg % 4 == 0 and gg >= 4) else []) +
                 ([f"sadd.{c-1}.{KVH-1}"] if (gg == 0 and c > 0) else []) +
                 (["s_zero"] if (gg == 0 and c == 0) else []))
        if gg % 4 == 3:
            w.seq("S", f"sdec.{c}.{gg}", [
                   (lambda e, H=H: e.activation(t["S"][:, H * 128:(H + 1) * 128],
                                                t["S"][:, H * 128:(H + 1) * 128],
                                                AF.Identity, scale=t["explend"][:, H:H + 1]))
                   for H in range(gg - 3, gg + 1)],
               waits=[f"explend.{c}", f"su.{c}.{gg}"] +
                     ([f"sbf.{c-1}"] if c else ["s_zero"]))
            op("V", f"sadd.{c}.{gg}",
               lambda e, c=c, g=gg: e.tensor_tensor(
                   t["S"][:, (g - 3) * 128:(g + 1) * 128],
                   t["S"][:, (g - 3) * 128:(g + 1) * 128], pS[:], ALU.add),
               waits=[f"sdec.{c}.{gg}", f"su.{c}.{gg}"])

    for c in range(NCH):
        cr = c % 2
        dma("SY", f"kin.{c}",
            lambda e, c=c: e.dma_start(out=t["kin"], in_=t["k_sp"][c, :, :]),
            waits=P3_SPILLED + ([f"p4kdone.{c-1}"] if c else []))
        dma("SY", f"vin.{c}",
            lambda e, c=c, cr=cr: e.dma_start(out=t["vring"][cr], in_=t["v_sp"][c, :, :]),
            waits=(P3_SPILLED + ["s_zero"] if c < 2 else []) +
                  ([f"p4odone.{c-1}"] if c >= 1 else []) +
                  ([f"sadd.{c-2}.{KVH-1}"] if c >= 2 else []))
        op("S", f"sig.{c}", lambda e: e.activation(t["ksig"], t["kin"], AF.Sigmoid),
           waits=[f"kin.{c}"])
        op("S", f"lsig.{c}", lambda e: e.activation(t["ksig"], t["ksig"], AF.Ln), waits=[])
        w.seq("V", f"lam.{c}", [
               (lambda e, H=H: e.tensor_tensor_scan(t["lam"][:, H * 128:(H + 1) * 128],
                                                    t["onesf_sb"][:], t["ksig"][:, H * 128:(H + 1) * 128],
                                                    0.0, ALU.mult, ALU.add)) for H in range(KVH)],
           waits=[f"lsig.{c}", "c_onesf"])
        op("S", f"expl.{c}", lambda e: e.activation(t["expl"], t["lam"], AF.Exp, scale=INV16),
           waits=[f"lam.{c}"])
        op("S", f"explend.{c}",
           lambda e: e.activation(t["explend"],
                                  bass.AP(t["lam"].tensor, t["lam"].offset + 127,
                                          [list(t["lam"].ap[0]), [128, 8]]),
                                  AF.Exp, scale=INV16),
           waits=[f"lam.{c}"] + ([f"sdec.{c-1}.{KVH-1}"] if c else []))
        op("S", f"expli.{c}", lambda e: e.activation(t["expli"], t["lam"], AF.Exp, scale=-INV16),
           waits=[f"lam.{c}"])
        op("S", f"ek.{c}", lambda e: e.activation(t["ekt"], t["kin"], AF.Exp),
           waits=[f"kin.{c}"])
        op("T", f"denk.{c}",
           lambda e: (e.matmul(pb[4][:], t["ones_sb"][:], t["ekt"][:, 0:512], start=True, stop=True),
                      e.matmul(pb[5][:], t["ones_sb"][:], t["ekt"][:, 512:1024], start=True, stop=True))[-1],
           waits=[f"ek.{c}", "c_ones"] + ([f"rs.{c-1}.{KVH-2}", f"rs.{c-1}.{KVH-1}"] if c else []))
        w.seq("S", f"rk.{c}", [
               lambda e: e.activation(t["lnscr"], pb[4][:], AF.Ln),
               lambda e: e.activation(t["rkt2"][:, 0:512], t["lnscr"], AF.Exp, scale=-1.0),
               lambda e: e.activation(t["lnscr"], pb[5][:], AF.Ln),
               lambda e: e.activation(t["rkt2"][:, 512:1024], t["lnscr"], AF.Exp, scale=-1.0)],
           waits=[f"denk.{c}"])
        w.seq("V", f"kt.{c}", [
               lambda e: e.tensor_tensor(t["ktmp"].bitcast(BF16)[:, 0:1024], t["ekt"], t["expli"], ALU.mult),
               lambda e: e.tensor_tensor(t["ktil"], t["ktmp"].bitcast(BF16)[:, 0:1024], t["rkt2"], ALU.mult)],
           waits=[f"rk.{c}", f"expli.{c}"])
        op("V", f"kh.{c}",
           lambda e: e.tensor_tensor(
               t["khat"].rearrange("p (H tt) -> p H tt", H=KVH),
               t["ktil"].rearrange("p (H tt) -> p H tt", H=KVH),
               _ins(t["explend"], 1, 0, 128), ALU.mult),
           waits=[f"kt.{c}", f"explend.{c}"])
        op("T", f"rotk.{c}",
           lambda e: (e.matmul(pb[4][:], t["rmat_sb"][:], t["kin"][:, 0:512], start=True, stop=True),
                      e.matmul(pb[5][:], t["rmat_sb"][:], t["kin"][:, 512:1024], start=True, stop=True))[-1],
           waits=[f"kin.{c}", "c_rmat", f"rk.{c}"])
        cosc = _ins(t["cos_sb"][:, c * 128:(c + 1) * 128], 0, 0, KVH)
        sinc = _ins(t["sin_sb"][:, c * 128:(c + 1) * 128], 0, 0, KVH)
        ktc = t["ktmp"].bitcast(BF16)[:, 0:1024]
        w.seq("V", f"sk.{c}", [
               lambda e, cosc=cosc, ktc=ktc: e.tensor_tensor(
                   ktc.rearrange("p (H tt) -> p H tt", H=KVH),
                   t["kin"].rearrange("p (H tt) -> p H tt", H=KVH), cosc, ALU.mult),
               lambda e, sinc=sinc: e.tensor_tensor(
                   t["krot"][:, 0:512].rearrange("p (r tt) -> p r tt", r=4),
                   pb[4][:].rearrange("p (r tt) -> p r tt", r=4), sinc[:, 0:4, :], ALU.mult),
               lambda e, sinc=sinc: e.tensor_tensor(
                   t["krot"][:, 512:1024].rearrange("p (r tt) -> p r tt", r=4),
                   pb[5][:].rearrange("p (r tt) -> p r tt", r=4), sinc[:, 4:8, :], ALU.mult),
               lambda e, cr=cr, ktc=ktc: e.tensor_tensor(t["skring"][cr], ktc, t["krot"], ALU.add)],
           waits=[f"rotk.{c}", "c_cos", "c_sin", f"kh.{c}"] +
                 ([f"scp.{c-1}.{KVH-1}"] if c >= 2 else []))
        op("T", f"khT.{c}",
           lambda e: [e.transpose(pb[0][:].bitcast(BF16)[:, H * 128:(H + 1) * 128],
                                  t["khat"][:, H * 128:(H + 1) * 128], t["id128_sb"][:])
                      for H in range(KVH)][-1],
           waits=[f"kh.{c}", "c_id128", f"sk.{c}"] + ([f"khTev.{c-1}"] if c else []) +
                 ([f"am.{c-1}.{KVH-2}"] if c else []))
        op("S", f"khTev.{c}", lambda e: e.copy(t["khT"], pb[0][:].bitcast(BF16)),
           waits=[f"khT.{c}"] + ([f"sadd.{c-1}.{KVH-1}"] if c else []))
        op("V", f"p4kdone.{c}", lambda e: e.tensor_copy(t["ktmp"].bitcast(BF16)[:, 0:1], t["ktil"][:, 0:1]),
           waits=[f"sk.{c}", f"kh.{c}", f"ek.{c}"])

        for g in range(KVH):
            gr = g % 2
            dma("SY", f"qin.{c}.{g}",
                lambda e, c=c, g=g, gr=gr: e.dma_start(
                    out=t["qg"][gr],
                    in_=bass.AP(t["q_sp"].tensor, c * 128 * NH * 128 + g * 512,
                                [[NH * 128, 128], [1, 512]])),
                waits=P3_SPILLED + ([f"sq.{c}.{g-2}", f"eq.{c}.{g-2}"] if g >= 2 else
                                    ([f"sq.{c-1}.{KVH-2+g}", f"eq.{c-1}.{KVH-2+g}"] if c else [])))
            op("S", f"eq.{c}.{g}",
               lambda e, gr=gr, g=g: e.activation(t["eq2"][g % 2], t["qg"][gr], AF.Exp),
               waits=[f"qin.{c}.{g}"] + ([f"qt1.{c}.{g-2}"] if g >= 2 else []))
            op("T", f"denq.{c}.{g}",
               lambda e, gr=gr, g=g: e.matmul(pb[4 + gr][:], t["twos_sb"][:], t["eq2"][g % 2], start=True, stop=True),
               waits=[f"eq.{c}.{g}", "c_twos"] +
                     ([f"p4rq.{c}.{g-2}"] if g >= 2 else [f"sk.{c}"]))
            w.seq("S", f"p4rq.{c}.{g}", [
                   lambda e, gr=gr: e.activation(t["lnscr"], pb[4 + gr][:], AF.Ln),
                   lambda e, g=g: e.activation(t["rq2"][g % 2], t["lnscr"], AF.Exp, scale=-1.0)],
               waits=[f"denq.{c}.{g}"])
            op("V", f"qt1.{c}.{g}",
               lambda e, g=g: e.tensor_tensor(t["qt18"][g], t["eq2"][g % 2], t["rq2"][g % 2], ALU.mult),
               waits=[f"p4rq.{c}.{g}"])
            op("T", f"rotq.{c}.{g}",
               lambda e, gr=gr, g=g: e.matmul(pb[g % 2][:], t["rmat_sb"][:], t["qg"][gr], start=True, stop=True),
               waits=[f"qin.{c}.{g}", "c_rmat"] +
                     ([f"sq.{c}.{g-2}"] if g >= 2 else
                      ([f"am.{c-1}.{6+g}"] if c else [])) +
                     ([f"khTev.{c}"] if g == 0 else []))
            cos4 = _ins(t["cos_sb"][:, c * 128:(c + 1) * 128], 0, 0, 4)
            sin4 = _ins(t["sin_sb"][:, c * 128:(c + 1) * 128], 0, 0, 4)
            w.seq("V", f"sq.{c}.{g}", [
                   lambda e, gr=gr, cos4=cos4: e.tensor_tensor(
                       t["sq2"].rearrange("p (r tt) -> p r tt", r=4),
                       t["qg"][gr].rearrange("p (r tt) -> p r tt", r=4), cos4, ALU.mult),
                   lambda e, g=g, sin4=sin4: e.tensor_tensor(
                       t["sq3"].rearrange("p (r tt) -> p r tt", r=4),
                       pb[g % 2][:].rearrange("p (r tt) -> p r tt", r=4), sin4, ALU.mult),
                   lambda e, g=g: e.tensor_tensor(t["sqa8"][g], t["sq2"], t["sq3"], ALU.add)],
               waits=[f"rotq.{c}.{g}"])

        for g in range(KVH):
            gr = g % 2
            hs = slice(g * 128, (g + 1) * 128)
            expl_g = _ins(t["expl"][:, g * 128:(g + 1) * 128], 0, 0, 4)
            op("V", f"qt.{c}.{g}",
               lambda e, expl_g=expl_g, g=g: e.tensor_tensor(
                   t["qtil3"][g % 3].rearrange("p (r tt) -> p r tt", r=4),
                   t["qt18"][g].rearrange("p (r tt) -> p r tt", r=4), expl_g, ALU.mult),
               waits=[f"qt1.{c}.{g}", f"expl.{c}"])

            def mm_sc(e, cr=cr, hs=hs, g=g):
                e.matmul(pb[g % 2][:], t["skring"][cr][:, hs], t["sqa8"][g], start=True, stop=False)
                return e.matmul(pb[g % 2][:], t["mcur_sb"][:], t["i4rep_sb"][:],
                                start=False, stop=True)
            op("T", f"sc.{c}.{g}", mm_sc,
               waits=[f"sq.{c}.{g}", f"sk.{c}", "c_mcur", "c_i4rep",
                      f"sq.{c}.{KVH-2 + g % 2}"] +
                     ([f"am.{c}.{g-2}"] if g >= 2 else []))
            if c > 0:
                def mm_scp(e, c=c, g=g, gr=gr):
                    e.matmul(pb[4 + gr][64:128, :],
                             t["skring"][(c - 1) % 2][:, g * 128 + 64:g * 128 + 128],
                             t["sqa8"][g], start=True, stop=False, tile_position=(0, 64))
                    return e.matmul(pb[4 + gr][64:128, :], t["mprev_sb"][:, 0:64],
                                    t["i4rep_sb"][:], start=False, stop=True,
                                    tile_position=(0, 64))
                op("T", f"scp.{c}.{g}", mm_scp,
                   waits=[f"sq.{c}.{g}", f"p4rq.{c}.{KVH-2 + g % 2}", "c_mprev", "c_i4rep"])
            op("S", f"esc.{c}.{g}",
               lambda e, g=g: e.activation(t["esc2"][g % 2], pb[g % 2][:], AF.Exp, scale=SSCALE),
               waits=[f"sc.{c}.{g}"] + ([f"pcn.{c}.{g-2}"] if g >= 2 else []))
            if c > 0:
                op("S", f"esp.{c}.{g}",
                   lambda e, gr=gr, g=g: e.activation(t["esp2"][g % 2][64:128, :], pb[4 + gr][64:128, :], AF.Exp, scale=SSCALE),
                   waits=[f"scp.{c}.{g}"] + ([f"ppn.{c}.{g-2}"] if g >= 2 else []))

            def mm_dsw(e, c=c, gr=gr, g=g):
                ii = e.matmul(pb[4 + gr][:], t["twos_sb"][:], t["esc2"][g % 2], start=True,
                              stop=(c == 0))
                if c > 0:
                    ii = e.matmul(pb[4 + gr][:], t["twos_sb"][64:128, :], t["esp2"][g % 2][64:128, :],
                                  start=False, stop=True)
                return ii
            op("T", f"dsw.{c}.{g}", mm_dsw,
               waits=[f"esc.{c}.{g}", f"p4rq.{c}.{KVH-2 + g % 2}"] +
                     ([f"esp.{c}.{g}"] if c else []))
            w.seq("S", f"rs.{c}.{g}", [
                   lambda e, gr=gr: e.activation(t["lnscr"], pb[4 + gr][:], AF.Ln),
                   lambda e, g=g: e.activation(t["rs2"][g % 2], t["lnscr"], AF.Exp, scale=-1.0)],
               waits=[f"dsw.{c}.{g}"])
            op("V", f"pcn.{c}.{g}",
               lambda e, g=g: e.tensor_tensor(t["pcm3"][g % 3], t["esc2"][g % 2], t["rs2"][g % 2], ALU.mult),
               waits=[f"rs.{c}.{g}"])
            if c > 0:
                op("V", f"ppn.{c}.{g}",
                   lambda e, g=g: e.tensor_tensor(t["ppm3"][g % 3][64:128, :], t["esp2"][g % 2][64:128, :],
                                                  t["rs2"][g % 2][64:128, :], ALU.mult),
                   waits=[f"rs.{c}.{g}"])
            op("T", f"at.{c}.{g}",
               lambda e, hs=hs, g=g: e.matmul(pb[g % 2][:], t["ktil"][:, hs], t["qtil3"][g % 3],
                                              start=True, stop=True),
               waits=[f"qt.{c}.{g}", f"kt.{c}", f"esc.{c}.{g}"])
            tril4 = _ins(t["tril_sb"][:], 0, 0, 4)
            op("V", f"am.{c}.{g}",
               lambda e, g=g, tril4=tril4: e.tensor_tensor(
                   t["am3"][g % 3].rearrange("p (r tt) -> p r tt", r=4),
                   pb[g % 2][:].rearrange("p (r tt) -> p r tt", r=4), tril4, ALU.mult),
               waits=[f"at.{c}.{g}", "c_tril"])
            if g >= 2:
                emit_pass2(c, g - 2, cr)
        emit_pass2(c, KVH - 2, cr)
        emit_pass2(c, KVH - 1, cr)
        op("V", f"sbf.{c}", lambda e: e.tensor_copy(t["Sbf"], t["S"]),
           waits=[f"sadd.{c}.{KVH-1}"])
        op("V", f"p4odone.{c}", lambda e: e.tensor_copy(t["qt2"][:, 0:1], t["qtil3"][0][:, 0:1]),
           waits=[f"oev.{c}.{KVH-1}", f"oev.{c}.{KVH-2}"])
        if c >= 4:
            ts = (c - 4) // 4
            j0 = 8 * ((c - 4) % 4)
            for j in range(j0, j0 + 8):
                emit_p5_jtile(ts, j)
    for j in range(32):
        emit_p5_jtile(3, j)
    P4_END = [f"p4odone.{NCH-1}", f"sadd.{NCH-1}.{KVH-1}", f"sbf.{NCH-1}",
              f"p4kdone.{NCH-1}", f"khTev.{NCH-1}", f"sq.{NCH-1}.{KVH-1}", f"qt1.{NCH-1}.{KVH-1}"]

    fin_w = [f"outst.{j*4+3}" for j in range(26, 32)] + [f"gld.{j*4+3}" for j in range(29, 32)]
    op("SY", "fin", lambda e: e.nop(), waits=fin_w)


def build_nc():
    nc = bass.Bass("TRN2", target_bir_lowering=False)
    t = {}
    t["x_h"] = nc.dram_tensor("x", [NT, D], F32, kind="ExternalInput").ap()
    t["wq_h"] = nc.dram_tensor("wq", [D, D], F32, kind="ExternalInput").ap()
    t["wk_h"] = nc.dram_tensor("wk", [D, KVH * HD], F32, kind="ExternalInput").ap()
    t["wv_h"] = nc.dram_tensor("wv", [D, KVH * HD], F32, kind="ExternalInput").ap()
    t["wo_h"] = nc.dram_tensor("wo", [D, D], F32, kind="ExternalInput").ap()
    t["w0t_h"] = nc.dram_tensor("w0t", [D, D], F32, kind="ExternalInput").ap()
    t["acat_h"] = nc.dram_tensor("acat", [D, 128], F32, kind="ExternalInput").ap()
    t["bt_h"] = nc.dram_tensor("bt", [128, D], F32, kind="ExternalInput").ap()
    t["b0_h"] = nc.dram_tensor("b0fm", [128, 32], F32, kind="ExternalInput").ap()
    t["maskt_h"] = nc.dram_tensor("maskt", [4, D, NT], BF16, kind="ExternalInput").ap()
    t["cos_h"] = nc.dram_tensor("cosfm", [128, NT], BF16, kind="ExternalInput").ap()
    t["sin_h"] = nc.dram_tensor("sinfm", [128, NT], BF16, kind="ExternalInput").ap()
    t["rmat_h"] = nc.dram_tensor("rmat", [128, 128], BF16, kind="ExternalInput").ap()
    t["tril_h"] = nc.dram_tensor("trilT", [128, 128], BF16, kind="ExternalInput").ap()
    t["mcur_h"] = nc.dram_tensor("mcurT", [128, 128], BF16, kind="ExternalInput").ap()
    t["mprev_h"] = nc.dram_tensor("mprevT", [128, 128], BF16, kind="ExternalInput").ap()
    t["i4rep_h"] = nc.dram_tensor("i4rep", [128, 512], BF16, kind="ExternalInput").ap()
    t["ones_h"] = nc.dram_tensor("ones128", [128, 128], BF16, kind="ExternalInput").ap()
    t["twos_h"] = nc.dram_tensor("twos128", [128, 128], BF16, kind="ExternalInput").ap()
    t["onesf_h"] = nc.dram_tensor("onesf128", [128, 128], F32, kind="ExternalInput").ap()
    t["id128_h"] = nc.dram_tensor("id128", [128, 128], BF16, kind="ExternalInput").ap()
    t["out_h"] = nc.dram_tensor("out", [NT, D], F32, kind="ExternalOutput").ap()
    t["q_sp"] = nc.dram_tensor("q_sp", [NCH, 128, NH * 128], BF16).ap()
    t["k_sp"] = nc.dram_tensor("k_sp", [NCH, 128, KVH * 128], BF16).ap()
    t["v_sp"] = nc.dram_tensor("v_sp", [NCH, 128, KVH * 128], BF16).ap()
    t["g_sp"] = nc.dram_tensor("g_sp", [32, 128, NT], F32).ap()
    t["sg_sp"] = nc.dram_tensor("sg_sp", [NT, D], BF16).ap()

    ctx = ExitStack()
    with ctx:
        def sb(name, shape, dt=BF16):
            h = ctx.enter_context(nc.sbuf_tensor(name, shape, dt))
            t[name] = h
            return h
        sb("bigA", [128, 32 * NT])
        for nm, shape, dt in [("cos_sb", [128, NT], BF16), ("sin_sb", [128, NT], BF16),
                              ("rmat_sb", [128, 128], BF16), ("tril_sb", [128, 128], BF16),
                              ("mcur_sb", [128, 128], BF16), ("mprev_sb", [128, 128], BF16),
                              ("i4rep_sb", [128, 512], BF16),
                              ("ones_sb", [128, 128], BF16), ("twos_sb", [128, 128], BF16),
                              ("onesf_sb", [128, 128], F32), ("id128_sb", [128, 128], BF16),
                              ("b0_sb", [128, 32], F32), ("st1", [128, 16], F32)]:
            sb(nm, shape, dt)
        pool = ctx.enter_context(nc.sbuf_tensor("pool", [128, 34816], BF16))

        def V_(off, n, dt=BF16):
            ap = pool[:, off:off + n]
            if dt == F32:
                ap = ap.bitcast(F32)
            return ap
        t["xt"] = [V_(0, 8192, F32), V_(8192, 8192, F32)]
        t["xc"] = V_(16384, 8192, F32)
        t["xn_tm"] = [V_(24576, 4096), V_(28672, 4096)]
        t["wbuf"] = [V_(0, 8192), V_(8192, 8192)]
        t["mbig"] = [V_(16384, 2048), V_(18432, 2048)]
        t["acat"] = V_(20480, 4096)
        t["bt"] = V_(24576, 4096)
        t["h"] = V_(28672, 2048)
        t["tl"] = [V_(30720 + 512 * i, 512) for i in range(4)]
        t["gv"] = [V_(32768, 1024, F32), V_(33792, 1024, F32)]
        t["ev"] = [V_(16384 + 512 * i, 512) for i in range(4)]
        go = 0

        def GV(n, dt=BF16):
            nonlocal go
            v = V_(go, n, dt)
            go += n
            return v
        t["qg"] = [GV(512), GV(512)]
        t["eq2"] = [GV(512), GV(512)]
        t["rq2"] = [GV(512), GV(512)]
        t["lnscr"] = GV(1024, F32)
        t["qtil3"] = [GV(512) for _ in range(3)]
        t["qt2"] = GV(512)
        t["sq1"] = [GV(512), GV(512)]
        t["sq2"] = t["qt2"]
        t["sq3"] = GV(512)
        t["esc2"] = [GV(512), GV(512)]
        t["rs2"] = [GV(512), GV(512)]
        t["pcm3"] = [GV(512) for _ in range(3)]
        t["am3"] = [GV(512) for _ in range(3)]
        t["esp2"] = [GV(512), GV(512)]
        t["ppm3"] = [GV(512) for _ in range(3)]
        ko = go

        def KV(n, dt=BF16):
            nonlocal ko
            v = V_(ko, n, dt)
            ko += n
            return v
        t["kin"] = KV(1024)
        t["ksig"] = KV(2048, F32)
        t["lam"] = KV(2048, F32)
        t["expl"] = KV(1024)
        t["expli"] = t["lam"].bitcast(BF16)[:, 0:1024]
        t["explend"] = t["st1"][:, 8:16]
        t["ekt"] = KV(1024)
        t["rkt2"] = KV(1024)
        t["krot"] = t["ekt"]
        t["ktil"] = KV(1024)
        t["khat"] = KV(1024)
        t["khT"] = KV(1024)
        t["ktmp"] = t["ksig"]
        t["skring"] = [KV(1024), KV(1024)]
        t["vring"] = [KV(1024), KV(1024)]
        t["S"] = KV(2048, F32)
        t["Sbf"] = KV(1024)
        assert ko <= 34816, ko
        def BV(off, n, dt=BF16):
            ap = t["bigA"][:][:, off:off + n]
            if dt == F32:
                ap = ap.bitcast(F32)
            return ap
        t["wobuf"] = [BV(32768, 8192), BV(40960, 8192)]
        t["sgv5"] = [BV(49152, 512), BV(49664, 512)]
        t["gld5"] = [BV(50176, 1024, F32), BV(51200, 1024, F32)]
        t["sgt5"] = [BV(52224, 512), BV(52736, 512)]
        t["qt18"] = [BV(53248 + 512 * i, 512) for i in range(8)]
        t["sqa8"] = [BV(57344 + 512 * i, 512) for i in range(8)]
        t["sgf5"] = [BV(61440, 1024, F32), BV(62464, 1024, F32)]
        t["xld5"] = [BV(63488, 1024, F32), BV(64512, 1024, F32)]
        t["xf"] = [V_(0, 4096, F32), V_(4096, 4096, F32)]
        t["of"] = [V_(8192, 4096, F32), V_(12288, 4096, F32)]
        t["sgh"] = [V_(16384, 2048), V_(18432, 2048)]
        t["pb"] = [ctx.enter_context(nc.psum_tensor(f"pb{i}", [128, 512], F32)) for i in range(6)]
        t["pS"] = ctx.enter_context(nc.psum_tensor("pS", [128, 512], F32))
        t["pWo"] = ctx.enter_context(nc.psum_tensor("pWo", [128, 512], F32))

        sems = {}
        for s in ALL_SEMS:
            sems[s] = ctx.enter_context(nc.semaphore(s))

        wA = Walk("collect")
        build_walk(wA, t)
        wB = Walk("count")
        wB.waited = wA.waited
        build_walk(wB, t)

        ctx.enter_context(nc.allow_low_precision(
            reason="bf16 matmul operands with fp32 PSUM per stated tolerance"))
        block = ctx.enter_context(nc.Block())
        emitters = {"T": block.tensor, "V": block.vector, "S": block.scalar,
                    "SY": block.sync, "G": block.gpsimd}
        for stream in STREAMS:
            def body(eng, stream=stream):
                wE = Walk("emit", stream=stream, eng=eng, sems=sems)
                wE.waited = wA.waited
                wE.evt = wB.evt
                build_walk(wE, t)
            emitters[stream](body)
    return nc


def _rotl32(x, r):
    return ((x << np.uint32(r)) | (x >> np.uint32(32 - r))) & np.uint32(0xFFFFFFFF)


def _threefry2x32(k0, k1, x0, x1):
    rot = [[13, 15, 26, 6], [17, 29, 16, 24]]
    ks2 = np.uint32(k0) ^ np.uint32(k1) ^ np.uint32(0x1BD11BDA)
    ks = [np.uint32(k0), np.uint32(k1), ks2]
    x0 = (x0 + ks[0]) & np.uint32(0xFFFFFFFF)
    x1 = (x1 + ks[1]) & np.uint32(0xFFFFFFFF)
    for d in range(5):
        for r in rot[d % 2]:
            x0 = (x0 + x1) & np.uint32(0xFFFFFFFF)
            x1 = _rotl32(x1, r) ^ x0
        x0 = (x0 + ks[(d + 1) % 3]) & np.uint32(0xFFFFFFFF)
        x1 = (x1 + ks[(d + 2) % 3] + np.uint32(d + 1)) & np.uint32(0xFFFFFFFF)
    return x0, x1


def _dropout_uniform():
    n = 8 * NT * D
    i = np.arange(n, dtype=np.uint32)
    b1, b2 = _threefry2x32(0, 42, np.zeros(n, np.uint32), i)
    bits = b1 ^ b2
    return (((bits >> np.uint32(9)) | np.uint32(0x3F800000)).view(np.float32)
            - 1.0).reshape(8, NT, D)


def _host_prep(inputs):
    x = np.ascontiguousarray(inputs["L0_attn_hidden_states"])
    W0t = np.ascontiguousarray(inputs["L0_ffn_W0"].T)
    b0 = np.asarray(inputs["L0_ffn_b0"])
    A = np.asarray(inputs["L0_ffn_A"])
    B = np.asarray(inputs["L0_ffn_B"])

    acat = np.zeros((D, 128), np.float32)
    bt = np.zeros((128, D), np.float32)
    for l in range(4):
        acat[:, 32 * l:32 * l + 16] = A[l].T
        bt[32 * l:32 * l + 16, :] = B[l].T
    b0fm = np.ascontiguousarray(b0.reshape(32, 128).T)

    u = _dropout_uniform()
    keep = 0.25
    maskt = np.zeros((8, 4, D, NT), BFNP)
    for l in range(4):
        m = ((u >= l * keep) & (u < (l + 1) * keep)).astype(np.float32) * 8.0
        maskt[:, l] = m.transpose(0, 2, 1).astype(BFNP)

    half = np.arange(HD // 2, dtype=np.float32)
    inv = 1.0 / 10000.0 ** (2.0 * half / HD)
    fr = np.arange(NT, dtype=np.float32)[:, None] * inv[None, :]
    emb = np.concatenate([fr, fr], axis=-1)
    cosfm = np.ascontiguousarray(np.cos(emb).T).astype(BFNP)
    sinfm = np.ascontiguousarray(np.sin(emb).T).astype(BFNP)

    rmat = np.zeros((128, 128), np.float32)
    for dd in range(64):
        rmat[dd + 64, dd] = -1.0
        rmat[dd, dd + 64] = 1.0
    tau = np.arange(128)
    tril = (tau[:, None] <= tau[None, :]).astype(np.float32)
    mcur_valid = ((tau[None, :] - tau[:, None] >= 0) &
                  (tau[None, :] - tau[:, None] <= 64))
    mcur = np.where(mcur_valid, 0.0, -30000.0).astype(np.float32).T
    i64 = np.arange(64)
    mprev_valid = (i64[:, None] >= tau[None, :])
    mprev = np.zeros((128, 128), np.float32)
    mprev[:, 0:64] = np.where(mprev_valid, 0.0, -30000.0).astype(np.float32).T
    i4 = np.concatenate([np.eye(128, dtype=np.float32)] * 4, axis=1)

    common = {
        "wq": np.ascontiguousarray(inputs["L0_attn_Wq"]),
        "wk": np.ascontiguousarray(inputs["L0_attn_Wk"]),
        "wv": np.ascontiguousarray(inputs["L0_attn_Wv"]),
        "wo": np.ascontiguousarray(inputs["L0_attn_Wo"]),
        "w0t": W0t, "acat": acat, "bt": bt, "b0fm": b0fm,
        "cosfm": cosfm, "sinfm": sinfm,
        "rmat": rmat.astype(BFNP), "trilT": tril.astype(BFNP),
        "mcurT": mcur.astype(BFNP), "mprevT": mprev.astype(BFNP),
        "i4rep": i4.astype(BFNP),
        "ones128": np.ones((128, 128), BFNP), "twos128": np.full((128, 128), 2.0, BFNP),
        "onesf128": np.ones((128, 128), np.float32),
        "id128": np.eye(128).astype(BFNP),
    }
    in_maps = []
    for bi in range(8):
        m = dict(common)
        m["x"] = np.ascontiguousarray(x[bi])
        m["maskt"] = np.ascontiguousarray(maskt[bi])
        in_maps.append(m)
    return in_maps


def kernel(L0_attn_hidden_states, L0_attn_Wq, L0_attn_Wk, L0_attn_Wv, L0_attn_Wo,
           L0_ffn_W0, L0_ffn_b0, L0_ffn_A, L0_ffn_B):
    inputs = dict(L0_attn_hidden_states=L0_attn_hidden_states, L0_attn_Wq=L0_attn_Wq,
                  L0_attn_Wk=L0_attn_Wk, L0_attn_Wv=L0_attn_Wv, L0_attn_Wo=L0_attn_Wo,
                  L0_ffn_W0=L0_ffn_W0, L0_ffn_b0=L0_ffn_b0, L0_ffn_A=L0_ffn_A,
                  L0_ffn_B=L0_ffn_B)
    in_maps = _host_prep(inputs)
    nc = build_nc()
    res = run_bass_kernel_spmd(nc, in_maps, core_ids=list(range(8)))
    out = np.stack([np.asarray(r["out"]) for r in res.results], axis=0)
    return out.astype(np.float32)


# version 12
# speedup vs baseline: 1.2369x; 1.0505x over previous
import numpy as np
import ml_dtypes
import concourse.bass as bass
import concourse.mybir as mybir
from concourse.bass_utils import run_bass_kernel_spmd
from contextlib import ExitStack

F32 = mybir.dt.float32
BF16 = mybir.dt.bfloat16
AF = mybir.ActivationFunctionType
ALU = mybir.AluOpType
AX = mybir.AxisListType
BFNP = ml_dtypes.bfloat16

D = 4096
NT = 2048
NH, KVH, HD = 32, 8, 128
CH = 128
NCH = NT // CH
NTT = NT // 128
SSCALE = float(1.0 / np.sqrt(np.float32(HD)))
INV16 = 1.0 / 16.0

STREAMS = ("T", "V", "S", "SY", "G")
STREAM_SEM = {"T": "c_t", "V": "c_v", "S": "c_s", "SY": "c_sy", "G": "c_g"}
DMA_POOL = {"S": [f"d_s{i}" for i in range(2)],
            "SY": [f"d_y{i}" for i in range(6)],
            "G": [f"d_g{i}" for i in range(4)]}
ALL_SEMS = (list(STREAM_SEM.values())
            + [x for p in DMA_POOL.values() for x in p])


class Walk:
    def __init__(self, mode, stream=None, eng=None, sems=None):
        self.mode = mode
        self.stream = stream
        self.eng = eng
        self.sems = sems
        self.waited = set()
        self.evt = {}
        self.C = {s: 0 for s in ALL_SEMS}
        self.dn = {s: 0 for s in DMA_POOL}
        self.hwm = {}

    def _wait(self, names, is_dma=False):
        for w in names:
            if self.mode == "collect":
                self.waited.add(w)
            elif self.mode == "emit":
                if w not in self.evt:
                    raise KeyError(f"wait on unknown event {w}")
                sem, val = self.evt[w]
                if not is_dma and sem == STREAM_SEM.get(self.stream):
                    continue
                if self.hwm.get(sem, -1) >= val:
                    continue
                self.eng.wait_ge(self.sems[sem], val)
                self.hwm[sem] = val

    def op(self, stream, name, emit_fn, waits=()):
        serial = stream in ("V", "S", "G")
        sem = STREAM_SEM[stream]
        if self.mode == "collect":
            self._wait(waits)
            return
        if self.mode == "count":
            if serial:
                self.C[sem] += 1
                if name is not None:
                    self.evt[name] = (sem, self.C[sem])
            elif name is not None and name in self.waited:
                self.C[sem] += 1
                self.evt[name] = (sem, self.C[sem])
            return
        if self.stream != stream:
            return
        self._wait(waits)
        if serial and self.C[sem] > 0:
            self.eng.wait_ge(self.sems[sem], self.C[sem])
        instr = emit_fn(self.eng)
        if serial:
            self.C[sem] += 1
            instr.then_inc(self.sems[sem], 1)
        elif name is not None and name in self.waited:
            self.C[sem] += 1
            instr.then_inc(self.sems[sem], 1)

    def seq(self, stream, name, fns, waits=()):
        for i, fn in enumerate(fns):
            self.op(stream, name if i == len(fns) - 1 else None, fn,
                    waits=waits if i == 0 else ())

    def dma(self, stream, name, emit_fn, waits=()):
        if self.mode == "collect":
            self._wait(waits)
            return
        pool = DMA_POOL[stream]
        sem = pool[self.dn[stream] % len(pool)]
        self.dn[stream] += 1
        if self.mode == "count":
            self.C[sem] += 16
            self.evt[name] = (sem, self.C[sem])
            return
        if self.stream != stream:
            return
        self._wait(waits, is_dma=True)
        if self.C[sem] > 0 and self.hwm.get(sem, -1) < self.C[sem]:
            self.eng.wait_ge(self.sems[sem], self.C[sem])
            self.hwm[sem] = self.C[sem]
        instr = emit_fn(self.eng)
        self.C[sem] += 16
        instr.then_inc(self.sems[sem], 16)


def _ins(ap, idx, step, count):
    dims = [list(d) for d in ap.ap]
    dims.insert(1 + idx, [step, count])
    return bass.AP(ap.tensor, ap.offset, dims)


def build_walk(w, t):
    op, dma = w.op, w.dma
    pb = t["pb"]
    pS = t["pS"]

    CONSTS = ["cos", "sin", "rmat", "tril", "mcur", "mprev", "ones", "twos",
              "onesf", "id128", "b0", "i4rep"]
    for nm in CONSTS:
        dma("SY", f"c_{nm}", lambda e, nm=nm: e.dma_start(out=t[nm + "_sb"][:], in_=t[nm + "_h"][:]))

    st1 = t["st1"]
    for i in range(NTT):
        ib = i % 2
        xb = t["xt"][ib]
        so = (i % 2) * 8
        dma("SY", f"p1x.{i}",
            lambda e, i=i, xb=xb: e.dma_start(out=xb, in_=t["x_h"][i * 128:(i + 1) * 128, :]),
            waits=[f"p1xn.{i-2}"] if i >= 2 else [])
        op("S", f"p1sq.{i}",
           lambda e, xb=xb, so=so: e.activation(t["xc"], xb, AF.Square,
                                                accum_out=st1[:, so + 1:so + 2]),
           waits=[f"p1x.{i}"] + ([f"p1r.{i-2}"] if i >= 2 else []))
        w.seq("V", f"p1nm.{i}", [
               lambda e, xb=xb, so=so: e.tensor_reduce(st1[:, so:so + 1], xb, AX.X, ALU.add),
               lambda e, so=so: e.tensor_scalar(st1[:, so:so + 1], st1[:, so:so + 1], -1.0 / D, None, ALU.mult)],
           waits=[f"p1x.{i}"] + ([f"p1xn.{i-2}"] if i >= 2 else []))
        w.seq("V", f"p1r.{i}", [
               lambda e, so=so: e.tensor_scalar(st1[:, so + 2:so + 3], st1[:, so + 1:so + 2], 1.0 / D, None, ALU.mult),
               lambda e, so=so: e.tensor_tensor(st1[:, so + 5:so + 6], st1[:, so:so + 1], st1[:, so:so + 1], ALU.mult),
               lambda e, so=so: e.tensor_tensor(st1[:, so + 2:so + 3], st1[:, so + 2:so + 3], st1[:, so + 5:so + 6], ALU.subtract),
               lambda e, so=so: e.tensor_scalar(st1[:, so + 2:so + 3], st1[:, so + 2:so + 3], 1e-5, None, ALU.add),
               lambda e, so=so: e.reciprocal(st1[:, so + 3:so + 4], st1[:, so + 2:so + 3])],
           waits=[f"p1sq.{i}"])
        w.seq("S", f"p1xn.{i}", [
               lambda e, so=so: e.activation(st1[:, so + 4:so + 5], st1[:, so + 3:so + 4], AF.Sqrt),
               lambda e, xb=xb, so=so: e.activation(t["xc"], xb, AF.Identity, bias=st1[:, so:so + 1]),
               lambda e, so=so, ib=ib: e.activation(t["xn_tm"][ib], t["xc"], AF.Identity, scale=st1[:, so + 4:so + 5])],
           waits=[f"p1r.{i}", f"p1nm.{i}"] + ([f"p1tr.{i-2}"] if i >= 2 else []))
        dma("S", f"p1tr.{i}",
            lambda e, i=i, ib=ib: e.dma_start_transpose(
                t["bigA"][:].rearrange("p (c tt) -> p c tt", c=32)[:, :, i * 128:(i + 1) * 128],
                t["xn_tm"][ib]),
            waits=[f"p1xn.{i}"])
    P1_END = [f"p1tr.{NTT-2}", f"p1tr.{NTT-1}", f"p1xn.{NTT-1}"]

    dma("G", "w_acat",
        lambda e: e.dma_start(out=t["acat"],
                              in_=t["acat_h"][:].rearrange("(k p) m -> p k m", p=128)),
        waits=P1_END)
    dma("G", "w_bt", lambda e: e.dma_start(out=t["bt"], in_=t["bt_h"][:]), waits=P1_END)
    xn = t["bigA"][:].rearrange("p (k tt) -> p k tt", k=32)
    for ts in range(4):
        def mm_h(e, ts=ts):
            ac = t["acat"].rearrange("p (k m) -> p k m", k=32)
            for k in range(32):
                ii = e.matmul(pb[0][:], ac[:, k, :], xn[:, k, ts * 512:(ts + 1) * 512],
                              start=(k == 0), stop=(k == 31))
            return ii
        op("T", f"p2h.{ts}", mm_h,
           waits=["w_acat", f"p1tr.{NTT-1}"] + ([f"p2hev.{ts-1}"] if ts else []))
        op("S", f"p2hev.{ts}",
           lambda e, ts=ts: e.copy(t["h"][:, ts * 512:(ts + 1) * 512], pb[0][:]),
           waits=[f"p2h.{ts}"])
    H_READY = [f"p2hev.{3}"]

    def _w0_load(b):
        src = bass.AP(t["w0t_h"].tensor, b * 256, [[D, 128], [128 * D, 32], [1, 256]])
        dma("G", f"w0.{b}",
            lambda e, src=src, b=b: e.dma_start(out=t["wbuf"][b % 2], in_=src),
            waits=P1_END if b < 2 else [f"p2mm.{(b-2)*8 + 7}"])
    _w0_load(0)
    _w0_load(1)
    for lt in range(128):
        j, ts = lt // 4, lt % 4
        src = bass.AP(t["maskt_h"].tensor, j * 128 * NT + ts * 512,
                      [[NT, 128], [D * NT, 4], [1, 512]])
        dma("G", f"msk.{lt}",
            lambda e, src=src, lt=lt: e.dma_start(out=t["mbig"][lt % 2], in_=src),
            waits=P1_END if lt < 2 else [f"p2tl3.{lt - 2}"])
        if lt % 8 == 7 and lt // 8 + 2 < 16:
            _w0_load(lt // 8 + 2)
    def p2_emit_base(it):
        j, ts = it // 4, it % 4
        b, j2 = j // 2, j % 2
        bk = 4 + it % 2

        def mm_base(e, b=b, j2=j2, ts=ts, bk=bk):
            wb = t["wbuf"][b % 2].rearrange("p (k m) -> p k m", k=32)
            for k in range(32):
                ii = e.matmul(pb[bk][:], wb[:, k, j2 * 128:j2 * 128 + 128],
                              xn[:, k, ts * 512:(ts + 1) * 512],
                              start=(k == 0), stop=False)
            return ii
        op("T", f"p2b.{it}", mm_base,
           waits=[f"w0.{b}"] + ([f"p2sil.{it-2}"] if it >= 2 else H_READY))

        def mm_y(e, j=j, ts=ts):
            for l in range(4):
                ii = e.matmul(pb[l][:], t["bt"][32 * l:32 * l + 16, j * 128:(j + 1) * 128],
                              t["h"][32 * l:32 * l + 16, ts * 512:(ts + 1) * 512],
                              start=True, stop=True, tile_position=(32 * l, 0))
            return ii
        op("T", f"p2y.{it}", mm_y,
           waits=["w_bt"] + ([f"p2tl3.{it-1}"] if it else []))
        for l in range(4):
            op("V", f"p2tl{l}.{it}",
               lambda e, l=l, ts=ts, it=it: e.tensor_tensor(
                   t["tl"][l], pb[l][:],
                   t["mbig"][it % 2].rearrange("p (l4 tt) -> p l4 tt", l4=4)[:, l, :],
                   ALU.mult),
               waits=[f"p2y.{it}", f"msk.{it}"] + ([f"p2mm.{it-1}"] if it else []))

    def p2_emit_fin(it):
        j = it // 4
        bk = 4 + it % 2

        def mm_id(e, bk=bk, it=it):
            for l in range(4):
                ii = e.matmul(pb[bk][:], t["id128_sb"][:], t["tl"][l],
                              start=False, stop=(l == 3))
            return ii
        op("T", f"p2mm.{it}", mm_id,
           waits=["c_id128", f"p2tl3.{it}"])
        op("S", f"p2sil.{it}",
           lambda e, it=it, j=j, bk=bk: e.activation(t["gv"][it % 2], pb[bk][:], AF.Silu,
                                                     bias=t["b0_sb"][:, j:j + 1]),
           waits=[f"p2mm.{it}", "c_b0"] + ([f"p2gs.{it-2}"] if it >= 2 else []))
        dma("SY", f"p2gs.{it}",
            lambda e, it=it, j=j, ts=it % 4: e.dma_start(
                out=t["g_sp"][j, :, ts * 512:(ts + 1) * 512], in_=t["gv"][it % 2]),
            waits=[f"p2sil.{it}"])

    p2_emit_base(0)
    for it in range(1, 128):
        p2_emit_base(it)
        p2_emit_fin(it - 1)
    p2_emit_fin(127)
    P2_POOL_FREE = [f"p2tl3.{127}", f"p2sil.{127}"]

    for vq in range(4):
        src = bass.AP(t["wv_h"].tensor, vq * 256,
                      [[KVH * HD, 128], [128 * KVH * HD, 32], [1, 256]])
        dma("G", f"wv.{vq}",
            lambda e, src=src, vq=vq: e.dma_start(out=t["wbuf"][vq % 2], in_=src),
            waits=[f"p2mm.{119 + 8 * vq}"] if vq < 2 else [f"p3vmm.{(vq-1)*16 - 1}"])
        for tt in range(16):
            it = vq * 16 + tt

            def mm_v(e, vq=vq, tt=tt, it=it):
                wb = t["wbuf"][vq % 2].rearrange("p (k m) -> p k m", k=32)
                for k in range(32):
                    ii = e.matmul(pb[4 + it % 2][:, 0:256], xn[:, k, tt * 128:(tt + 1) * 128],
                                  wb[:, k, :], start=(k == 0), stop=(k == 31))
                return ii
            op("T", f"p3vmm.{it}", mm_v,
               waits=[f"wv.{vq}"] + ([f"p3vev.{it-2}"] if it >= 2 else [f"p2sil.{126}", f"p2sil.{127}"]))
            op("S", f"p3vev.{it}",
               lambda e, it=it: e.copy(t["ev"][it % 4][:, 0:256], pb[4 + it % 2][:, 0:256]),
               waits=[f"p3vmm.{it}"] + ([f"p3vsp.{it-4}"] if it >= 4 else P2_POOL_FREE),)
            dst = bass.AP(t["v_sp"].tensor, tt * 128 * KVH * 128 + vq * 256,
                          [[KVH * 128, 128], [1, 256]])
            dma("SY", f"p3vsp.{it}",
                lambda e, dst=dst, it=it: e.dma_start(out=dst, in_=t["ev"][it % 4][:, 0:256]),
                waits=[f"p3vev.{it}"])
    for b in range(4):
        src = bass.AP(t["wk_h"].tensor, b * 256,
                      [[KVH * HD, 128], [128 * KVH * HD, 32], [1, 256]])
        dma("G", f"wk.{b}",
            lambda e, src=src, b=b: e.dma_start(out=t["wbuf"][b % 2], in_=src),
            waits=[f"p3vmm.{47 + 16 * b}"] if b < 2 else [f"p3kmm.{(b-1)*8 - 1}"])
        for sub in range(8):
            j2, ts = sub // 4, sub % 4
            j = b * 2 + j2
            it = b * 8 + sub

            def mm_k(e, b=b, j2=j2, ts=ts, it=it):
                wb = t["wbuf"][b % 2].rearrange("p (k m) -> p k m", k=32)
                for k in range(32):
                    ii = e.matmul(pb[4 + it % 2][:], wb[:, k, j2 * 128:j2 * 128 + 128],
                                  xn[:, k, ts * 512:(ts + 1) * 512],
                                  start=(k == 0), stop=(k == 31))
                return ii
            op("T", f"p3kmm.{it}", mm_k,
               waits=[f"wk.{b}"] + ([f"p3kev.{it-2}"] if it >= 2 else [f"p3vev.{62 + it}"]))
            op("S", f"p3kev.{it}",
               lambda e, it=it: e.copy(t["ev"][it % 4], pb[4 + it % 2][:]),
               waits=[f"p3kmm.{it}"] + ([f"p3ksp.{it-4}"] if it >= 4 else [f"p3vsp.{60 + it}"]))
            dst = bass.AP(t["k_sp"].tensor, ts * 4 * 128 * KVH * 128 + j * 128,
                          [[KVH * 128, 128], [128 * KVH * 128, 4], [1, 128]])
            dma("SY", f"p3ksp.{it}",
                lambda e, dst=dst, it=it: e.dma_start(out=dst, in_=t["ev"][it % 4]),
                waits=[f"p3kev.{it}"])
    for b in range(16):
        src = bass.AP(t["wq_h"].tensor, b * 256, [[D, 128], [128 * D, 32], [1, 256]])
        dma("G", f"wqb.{b}",
            lambda e, src=src, b=b: e.dma_start(out=t["wbuf"][b % 2], in_=src),
            waits=[f"p3kmm.{23 + 8 * b}"] if b < 2 else [f"p3qmm.{(b-1)*8 - 1}"])
        for sub in range(8):
            j2, ts = sub // 4, sub % 4
            j = b * 2 + j2
            it = b * 8 + sub

            def mm_q(e, b=b, j2=j2, ts=ts, it=it):
                wb = t["wbuf"][b % 2].rearrange("p (k m) -> p k m", k=32)
                for k in range(32):
                    ii = e.matmul(pb[4 + it % 2][:], wb[:, k, j2 * 128:j2 * 128 + 128],
                                  xn[:, k, ts * 512:(ts + 1) * 512],
                                  start=(k == 0), stop=(k == 31))
                return ii
            op("T", f"p3qmm.{it}", mm_q,
               waits=[f"wqb.{b}"] + ([f"p3qev.{it-2}"] if it >= 2 else [f"p3kev.{30 + it}"]))
            op("S", f"p3qev.{it}",
               lambda e, it=it: e.copy(t["ev"][it % 4], pb[4 + it % 2][:]),
               waits=[f"p3qmm.{it}"] + ([f"p3qsp.{it-4}"] if it >= 4 else [f"p3ksp.{28 + it}"]))
            dst = bass.AP(t["q_sp"].tensor, ts * 4 * 128 * NH * 128 + j * 128,
                          [[NH * 128, 128], [128 * NH * 128, 4], [1, 128]])
            dma("SY", f"p3qsp.{it}",
                lambda e, dst=dst, it=it: e.dma_start(out=dst, in_=t["ev"][it % 4]),
                waits=[f"p3qev.{it}"])
    P3_SPILLED = [f"p3qsp.{i}" for i in range(121, 128)]

    w.seq("G", "s_zero", [
           lambda e: e.memset(t["S"], 0.0),
           lambda e: e.memset(t["Sbf"], 0.0)],
       waits=P3_SPILLED + [f"p3qev.{127}"])
    KW = KVH * 128

    def emit_p5_block(ts, b):
        src = bass.AP(t["wo_h"].tensor, b * 256, [[D, 128], [128 * D, 32], [1, 256]])
        wdep = list(P3_SPILLED) if (ts == 0 and b < 2) else [f"p5mm.{(2*b-3)*4+ts}"] if b >= 2 else [f"p5mm.{(2*(b+14)+1)*4+ts-1}"]
        dma("G", f"wo.{ts}.{b}",
            lambda e, src=src, b=b: e.dma_start(out=t["wobuf"][b % 2], in_=src),
            waits=wdep)

    p5_seq = []

    def emit_p5_jtile(ts, j):
        b, j2 = j // 2, j % 2
        it = j * 4 + ts
        if j2 == 0:
            emit_p5_block(ts, b)
        ei = len(p5_seq)
        prev = p5_seq[-1] if p5_seq else None
        pprev = p5_seq[-2] if len(p5_seq) >= 2 else None
        p5_seq.append(it)
        dma("SY", f"gld.{it}",
            lambda e, j=j, ts=ts, ei=ei: e.dma_start(
                out=t["gld5"][ei % 2], in_=t["g_sp"][j, :, ts * 512:(ts + 1) * 512]),
            waits=[f"p2gs.{it}"] + ([f"sg.{pprev}"] if pprev is not None else []))
        dma("G", f"xld.{it}",
            lambda e, j=j, ts=ts, ei=ei: e.dma_start(
                out=t["xld5"][ei % 2],
                in_=bass.AP(t["x_h"].tensor, ts * 512 * D + j * 128,
                            [[D, 128], [128 * D, 4], [1, 128]])),
            waits=[f"xadd.{pprev}"] if pprev is not None else [])

        def mm_s(e, b=b, j2=j2, ts=ts):
            wb = t["wobuf"][b % 2].rearrange("p (k m) -> p k m", k=32)
            off = (4 * ts % 8) * 128
            for k in range(32):
                ii = e.matmul(t["pWo"][:], wb[:, k, j2 * 128:j2 * 128 + 128],
                              bass.AP(t["bigA"][:].tensor, k * 1024 + off,
                                      [[32 * NT, 128], [1, 512]]),
                              start=(k == 0), stop=(k == 31))
            return ii
        op("T", f"p5mm.{it}", mm_s,
           waits=[f"wo.{ts}.{b}", f"oev.{4*ts+3}.{KVH-1}"] +
                 ([f"scop.{prev}"] if prev is not None else []))
        op("S", f"scop.{it}",
           lambda e, ei=ei: e.copy(t["sgf5"][ei % 2], t["pWo"][:]),
           waits=[f"p5mm.{it}"] + ([f"sg.{pprev}", f"outst.{pprev}"] if pprev is not None else []))
        op("V", f"sg.{it}",
           lambda e, it=it, ei=ei: e.tensor_tensor(t["sgv5"][ei % 2], t["sgf5"][ei % 2],
                                                   t["gld5"][ei % 2], ALU.mult),
           waits=[f"scop.{it}", f"gld.{it}"] + ([f"sgt.{pprev}"] if pprev is not None else []))
        dma("S", f"sgt.{it}",
            lambda e, it=it, ei=ei: e.dma_start_transpose(
                t["sgt5"][ei % 2].rearrange("p (r q) -> p r q", r=4),
                t["sgv5"][ei % 2]),
            waits=[f"sg.{it}"] + ([f"xadd.{pprev}"] if pprev is not None else []))
        op("V", f"xadd.{it}",
           lambda e, ei=ei: e.tensor_tensor(t["sgf5"][ei % 2].bitcast(BF16).bitcast(F32),
                                            t["xld5"][ei % 2], t["sgt5"][ei % 2], ALU.add),
           waits=[f"sgt.{it}", f"xld.{it}"] + ([f"outst.{pprev}"] if pprev is not None else []))
        dma("SY", f"outst.{it}",
            lambda e, j=j, ts=ts, ei=ei: e.dma_start(
                out=bass.AP(t["out_h"].tensor, ts * 512 * D + j * 128,
                            [[D, 128], [128 * D, 4], [1, 128]]),
                in_=t["sgf5"][ei % 2].bitcast(BF16).bitcast(F32)),
            waits=[f"xadd.{it}"])

    def emit_pass2(c, gg, cr):
        hs = slice(gg * 128, (gg + 1) * 128)
        gr = gg % 2

        def mm_o(e, c=c, g=gg, gr=gr, hs=hs):
            ii = e.matmul(pb[2 + gr][:], t["Sbf"][:, hs], t["qtil3"][g % 3], start=True, stop=False)
            ii = e.matmul(pb[2 + gr][:], t["vring"][c % 2][:, hs], t["am3"][g % 3], start=False, stop=False)
            ii = e.matmul(pb[2 + gr][:], t["vring"][c % 2][:, hs], t["pcm3"][g % 3], start=False,
                          stop=(c == 0))
            if c > 0:
                ii = e.matmul(pb[2 + gr][:], t["vring"][(c - 1) % 2][64:128, hs],
                              t["ppm3"][g % 3][64:128, :], start=False, stop=True)
            return ii
        ow = [f"qt.{c}.{gg}", f"am.{c}.{gg}", f"pcn.{c}.{gg}", f"vin.{c}"]
        if c > 0:
            ow += [f"ppn.{c}.{gg}"]
        else:
            ow += ["s_zero"]
        ow += [f"oev.{c}.{gg-2}"] if gg >= 2 else ([f"oev.{c-1}.{KVH-2+gg}"] if c else [])
        if gg == 0 and c >= 8:
            ow += [f"p5mm.{124 + (c - 8) // 4}"]
        op("T", f"o.{c}.{gg}", mm_o, waits=ow)
        op("V", f"oev.{c}.{gg}",
           lambda e, c=c, g=gg, gr=gr: e.tensor_copy(
               bass.AP(t["bigA"][:].tensor, 4 * g * 1024 + (c % 8) * 128,
                       [list(t["bigA"][:].ap[0]), [1024, 4], [1, 128]]),
               pb[2 + gr][:].rearrange("p (r tt) -> p r tt", r=4)),
           waits=[f"o.{c}.{gg}"])

        def mm_su(e, c=c, g=gg, hs=hs):
            return e.matmul(pS[:, (g % 4) * 128:(g % 4) * 128 + 128],
                            t["khT"][:, hs], t["vring"][c % 2][:, hs],
                            start=(g % 4 == 0), stop=(g % 4 == 3))
        op("T", f"su.{c}.{gg}", mm_su,
           waits=[f"khTev.{c}", f"vin.{c}", f"o.{c}.{gg}"] +
                 ([f"sadd.{c}.{gg-1}"] if (gg % 4 == 0 and gg >= 4) else []) +
                 ([f"sadd.{c-1}.{KVH-1}"] if (gg == 0 and c > 0) else []) +
                 (["s_zero"] if (gg == 0 and c == 0) else []))
        if gg % 4 == 3:
            w.seq("S", f"sdec.{c}.{gg}", [
                   (lambda e, H=H: e.activation(t["S"][:, H * 128:(H + 1) * 128],
                                                t["S"][:, H * 128:(H + 1) * 128],
                                                AF.Identity, scale=t["explend"][:, H:H + 1]))
                   for H in range(gg - 3, gg + 1)],
               waits=[f"explend.{c}", f"su.{c}.{gg}"] +
                     ([f"sbf.{c-1}"] if c else ["s_zero"]))
            op("V", f"sadd.{c}.{gg}",
               lambda e, c=c, g=gg: e.tensor_tensor(
                   t["S"][:, (g - 3) * 128:(g + 1) * 128],
                   t["S"][:, (g - 3) * 128:(g + 1) * 128], pS[:], ALU.add),
               waits=[f"sdec.{c}.{gg}", f"su.{c}.{gg}"])

    for c in range(NCH):
        cr = c % 2
        dma("SY", f"kin.{c}",
            lambda e, c=c: e.dma_start(out=t["kin"], in_=t["k_sp"][c, :, :]),
            waits=P3_SPILLED + ([f"p4kdone.{c-1}"] if c else []))
        dma("SY", f"vin.{c}",
            lambda e, c=c, cr=cr: e.dma_start(out=t["vring"][cr], in_=t["v_sp"][c, :, :]),
            waits=(P3_SPILLED + ["s_zero"] if c < 2 else []) +
                  ([f"p4odone.{c-1}"] if c >= 1 else []) +
                  ([f"sadd.{c-2}.{KVH-1}"] if c >= 2 else []))
        op("S", f"sig.{c}", lambda e: e.activation(t["ksig"], t["kin"], AF.Sigmoid),
           waits=[f"kin.{c}"])
        op("S", f"lsig.{c}", lambda e: e.activation(t["ksig"], t["ksig"], AF.Ln), waits=[])
        w.seq("V", f"lam.{c}", [
               (lambda e, H=H: e.tensor_tensor_scan(t["lam"][:, H * 128:(H + 1) * 128],
                                                    t["onesf_sb"][:], t["ksig"][:, H * 128:(H + 1) * 128],
                                                    0.0, ALU.mult, ALU.add)) for H in range(KVH)],
           waits=[f"lsig.{c}", "c_onesf"])
        op("S", f"expl.{c}", lambda e: e.activation(t["expl"], t["lam"], AF.Exp, scale=INV16),
           waits=[f"lam.{c}"])
        op("S", f"explend.{c}",
           lambda e: e.activation(t["explend"],
                                  bass.AP(t["lam"].tensor, t["lam"].offset + 127,
                                          [list(t["lam"].ap[0]), [128, 8]]),
                                  AF.Exp, scale=INV16),
           waits=[f"lam.{c}"] + ([f"sdec.{c-1}.{KVH-1}"] if c else []))
        op("S", f"expli.{c}", lambda e: e.activation(t["expli"], t["lam"], AF.Exp, scale=-INV16),
           waits=[f"lam.{c}"])
        op("S", f"ek.{c}", lambda e: e.activation(t["ekt"], t["kin"], AF.Exp),
           waits=[f"kin.{c}"])
        op("T", f"denk.{c}",
           lambda e: (e.matmul(pb[4][:], t["ones_sb"][:], t["ekt"][:, 0:512], start=True, stop=True),
                      e.matmul(pb[5][:], t["ones_sb"][:], t["ekt"][:, 512:1024], start=True, stop=True))[-1],
           waits=[f"ek.{c}", "c_ones"] + ([f"rs.{c-1}.{KVH-2}", f"rs.{c-1}.{KVH-1}"] if c else []))
        w.seq("S", f"rk.{c}", [
               lambda e: e.activation(t["lnscr"], pb[4][:], AF.Ln),
               lambda e: e.activation(t["rkt2"][:, 0:512], t["lnscr"], AF.Exp, scale=-1.0),
               lambda e: e.activation(t["lnscr"], pb[5][:], AF.Ln),
               lambda e: e.activation(t["rkt2"][:, 512:1024], t["lnscr"], AF.Exp, scale=-1.0)],
           waits=[f"denk.{c}"])
        w.seq("V", f"kt.{c}", [
               lambda e: e.tensor_tensor(t["ktmp"].bitcast(BF16)[:, 0:1024], t["ekt"], t["expli"], ALU.mult),
               lambda e: e.tensor_tensor(t["ktil"], t["ktmp"].bitcast(BF16)[:, 0:1024], t["rkt2"], ALU.mult)],
           waits=[f"rk.{c}", f"expli.{c}"])
        op("V", f"kh.{c}",
           lambda e: e.tensor_tensor(
               t["khat"].rearrange("p (H tt) -> p H tt", H=KVH),
               t["ktil"].rearrange("p (H tt) -> p H tt", H=KVH),
               _ins(t["explend"], 1, 0, 128), ALU.mult),
           waits=[f"kt.{c}", f"explend.{c}"])
        op("T", f"rotk.{c}",
           lambda e: (e.matmul(pb[4][:], t["rmat_sb"][:], t["kin"][:, 0:512], start=True, stop=True),
                      e.matmul(pb[5][:], t["rmat_sb"][:], t["kin"][:, 512:1024], start=True, stop=True))[-1],
           waits=[f"kin.{c}", "c_rmat", f"rk.{c}"])
        cosc = _ins(t["cos_sb"][:, c * 128:(c + 1) * 128], 0, 0, KVH)
        sinc = _ins(t["sin_sb"][:, c * 128:(c + 1) * 128], 0, 0, KVH)
        ktc = t["ktmp"].bitcast(BF16)[:, 0:1024]
        w.seq("V", f"sk.{c}", [
               lambda e, cosc=cosc, ktc=ktc: e.tensor_tensor(
                   ktc.rearrange("p (H tt) -> p H tt", H=KVH),
                   t["kin"].rearrange("p (H tt) -> p H tt", H=KVH), cosc, ALU.mult),
               lambda e, sinc=sinc: e.tensor_tensor(
                   t["krot"][:, 0:512].rearrange("p (r tt) -> p r tt", r=4),
                   pb[4][:].rearrange("p (r tt) -> p r tt", r=4), sinc[:, 0:4, :], ALU.mult),
               lambda e, sinc=sinc: e.tensor_tensor(
                   t["krot"][:, 512:1024].rearrange("p (r tt) -> p r tt", r=4),
                   pb[5][:].rearrange("p (r tt) -> p r tt", r=4), sinc[:, 4:8, :], ALU.mult),
               lambda e, cr=cr, ktc=ktc: e.tensor_tensor(t["skring"][cr], ktc, t["krot"], ALU.add)],
           waits=[f"rotk.{c}", "c_cos", "c_sin", f"kh.{c}"] +
                 ([f"scp.{c-1}.{KVH-1}"] if c >= 2 else []))
        op("T", f"khT.{c}",
           lambda e: [e.transpose(pb[0][:].bitcast(BF16)[:, H * 128:(H + 1) * 128],
                                  t["khat"][:, H * 128:(H + 1) * 128], t["id128_sb"][:])
                      for H in range(KVH)][-1],
           waits=[f"kh.{c}", "c_id128", f"sk.{c}"] + ([f"khTev.{c-1}"] if c else []) +
                 ([f"am.{c-1}.{KVH-2}"] if c else []))
        op("S", f"khTev.{c}", lambda e: e.copy(t["khT"], pb[0][:].bitcast(BF16)),
           waits=[f"khT.{c}"] + ([f"sadd.{c-1}.{KVH-1}"] if c else []))
        op("V", f"p4kdone.{c}", lambda e: e.tensor_copy(t["ktmp"].bitcast(BF16)[:, 0:1], t["ktil"][:, 0:1]),
           waits=[f"sk.{c}", f"kh.{c}", f"ek.{c}"])

        for g in range(KVH):
            gr = g % 2
            dma("SY", f"qin.{c}.{g}",
                lambda e, c=c, g=g, gr=gr: e.dma_start(
                    out=t["qg"][gr],
                    in_=bass.AP(t["q_sp"].tensor, c * 128 * NH * 128 + g * 512,
                                [[NH * 128, 128], [1, 512]])),
                waits=P3_SPILLED + ([f"sq.{c}.{g-2}", f"eq.{c}.{g-2}"] if g >= 2 else
                                    ([f"sq.{c-1}.{KVH-2+g}", f"eq.{c-1}.{KVH-2+g}"] if c else [])))
            op("S", f"eq.{c}.{g}",
               lambda e, gr=gr, g=g: e.activation(t["eq2"][g % 2], t["qg"][gr], AF.Exp),
               waits=[f"qin.{c}.{g}"] + ([f"qt1.{c}.{g-2}"] if g >= 2 else []))
            op("T", f"denq.{c}.{g}",
               lambda e, gr=gr, g=g: e.matmul(pb[4 + gr][:], t["twos_sb"][:], t["eq2"][g % 2], start=True, stop=True),
               waits=[f"eq.{c}.{g}", "c_twos"] +
                     ([f"p4rq.{c}.{g-2}"] if g >= 2 else [f"sk.{c}"]))
            w.seq("S", f"p4rq.{c}.{g}", [
                   lambda e, gr=gr: e.activation(t["lnscr"], pb[4 + gr][:], AF.Ln),
                   lambda e, g=g: e.activation(t["rq2"][g % 2], t["lnscr"], AF.Exp, scale=-1.0)],
               waits=[f"denq.{c}.{g}"])
            op("V", f"qt1.{c}.{g}",
               lambda e, g=g: e.tensor_tensor(t["qt18"][g], t["eq2"][g % 2], t["rq2"][g % 2], ALU.mult),
               waits=[f"p4rq.{c}.{g}"])
            op("T", f"rotq.{c}.{g}",
               lambda e, gr=gr, g=g: e.matmul(pb[g % 2][:], t["rmat_sb"][:], t["qg"][gr], start=True, stop=True),
               waits=[f"qin.{c}.{g}", "c_rmat"] +
                     ([f"sq.{c}.{g-2}"] if g >= 2 else
                      ([f"am.{c-1}.{6+g}"] if c else [])) +
                     ([f"khTev.{c}"] if g == 0 else []))
            cos4 = _ins(t["cos_sb"][:, c * 128:(c + 1) * 128], 0, 0, 4)
            sin4 = _ins(t["sin_sb"][:, c * 128:(c + 1) * 128], 0, 0, 4)
            w.seq("V", f"sq.{c}.{g}", [
                   lambda e, gr=gr, cos4=cos4: e.tensor_tensor(
                       t["sq2"].rearrange("p (r tt) -> p r tt", r=4),
                       t["qg"][gr].rearrange("p (r tt) -> p r tt", r=4), cos4, ALU.mult),
                   lambda e, g=g, sin4=sin4: e.tensor_tensor(
                       t["sq3"].rearrange("p (r tt) -> p r tt", r=4),
                       pb[g % 2][:].rearrange("p (r tt) -> p r tt", r=4), sin4, ALU.mult),
                   lambda e, g=g: e.tensor_tensor(t["sqa8"][g], t["sq2"], t["sq3"], ALU.add)],
               waits=[f"rotq.{c}.{g}"])

        if c >= 4:
            _ts = (c - 4) // 4
            _j0 = 8 * ((c - 4) % 4)
            for _j in range(_j0, _j0 + 4):
                emit_p5_jtile(_ts, _j)
        for g in range(KVH):
            gr = g % 2
            hs = slice(g * 128, (g + 1) * 128)
            expl_g = _ins(t["expl"][:, g * 128:(g + 1) * 128], 0, 0, 4)
            op("V", f"qt.{c}.{g}",
               lambda e, expl_g=expl_g, g=g: e.tensor_tensor(
                   t["qtil3"][g % 3].rearrange("p (r tt) -> p r tt", r=4),
                   t["qt18"][g].rearrange("p (r tt) -> p r tt", r=4), expl_g, ALU.mult),
               waits=[f"qt1.{c}.{g}", f"expl.{c}"])

            def mm_sc(e, cr=cr, hs=hs, g=g):
                e.matmul(pb[g % 2][:], t["skring"][cr][:, hs], t["sqa8"][g], start=True, stop=False)
                return e.matmul(pb[g % 2][:], t["mcur_sb"][:], t["i4rep_sb"][:],
                                start=False, stop=True)
            op("T", f"sc.{c}.{g}", mm_sc,
               waits=[f"sq.{c}.{g}", f"sk.{c}", "c_mcur", "c_i4rep",
                      f"sq.{c}.{KVH-2 + g % 2}"] +
                     ([f"am.{c}.{g-2}"] if g >= 2 else []))
            if c > 0:
                def mm_scp(e, c=c, g=g, gr=gr):
                    e.matmul(pb[4 + gr][64:128, :],
                             t["skring"][(c - 1) % 2][:, g * 128 + 64:g * 128 + 128],
                             t["sqa8"][g], start=True, stop=False, tile_position=(0, 64))
                    return e.matmul(pb[4 + gr][64:128, :], t["mprev_sb"][:, 0:64],
                                    t["i4rep_sb"][:], start=False, stop=True,
                                    tile_position=(0, 64))
                op("T", f"scp.{c}.{g}", mm_scp,
                   waits=[f"sq.{c}.{g}", f"p4rq.{c}.{KVH-2 + g % 2}", "c_mprev", "c_i4rep"])
            op("S", f"esc.{c}.{g}",
               lambda e, g=g: e.activation(t["esc2"][g % 2], pb[g % 2][:], AF.Exp, scale=SSCALE),
               waits=[f"sc.{c}.{g}"] + ([f"pcn.{c}.{g-2}"] if g >= 2 else []))
            if c > 0:
                op("S", f"esp.{c}.{g}",
                   lambda e, gr=gr, g=g: e.activation(t["esp2"][g % 2][64:128, :], pb[4 + gr][64:128, :], AF.Exp, scale=SSCALE),
                   waits=[f"scp.{c}.{g}"] + ([f"ppn.{c}.{g-2}"] if g >= 2 else []))

            def mm_dsw(e, c=c, gr=gr, g=g):
                ii = e.matmul(pb[4 + gr][:], t["twos_sb"][:], t["esc2"][g % 2], start=True,
                              stop=(c == 0))
                if c > 0:
                    ii = e.matmul(pb[4 + gr][:], t["twos_sb"][64:128, :], t["esp2"][g % 2][64:128, :],
                                  start=False, stop=True)
                return ii
            op("T", f"dsw.{c}.{g}", mm_dsw,
               waits=[f"esc.{c}.{g}", f"p4rq.{c}.{KVH-2 + g % 2}"] +
                     ([f"esp.{c}.{g}"] if c else []))
            w.seq("S", f"rs.{c}.{g}", [
                   lambda e, gr=gr: e.activation(t["lnscr"], pb[4 + gr][:], AF.Ln),
                   lambda e, g=g: e.activation(t["rs2"][g % 2], t["lnscr"], AF.Exp, scale=-1.0)],
               waits=[f"dsw.{c}.{g}"])
            op("V", f"pcn.{c}.{g}",
               lambda e, g=g: e.tensor_tensor(t["pcm3"][g % 3], t["esc2"][g % 2], t["rs2"][g % 2], ALU.mult),
               waits=[f"rs.{c}.{g}"])
            if c > 0:
                op("V", f"ppn.{c}.{g}",
                   lambda e, g=g: e.tensor_tensor(t["ppm3"][g % 3][64:128, :], t["esp2"][g % 2][64:128, :],
                                                  t["rs2"][g % 2][64:128, :], ALU.mult),
                   waits=[f"rs.{c}.{g}"])
            op("T", f"at.{c}.{g}",
               lambda e, hs=hs, g=g: e.matmul(pb[g % 2][:], t["ktil"][:, hs], t["qtil3"][g % 3],
                                              start=True, stop=True),
               waits=[f"qt.{c}.{g}", f"kt.{c}", f"esc.{c}.{g}"])
            tril4 = _ins(t["tril_sb"][:], 0, 0, 4)
            op("V", f"am.{c}.{g}",
               lambda e, g=g, tril4=tril4: e.tensor_tensor(
                   t["am3"][g % 3].rearrange("p (r tt) -> p r tt", r=4),
                   pb[g % 2][:].rearrange("p (r tt) -> p r tt", r=4), tril4, ALU.mult),
               waits=[f"at.{c}.{g}", "c_tril"])
            if g >= 2:
                emit_pass2(c, g - 2, cr)
        emit_pass2(c, KVH - 2, cr)
        emit_pass2(c, KVH - 1, cr)
        if c >= 4:
            _ts = (c - 4) // 4
            _j0 = 8 * ((c - 4) % 4)
            for _j in range(_j0 + 4, _j0 + 8):
                emit_p5_jtile(_ts, _j)
        op("V", f"sbf.{c}", lambda e: e.tensor_copy(t["Sbf"], t["S"]),
           waits=[f"sadd.{c}.{KVH-1}"])
        op("V", f"p4odone.{c}", lambda e: e.tensor_copy(t["qt2"][:, 0:1], t["qtil3"][0][:, 0:1]),
           waits=[f"oev.{c}.{KVH-1}", f"oev.{c}.{KVH-2}"])

    for j in range(32):
        emit_p5_jtile(3, j)
    P4_END = [f"p4odone.{NCH-1}", f"sadd.{NCH-1}.{KVH-1}", f"sbf.{NCH-1}",
              f"p4kdone.{NCH-1}", f"khTev.{NCH-1}", f"sq.{NCH-1}.{KVH-1}", f"qt1.{NCH-1}.{KVH-1}"]

    fin_w = [f"outst.{j*4+3}" for j in range(26, 32)] + [f"gld.{j*4+3}" for j in range(29, 32)]
    op("SY", "fin", lambda e: e.nop(), waits=fin_w)


def build_nc():
    nc = bass.Bass("TRN2", target_bir_lowering=False)
    t = {}
    t["x_h"] = nc.dram_tensor("x", [NT, D], F32, kind="ExternalInput").ap()
    t["wq_h"] = nc.dram_tensor("wq", [D, D], F32, kind="ExternalInput").ap()
    t["wk_h"] = nc.dram_tensor("wk", [D, KVH * HD], F32, kind="ExternalInput").ap()
    t["wv_h"] = nc.dram_tensor("wv", [D, KVH * HD], F32, kind="ExternalInput").ap()
    t["wo_h"] = nc.dram_tensor("wo", [D, D], F32, kind="ExternalInput").ap()
    t["w0t_h"] = nc.dram_tensor("w0t", [D, D], F32, kind="ExternalInput").ap()
    t["acat_h"] = nc.dram_tensor("acat", [D, 128], F32, kind="ExternalInput").ap()
    t["bt_h"] = nc.dram_tensor("bt", [128, D], F32, kind="ExternalInput").ap()
    t["b0_h"] = nc.dram_tensor("b0fm", [128, 32], F32, kind="ExternalInput").ap()
    t["maskt_h"] = nc.dram_tensor("maskt", [4, D, NT], BF16, kind="ExternalInput").ap()
    t["cos_h"] = nc.dram_tensor("cosfm", [128, NT], BF16, kind="ExternalInput").ap()
    t["sin_h"] = nc.dram_tensor("sinfm", [128, NT], BF16, kind="ExternalInput").ap()
    t["rmat_h"] = nc.dram_tensor("rmat", [128, 128], BF16, kind="ExternalInput").ap()
    t["tril_h"] = nc.dram_tensor("trilT", [128, 128], BF16, kind="ExternalInput").ap()
    t["mcur_h"] = nc.dram_tensor("mcurT", [128, 128], BF16, kind="ExternalInput").ap()
    t["mprev_h"] = nc.dram_tensor("mprevT", [128, 128], BF16, kind="ExternalInput").ap()
    t["i4rep_h"] = nc.dram_tensor("i4rep", [128, 512], BF16, kind="ExternalInput").ap()
    t["ones_h"] = nc.dram_tensor("ones128", [128, 128], BF16, kind="ExternalInput").ap()
    t["twos_h"] = nc.dram_tensor("twos128", [128, 128], BF16, kind="ExternalInput").ap()
    t["onesf_h"] = nc.dram_tensor("onesf128", [128, 128], F32, kind="ExternalInput").ap()
    t["id128_h"] = nc.dram_tensor("id128", [128, 128], BF16, kind="ExternalInput").ap()
    t["out_h"] = nc.dram_tensor("out", [NT, D], F32, kind="ExternalOutput").ap()
    t["q_sp"] = nc.dram_tensor("q_sp", [NCH, 128, NH * 128], BF16).ap()
    t["k_sp"] = nc.dram_tensor("k_sp", [NCH, 128, KVH * 128], BF16).ap()
    t["v_sp"] = nc.dram_tensor("v_sp", [NCH, 128, KVH * 128], BF16).ap()
    t["g_sp"] = nc.dram_tensor("g_sp", [32, 128, NT], F32).ap()
    t["sg_sp"] = nc.dram_tensor("sg_sp", [NT, D], BF16).ap()

    ctx = ExitStack()
    with ctx:
        def sb(name, shape, dt=BF16):
            h = ctx.enter_context(nc.sbuf_tensor(name, shape, dt))
            t[name] = h
            return h
        sb("bigA", [128, 32 * NT])
        for nm, shape, dt in [("cos_sb", [128, NT], BF16), ("sin_sb", [128, NT], BF16),
                              ("rmat_sb", [128, 128], BF16), ("tril_sb", [128, 128], BF16),
                              ("mcur_sb", [128, 128], BF16), ("mprev_sb", [128, 128], BF16),
                              ("i4rep_sb", [128, 512], BF16),
                              ("ones_sb", [128, 128], BF16), ("twos_sb", [128, 128], BF16),
                              ("onesf_sb", [128, 128], F32), ("id128_sb", [128, 128], BF16),
                              ("b0_sb", [128, 32], F32), ("st1", [128, 16], F32)]:
            sb(nm, shape, dt)
        pool = ctx.enter_context(nc.sbuf_tensor("pool", [128, 34816], BF16))

        def V_(off, n, dt=BF16):
            ap = pool[:, off:off + n]
            if dt == F32:
                ap = ap.bitcast(F32)
            return ap
        t["xt"] = [V_(0, 8192, F32), V_(8192, 8192, F32)]
        t["xc"] = V_(16384, 8192, F32)
        t["xn_tm"] = [V_(24576, 4096), V_(28672, 4096)]
        t["wbuf"] = [V_(0, 8192), V_(8192, 8192)]
        t["mbig"] = [V_(16384, 2048), V_(18432, 2048)]
        t["acat"] = V_(20480, 4096)
        t["bt"] = V_(24576, 4096)
        t["h"] = V_(28672, 2048)
        t["tl"] = [V_(30720 + 512 * i, 512) for i in range(4)]
        t["gv"] = [V_(32768, 1024, F32), V_(33792, 1024, F32)]
        t["ev"] = [V_(16384 + 512 * i, 512) for i in range(4)]
        go = 0

        def GV(n, dt=BF16):
            nonlocal go
            v = V_(go, n, dt)
            go += n
            return v
        t["qg"] = [GV(512), GV(512)]
        t["eq2"] = [GV(512), GV(512)]
        t["rq2"] = [GV(512), GV(512)]
        t["lnscr"] = GV(1024, F32)
        t["qtil3"] = [GV(512) for _ in range(3)]
        t["qt2"] = GV(512)
        t["sq1"] = [GV(512), GV(512)]
        t["sq2"] = t["qt2"]
        t["sq3"] = GV(512)
        t["esc2"] = [GV(512), GV(512)]
        t["rs2"] = [GV(512), GV(512)]
        t["pcm3"] = [GV(512) for _ in range(3)]
        t["am3"] = [GV(512) for _ in range(3)]
        t["esp2"] = [GV(512), GV(512)]
        t["ppm3"] = [GV(512) for _ in range(3)]
        ko = go

        def KV(n, dt=BF16):
            nonlocal ko
            v = V_(ko, n, dt)
            ko += n
            return v
        t["kin"] = KV(1024)
        t["ksig"] = KV(2048, F32)
        t["lam"] = KV(2048, F32)
        t["expl"] = KV(1024)
        t["expli"] = t["lam"].bitcast(BF16)[:, 0:1024]
        t["explend"] = t["st1"][:, 8:16]
        t["ekt"] = KV(1024)
        t["rkt2"] = KV(1024)
        t["krot"] = t["ekt"]
        t["ktil"] = KV(1024)
        t["khat"] = KV(1024)
        t["khT"] = KV(1024)
        t["ktmp"] = t["ksig"]
        t["skring"] = [KV(1024), KV(1024)]
        t["vring"] = [KV(1024), KV(1024)]
        t["S"] = KV(2048, F32)
        t["Sbf"] = KV(1024)
        assert ko <= 34816, ko
        def BV(off, n, dt=BF16):
            ap = t["bigA"][:][:, off:off + n]
            if dt == F32:
                ap = ap.bitcast(F32)
            return ap
        t["wobuf"] = [BV(32768, 8192), BV(40960, 8192)]
        t["sgv5"] = [BV(49152, 512), BV(49664, 512)]
        t["gld5"] = [BV(50176, 1024, F32), BV(51200, 1024, F32)]
        t["sgt5"] = [BV(52224, 512), BV(52736, 512)]
        t["qt18"] = [BV(53248 + 512 * i, 512) for i in range(8)]
        t["sqa8"] = [BV(57344 + 512 * i, 512) for i in range(8)]
        t["sgf5"] = [BV(61440, 1024, F32), BV(62464, 1024, F32)]
        t["xld5"] = [BV(63488, 1024, F32), BV(64512, 1024, F32)]
        t["xf"] = [V_(0, 4096, F32), V_(4096, 4096, F32)]
        t["of"] = [V_(8192, 4096, F32), V_(12288, 4096, F32)]
        t["sgh"] = [V_(16384, 2048), V_(18432, 2048)]
        t["pb"] = [ctx.enter_context(nc.psum_tensor(f"pb{i}", [128, 512], F32)) for i in range(6)]
        t["pS"] = ctx.enter_context(nc.psum_tensor("pS", [128, 512], F32))
        t["pWo"] = ctx.enter_context(nc.psum_tensor("pWo", [128, 512], F32))

        sems = {}
        for s in ALL_SEMS:
            sems[s] = ctx.enter_context(nc.semaphore(s))

        wA = Walk("collect")
        build_walk(wA, t)
        wB = Walk("count")
        wB.waited = wA.waited
        build_walk(wB, t)

        ctx.enter_context(nc.allow_low_precision(
            reason="bf16 matmul operands with fp32 PSUM per stated tolerance"))
        block = ctx.enter_context(nc.Block())
        emitters = {"T": block.tensor, "V": block.vector, "S": block.scalar,
                    "SY": block.sync, "G": block.gpsimd}
        for stream in STREAMS:
            def body(eng, stream=stream):
                wE = Walk("emit", stream=stream, eng=eng, sems=sems)
                wE.waited = wA.waited
                wE.evt = wB.evt
                build_walk(wE, t)
            emitters[stream](body)
    return nc


def _rotl32(x, r):
    return ((x << np.uint32(r)) | (x >> np.uint32(32 - r))) & np.uint32(0xFFFFFFFF)


def _threefry2x32(k0, k1, x0, x1):
    rot = [[13, 15, 26, 6], [17, 29, 16, 24]]
    ks2 = np.uint32(k0) ^ np.uint32(k1) ^ np.uint32(0x1BD11BDA)
    ks = [np.uint32(k0), np.uint32(k1), ks2]
    x0 = (x0 + ks[0]) & np.uint32(0xFFFFFFFF)
    x1 = (x1 + ks[1]) & np.uint32(0xFFFFFFFF)
    for d in range(5):
        for r in rot[d % 2]:
            x0 = (x0 + x1) & np.uint32(0xFFFFFFFF)
            x1 = _rotl32(x1, r) ^ x0
        x0 = (x0 + ks[(d + 1) % 3]) & np.uint32(0xFFFFFFFF)
        x1 = (x1 + ks[(d + 2) % 3] + np.uint32(d + 1)) & np.uint32(0xFFFFFFFF)
    return x0, x1


def _dropout_uniform():
    n = 8 * NT * D
    i = np.arange(n, dtype=np.uint32)
    b1, b2 = _threefry2x32(0, 42, np.zeros(n, np.uint32), i)
    bits = b1 ^ b2
    return (((bits >> np.uint32(9)) | np.uint32(0x3F800000)).view(np.float32)
            - 1.0).reshape(8, NT, D)


def _host_prep(inputs):
    x = np.ascontiguousarray(inputs["L0_attn_hidden_states"])
    W0t = np.ascontiguousarray(inputs["L0_ffn_W0"].T)
    b0 = np.asarray(inputs["L0_ffn_b0"])
    A = np.asarray(inputs["L0_ffn_A"])
    B = np.asarray(inputs["L0_ffn_B"])

    acat = np.zeros((D, 128), np.float32)
    bt = np.zeros((128, D), np.float32)
    for l in range(4):
        acat[:, 32 * l:32 * l + 16] = A[l].T
        bt[32 * l:32 * l + 16, :] = B[l].T
    b0fm = np.ascontiguousarray(b0.reshape(32, 128).T)

    u = _dropout_uniform()
    keep = 0.25
    maskt = np.zeros((8, 4, D, NT), BFNP)
    for l in range(4):
        m = ((u >= l * keep) & (u < (l + 1) * keep)).astype(np.float32) * 8.0
        maskt[:, l] = m.transpose(0, 2, 1).astype(BFNP)

    half = np.arange(HD // 2, dtype=np.float32)
    inv = 1.0 / 10000.0 ** (2.0 * half / HD)
    fr = np.arange(NT, dtype=np.float32)[:, None] * inv[None, :]
    emb = np.concatenate([fr, fr], axis=-1)
    cosfm = np.ascontiguousarray(np.cos(emb).T).astype(BFNP)
    sinfm = np.ascontiguousarray(np.sin(emb).T).astype(BFNP)

    rmat = np.zeros((128, 128), np.float32)
    for dd in range(64):
        rmat[dd + 64, dd] = -1.0
        rmat[dd, dd + 64] = 1.0
    tau = np.arange(128)
    tril = (tau[:, None] <= tau[None, :]).astype(np.float32)
    mcur_valid = ((tau[None, :] - tau[:, None] >= 0) &
                  (tau[None, :] - tau[:, None] <= 64))
    mcur = np.where(mcur_valid, 0.0, -30000.0).astype(np.float32).T
    i64 = np.arange(64)
    mprev_valid = (i64[:, None] >= tau[None, :])
    mprev = np.zeros((128, 128), np.float32)
    mprev[:, 0:64] = np.where(mprev_valid, 0.0, -30000.0).astype(np.float32).T
    i4 = np.concatenate([np.eye(128, dtype=np.float32)] * 4, axis=1)

    common = {
        "wq": np.ascontiguousarray(inputs["L0_attn_Wq"]),
        "wk": np.ascontiguousarray(inputs["L0_attn_Wk"]),
        "wv": np.ascontiguousarray(inputs["L0_attn_Wv"]),
        "wo": np.ascontiguousarray(inputs["L0_attn_Wo"]),
        "w0t": W0t, "acat": acat, "bt": bt, "b0fm": b0fm,
        "cosfm": cosfm, "sinfm": sinfm,
        "rmat": rmat.astype(BFNP), "trilT": tril.astype(BFNP),
        "mcurT": mcur.astype(BFNP), "mprevT": mprev.astype(BFNP),
        "i4rep": i4.astype(BFNP),
        "ones128": np.ones((128, 128), BFNP), "twos128": np.full((128, 128), 2.0, BFNP),
        "onesf128": np.ones((128, 128), np.float32),
        "id128": np.eye(128).astype(BFNP),
    }
    in_maps = []
    for bi in range(8):
        m = dict(common)
        m["x"] = np.ascontiguousarray(x[bi])
        m["maskt"] = np.ascontiguousarray(maskt[bi])
        in_maps.append(m)
    return in_maps


def kernel(L0_attn_hidden_states, L0_attn_Wq, L0_attn_Wk, L0_attn_Wv, L0_attn_Wo,
           L0_ffn_W0, L0_ffn_b0, L0_ffn_A, L0_ffn_B):
    inputs = dict(L0_attn_hidden_states=L0_attn_hidden_states, L0_attn_Wq=L0_attn_Wq,
                  L0_attn_Wk=L0_attn_Wk, L0_attn_Wv=L0_attn_Wv, L0_attn_Wo=L0_attn_Wo,
                  L0_ffn_W0=L0_ffn_W0, L0_ffn_b0=L0_ffn_b0, L0_ffn_A=L0_ffn_A,
                  L0_ffn_B=L0_ffn_B)
    in_maps = _host_prep(inputs)
    nc = build_nc()
    res = run_bass_kernel_spmd(nc, in_maps, core_ids=list(range(8)))
    out = np.stack([np.asarray(r["out"]) for r in res.results], axis=0)
    return out.astype(np.float32)
